# version 18
# baseline (speedup 1.0000x reference)
import contextlib
import numpy as np
import ml_dtypes
import concourse.bass as bass
import concourse.mybir as mybir
from concourse.bass_utils import run_bass_kernel_spmd

F32 = mybir.dt.float32
BF16 = mybir.dt.bfloat16
FP8 = mybir.dt.float8e4
AF = mybir.ActivationFunctionType
ALU = mybir.AluOpType
AX = mybir.AxisListType

NCORE = 8
B, S, D = 4, 2048, 2048
CH = D // 128
TLOC = 1024
HEADS, DH, WIN = 16, 128, 256
ISH, IRT, IL1 = 8192, 1024, 6144
DTR, DST = 256, 8
EPS = 1e-5

ENGS = ("sync", "scalar", "vector", "tensor", "gpsimd")
ESEM = {"sync": "qs", "scalar": "ac", "vector": "ve", "tensor": "pe", "gpsimd": "po"}
QSEM = {"sync": "qs_d", "scalar": "ac_d", "gpsimd": "po_d", "vector": "ve_d"}


class Ctx:

    def __init__(self, nc):
        self.nc = nc
        self.es = contextlib.ExitStack()
        self.sem_es = self.es
        self.sems = {}
        for s in set(ESEM.values()) | set(QSEM.values()):
            self.sems[s] = self.sem_es.enter_context(nc.semaphore("S_" + s))
        self.cnt = {s: 0 for s in self.sems}
        self.ops = {e: [] for e in ENGS}
        self.waited = {e: {} for e in ENGS}
        self._cpool = self.sem_es.enter_context(nc.sbuf_tensor("constpool", [1, 32], mybir.dt.float32))
        self._cpool128 = self.sem_es.enter_context(
            nc.sbuf_tensor("constpool128", [128, 8], mybir.dt.float32))
        self._n128 = 0
        self._cmap = {}

    def dma(self, eng, f, waits=(), stream="misc"):
        if stream not in self.sems:
            self.sems[stream] = self.sem_es.enter_context(self.nc.semaphore("S_" + stream))
            self.cnt[stream] = 0
        prev = (stream, self.cnt[stream]) if self.cnt[stream] else None
        sem = stream
        need = []
        for t in list(waits) + [prev]:
            if t is None:
                continue
            s, v = t
            if self.waited[eng].get(s, -1) < v:
                need.append((s, v))
                self.waited[eng][s] = v
        self.cnt[sem] += 16
        tok = (sem, self.cnt[sem])

        def closure(h, need=tuple(need), f=f, tok=tok):
            for s, v in need:
                h.wait_ge(self.sems[s], v)
            f(h).then_inc(self.sems[tok[0]], 16)
        self.ops[eng].append(closure)
        return tok

    def constf(self, v):
        v = float(v)
        if v not in self._cmap:
            i = len(self._cmap)
            ap = self._cpool[0:1, i:i + 1]
            tok = self.op("vector", lambda h, ap=ap, v=v: h.memset(ap, v))
            self._cmap[v] = (ap, tok)
        return self._cmap[v]

    def constf128(self, v):
        v = float(v)
        key = ("p128", v)
        if key not in self._cmap:
            i = self._n128
            self._n128 += 1
            ap = self._cpool128[:, i:i + 1]
            tok = self.op("vector", lambda h, ap=ap, v=v: h.memset(ap, v))
            self._cmap[key] = (ap, tok)
        return self._cmap[key]

    def op(self, eng, f, waits=(), inc=True, dma=False):
        sem = QSEM[eng] if dma else ESEM[eng]
        need = []
        for t in waits:
            if t is None:
                continue
            s, v = t
            if self.waited[eng].get(s, -1) < v:
                need.append((s, v))
                self.waited[eng][s] = v
        tok = None
        if inc:
            self.cnt[sem] += 16 if dma else 1
            tok = (sem, self.cnt[sem])

        def closure(h, need=tuple(need), f=f, tok=tok, dma=dma):
            for s, v in need:
                h.wait_ge(self.sems[s], v)
            ins = f(h)
            if tok is not None:
                ins.then_inc(self.sems[tok[0]], 16 if dma else 1)
        self.ops[eng].append(closure)
        return tok

    def scope(self):
        class _Scope:
            def __init__(s2):
                s2.prev = None
                s2.es = contextlib.ExitStack()
            def __enter__(s2):
                s2.prev = self.es
                self.es = s2.es
                return s2
            def __exit__(s2, *a):
                s2.es.close()
                self.es = s2.prev
                return False
        return _Scope()

    def barrier(self):
        snap = [(s, v) for s, v in self.cnt.items() if v > 0]
        for e in ENGS:
            self.wait(e, *[(s, v) for s, v in snap])

    def wait(self, eng, *toks):
        need = []
        for t in toks:
            if t is None:
                continue
            s, v = t
            if self.waited[eng].get(s, -1) < v:
                need.append((s, v))
                self.waited[eng][s] = v
        if not need:
            return

        def closure(h, need=tuple(need)):
            for s, v in need:
                h.wait_ge(self.sems[s], v)
        self.ops[eng].append(closure)

    def finish(self, block):
        for e in ENGS:
            fs = self.ops[e]
            if not fs:
                continue
            section = getattr(block, e)

            def body(h, fs=fs):
                for f in fs:
                    f(h)
            section(body)



def host_fm(a):
    T = a.shape[0]
    return np.ascontiguousarray(np.transpose(a.reshape(T, CH, 128), (2, 1, 0)))


def host_unfm(a):
    return np.ascontiguousarray(np.transpose(a, (2, 1, 0))).reshape(a.shape[2], -1)


def host_wT_oc(wT, ocs=128):
    Din, Dout = wT.shape
    a = np.transpose(wT.reshape(Din // 128, 128, Dout // ocs, ocs), (2, 1, 0, 3))
    return np.ascontiguousarray(a)


def host_colvec(v):
    n = v.shape[0] // 128
    return np.ascontiguousarray(v.reshape(n, 128).T)



class LNScratch:
    def __init__(self, ctx, Tt, tag="", xb_alias=None, ps_alias=None):
        nc, es = ctx.nc, ctx.es
        self.Tt = Tt
        g = tag
        assert Tt <= 512
        self.xb = xb_alias
        if self.xb is None:
            self.xb = es.enter_context(nc.sbuf_tensor("ln_xb" + g, [128, CH * Tt], BF16))
        self.t1 = es.enter_context(nc.sbuf_tensor("ln_t1" + g, [128, Tt], BF16))
        self.sq = self.t1
        self.ps = ps_alias if ps_alias is not None else es.enter_context(
            nc.psum_tensor("ln_ps" + g, [128, 512], F32))
        rows1 = es.enter_context(nc.sbuf_tensor("ln_rows1" + g, [128, Tt], F32))
        rows2 = es.enter_context(nc.sbuf_tensor("ln_rows2" + g, [128, Tt], F32))
        self.r0 = rows1[0:1, :]
        self.r1 = rows1[32:33, :]
        self.r2 = rows1[64:65, :]
        self.r3 = rows2[0:1, :]
        self.r4 = rows1[0:1, :]
        self.bc = es.enter_context(nc.sbuf_tensor("ln_bc" + g, [128, 2 * Tt], BF16))
        self.ones_col = es.enter_context(nc.sbuf_tensor("ones_col" + g, [128, 1], BF16))
        self.ones_row = es.enter_context(nc.sbuf_tensor("ones_row" + g, [1, 128], F32))
        t0 = ctx.op("vector", lambda v: v.memset(self.ones_col[:], 1.0))
        t1 = ctx.op("vector", lambda v: v.memset(self.ones_row[:], 1.0))
        self.ones_tok = [t0, t1]
        self.busy = None


def emit_ln(ctx, lns, src_chunks, out_chunks, src_tok, residual=None):
    Tt = lns.Tt
    waits = list(src_tok) + lns.ones_tok
    if lns.busy is not None:
        waits.append(lns.busy)

    xb_toks = {}
    xb_aps = []
    for c in range(CH):
        ap = src_chunks[c]
        if ap.dtype == BF16:
            xb_aps.append(ap)
        else:
            dst = lns.xb[:, c * Tt:(c + 1) * Tt]
            xb_toks[c] = ctx.op(
                "vector", lambda v, ap=ap, dst=dst: v.tensor_copy(out=dst, in_=ap),
                waits=waits)
            xb_aps.append(dst)

    ps_s = lns.ps[0:1, 0:Tt]
    for c in range(CH):
        w = waits + ([xb_toks[c]] if c in xb_toks else [])
        tk = ctx.op("tensor", lambda t, c=c, ap=xb_aps[c]: t.matmul(
            ps_s, lns.ones_col[:], ap,
            start=(c == 0), stop=(c == CH - 1)), waits=w)
    t_s1 = ctx.op("scalar", lambda s: s.activation(
        out=lns.r0, in_=ps_s, func=AF.Copy), waits=[tk])
    sq_dep = [t_s1]
    for c in range(CH):
        w = waits + ([xb_toks[c]] if c in xb_toks else [])
        sqt = ctx.op("scalar", lambda s, ap=xb_aps[c]: s.activation(
            out=lns.sq[:], in_=ap, func=AF.Square), waits=w + sq_dep)
        tk = ctx.op("tensor", lambda t, c=c: t.matmul(
            ps_s, lns.ones_col[:], lns.sq[:],
            start=(c == 0), stop=(c == CH - 1)), waits=[sqt, t_s1])
        sq_dep = [tk]
        if c == CH - 1:
            t_s2 = tk

    inv_d = 1.0 / D
    t_msq = ctx.op("scalar", lambda s: s.activation(
        out=lns.r1, in_=lns.r0, func=AF.Square, scale=inv_d), waits=[t_s1])
    t_var = ctx.op("vector", lambda v: v.scalar_tensor_tensor(
        out=lns.r2, in0=ps_s, scalar=inv_d, in1=lns.r1[:],
        op0=ALU.mult, op1=ALU.subtract), waits=[t_s2, t_msq])
    eps_ap, eps_tok = ctx.constf(EPS)
    t_sd = ctx.op("scalar", lambda s: s.activation(
        out=lns.r1, in_=lns.r2, func=AF.Sqrt, bias=eps_ap),
        waits=[t_var, t_msq, eps_tok])
    t_r = ctx.op("vector", lambda v: v.reciprocal(
        out=lns.r3, in_=lns.r1), waits=[t_sd])
    t_mr = ctx.op("vector", lambda v: v.scalar_tensor_tensor(
        out=lns.r4, in0=lns.r0, scalar=inv_d, in1=lns.r3,
        op0=ALU.mult, op1=ALU.mult), waits=[t_r])

    t_b1 = ctx.op("tensor", lambda t: t.matmul(
        lns.ps[:, 0:Tt], lns.ones_row[:], lns.r3, start=True, stop=True),
        waits=waits + [t_r, t_var])
    t_rb = ctx.op("scalar", lambda s: s.activation(
        out=lns.bc[:, 0:Tt], in_=lns.ps[:, 0:Tt], func=AF.Copy), waits=[t_b1])
    t_b2 = ctx.op("tensor", lambda t: t.matmul(
        lns.ps[:, 0:Tt], lns.ones_row[:], lns.r4, start=True, stop=True),
        waits=[t_mr, t_rb])
    t_bc = ctx.op("scalar", lambda s: s.activation(
        out=lns.bc[:, Tt:2 * Tt], in_=lns.ps[:, 0:Tt], func=AF.Copy), waits=[t_b2])
    rb = lns.bc[:, 0:Tt]
    mrb = lns.bc[:, Tt:2 * Tt]

    out_toks = []
    prev_read = [t_s2]
    for c in range(CH):
        w = [t_rb, t_bc] + ([xb_toks[c]] if c in xb_toks else []) + waits + prev_read
        tt1 = ctx.op("vector", lambda v, ap=xb_aps[c]: v.tensor_mul(
            lns.t1[:], ap, rb), waits=w)
        if residual is None:
            tk = ctx.op("vector", lambda v, c=c: v.tensor_sub(
                out_chunks[c], lns.t1[:], mrb), waits=[tt1])
        else:
            xin_chunks, xin_toks = residual
            tt2 = ctx.op("vector", lambda v: v.tensor_sub(
                lns.t1[:], lns.t1[:], mrb), waits=[tt1])
            xw = [tt2] + ([xin_toks[c]] if xin_toks else [])
            tk = ctx.op("vector", lambda v, c=c: v.tensor_add(
                out_chunks[c], xin_chunks[c], lns.t1[:]), waits=xw)
        prev_read = [tk]
        out_toks.append(tk)
    lns.busy = out_toks[-1]
    return out_toks, out_toks[-1]



def emit_l1ffn(ctx, xres, xin_toks, d, xout_ap):
    nc, es = ctx.nc, ctx.es
    Tt = 512
    OCG = IL1 // 128
    SLAB = 4
    NSL = OCG // SLAB
    KCD = IL1 // 128

    t_sp = ctx.dma("sync", lambda h: h.dma_start(out=d["x3_s"], in_=xres[:]),
                   waits=xin_toks, stream="x3s")
    m_sb = [es.enter_context(nc.sbuf_tensor("f1_m0", [128, OCG * Tt], BF16)),
            xres[:].bitcast(BF16)[:, 0:OCG * Tt]]
    ln_sb = es.enter_context(nc.sbuf_tensor("f1_ln", [128, CH * 2 * Tt], BF16))
    g_sb = es.enter_context(nc.sbuf_tensor("f1_g", [128, Tt], BF16))
    u_sb = es.enter_context(nc.sbuf_tensor("f1_u", [128, Tt], BF16))
    bo_sb = ln_sb
    xr_f = m_sb[0][:, 0:2 * CH * Tt].bitcast(F32)[:, 0:CH * Tt]
    wg8 = [es.enter_context(nc.sbuf_tensor(f"f1_wg{i}", [128, SLAB * CH * 128], FP8))
           for i in range(2)]
    wu8 = [es.enter_context(nc.sbuf_tensor(f"f1_wu{i}", [128, SLAB * CH * 128], FP8))
           for i in range(2)]
    wd8 = [es.enter_context(nc.sbuf_tensor(f"f1_wd{i}", [128, KCD * 128], FP8))
           for i in range(2)]
    scs = es.enter_context(nc.sbuf_tensor("f1_scs", [128, 2 * OCG + CH], F32))
    pg = [es.enter_context(nc.psum_tensor(f"f1_pg{i}", [128, Tt], F32)) for i in range(2)]
    pu = [es.enter_context(nc.psum_tensor(f"f1_pu{i}", [128, Tt], F32)) for i in range(2)]
    pd = [es.enter_context(nc.psum_tensor(f"f1_pd{i}", [128, Tt], F32)) for i in range(2)]
    lns = LNScratch(ctx, Tt, "f1", xb_alias=m_sb[0][:, 0:CH * Tt])

    t_sc = ctx.dma("sync", lambda h: h.dma_start(out=scs[:], in_=d["l1_scales"]), stream="sc")
    gsc = lambda oc: scs[:, oc:oc + 1]
    usc = lambda oc: scs[:, OCG + oc:OCG + oc + 1]
    dsc = lambda oc: scs[:, 2 * OCG + oc:2 * OCG + oc + 1]

    def load_gu(which, dst, sl, waits):
        src = d["l1_gateT" if which == "g" else "l1_upT"]
        ap = src[sl * SLAB:(sl + 1) * SLAB].rearrange("oc p kc o -> p oc kc o")
        dv = dst[:].rearrange("p (oc kc o) -> p oc kc o", oc=SLAB, kc=CH)
        return ctx.dma("gpsimd", lambda h, ap=ap, dv=dv: h.dma_start(out=dv, in_=ap),
                       waits=waits, stream="w" + which)

    def load_d(dst, oc, waits):
        ap = d["l1_downT"][oc]
        dv = dst[:].rearrange("p (kc o) -> p kc o", kc=KCD)
        return ctx.dma("gpsimd", lambda h, ap=ap, dv=dv: h.dma_start(out=dv, in_=ap),
                       waits=waits, stream="wd")

    ln_toks = {}
    for tt in range(2):
        t0 = tt * Tt
        srcx = [xres[:, c * TLOC + t0: c * TLOC + t0 + Tt] for c in range(CH)]
        lnc = [ln_sb[:, (c * 2 + tt) * Tt:(c * 2 + tt + 1) * Tt] for c in range(CH)]
        toks, _ = emit_ln(ctx, lns, srcx, lnc, list(xin_toks))
        ln_toks[tt] = toks
    lnv = lambda kc, tt: ln_sb[:, (kc * 2 + tt) * Tt:(kc * 2 + tt + 1) * Tt]

    gu_busy = [[None, None], [None, None]]
    d_busy = [None, None]
    m_toks = {}
    gev = {}
    g_reader = [None]
    ld_g = load_gu("g", wg8[0], 0, [gu_busy[0][0], t_sp])
    ld_u = load_gu("u", wu8[0], 0, [gu_busy[1][0], t_sp])
    for sl in range(NSL):
        bi = sl % 2
        nld_g = nld_u = None
        if sl + 1 < NSL:
            nld_g = load_gu("g", wg8[1 - bi], sl + 1, [gu_busy[0][1 - bi]])
            nld_u = load_gu("u", wu8[1 - bi], sl + 1, [gu_busy[1][1 - bi]])
        for os_ in range(SLAB):
            oc = sl * SLAB + os_
            for tt in range(2):
                for kc in range(CH):
                    wt_g = wg8[bi][:, (os_ * CH + kc) * 128:(os_ * CH + kc + 1) * 128]
                    wt_u = wu8[bi][:, (os_ * CH + kc) * 128:(os_ * CH + kc + 1) * 128]
                    w = [ld_g, ln_toks[tt][kc]] + ([gev.get(("g", oc - 1, tt))] if kc == 0 else [])
                    tkg = ctx.op("tensor", lambda t, wt=wt_g, rh=lnv(kc, tt), tt=tt, kc=kc: t.matmul(
                        pg[tt][:], wt, rh, start=(kc == 0), stop=(kc == CH - 1)), waits=w)
                    w = [ld_u, ln_toks[tt][kc]] + ([gev.get(("u", oc - 1, tt))] if kc == 0 else [])
                    tku = ctx.op("tensor", lambda t, wt=wt_u, rh=lnv(kc, tt), tt=tt, kc=kc: t.matmul(
                        pu[tt][:], wt, rh, start=(kc == 0), stop=(kc == CH - 1)), waits=w)
                ev_g = ctx.op("scalar", lambda s, oc=oc, tt=tt: s.activation(
                    out=g_sb[:], in_=pg[tt][:], func=AF.Silu, scale=gsc(oc)),
                    waits=[tkg, t_sc, g_reader[0]])
                ev_u = ctx.op("scalar", lambda s, oc=oc, tt=tt: s.activation(
                    out=u_sb[:], in_=pu[tt][:], func=AF.Identity, scale=usc(oc)),
                    waits=[tku, t_sc, g_reader[0]])
                gev[("g", oc, tt)] = ev_g
                gev[("u", oc, tt)] = ev_u
                mdst = m_sb[tt][:, oc * Tt:(oc + 1) * Tt]
                mw = [ev_g, ev_u]
                if tt == 1:
                    mw.append(t_sp)
                m_toks[(oc, tt)] = ctx.op("vector", lambda v, mdst=mdst: v.tensor_mul(
                    mdst, g_sb[:], u_sb[:]), waits=mw)
                g_reader[0] = m_toks[(oc, tt)]
        gu_busy[0][bi] = tkg
        gu_busy[1][bi] = tku
        if nld_g is not None:
            ld_g, ld_u = nld_g, nld_u

    down_ev = {}
    ld_d = load_d(wd8[0], 0, [d_busy[0]])
    for oc in range(CH):
        bi = oc % 2
        nld = None
        if oc + 1 < CH:
            nld = load_d(wd8[1 - bi], oc + 1, [d_busy[1 - bi]])
        for tt in range(2):
            for kc in range(KCD):
                wt = wd8[bi][:, kc * 128:(kc + 1) * 128]
                rh = m_sb[tt][:, kc * Tt:(kc + 1) * Tt]
                w = [ld_d, m_toks[(kc, tt)]] + ([down_ev.get((oc - 1, tt))] if kc == 0 else [])
                tkd = ctx.op("tensor", lambda t, wt=wt, rh=rh, kc=kc, tt=tt: t.matmul(
                    pd[tt][:], wt, rh, start=(kc == 0), stop=(kc == KCD - 1)), waits=w)
            d_busy[bi] = tkd
            bodst = bo_sb[:, (oc * 2 + tt) * Tt:(oc * 2 + tt + 1) * Tt]
            down_ev[(oc, tt)] = ctx.op("scalar", lambda s, oc=oc, bodst=bodst, tt=tt: s.activation(
                out=bodst, in_=pd[tt][:], func=AF.Identity, scale=dsc(oc)),
                waits=[tkd, t_sc])
        if nld is not None:
            ld_d = nld
    down_all = [d_busy[0], d_busy[1]]

    out_toks = []
    for tt in range(2):
        xrt = xr_f
        t_xr = ctx.dma("sync", lambda h, tt=tt, xrt=xrt: h.dma_start(
            out=xrt.rearrange("p (c t) -> p c t", c=CH),
            in_=d["x3_s"].rearrange("p (c t) -> p c t", c=CH)[:, :, tt * Tt:(tt + 1) * Tt]),
            waits=down_all + [t_sp] + out_toks[-1:], stream="xrr")
        bo = [bo_sb[:, (c * 2 + tt) * Tt:(c * 2 + tt + 1) * Tt] for c in range(CH)]
        xrs = [xrt.rearrange("p (c t) -> p c t", c=CH)[:, c, :] for c in range(CH)]
        toks, _ = emit_ln(ctx, lns, bo, xrs,
                          [down_ev[(c, tt)] for c in range(CH)] + [t_xr],
                          residual=(xrs, None))
        st = ctx.dma("sync", lambda h, tt=tt, xrt=xrt: h.dma_start(
            out=xout_ap[:, :, tt * Tt:(tt + 1) * Tt],
            in_=xrt.rearrange("p (c t) -> p c t", c=CH)),
            waits=toks, stream="xo4")
        out_toks.append(st)
    return out_toks


def build_prog_l1ffn():
    nc = bass.Bass("TRN2", target_bir_lowering=False)
    d = {}
    d["xin"] = nc.dram_tensor("xin", [128, CH, TLOC], F32, kind="ExternalInput").ap()
    d["x3_s"] = nc.dram_tensor("x3_s", [128, CH * TLOC], F32, kind="Internal").ap()
    d["l1_gateT"] = nc.dram_tensor("l1_gateT", [IL1 // 128, 128, CH, 128], F32, kind="ExternalInput").ap()
    d["l1_upT"] = nc.dram_tensor("l1_upT", [IL1 // 128, 128, CH, 128], F32, kind="ExternalInput").ap()
    d["l1_downT"] = nc.dram_tensor("l1_downT", [CH, 128, IL1 // 128, 128], F32, kind="ExternalInput").ap()
    d["l1_scales"] = nc.dram_tensor("l1_scales", [128, 2 * (IL1 // 128) + CH], F32, kind="ExternalInput").ap()
    xout = nc.dram_tensor("xout", [128, CH, TLOC], F32, kind="ExternalOutput").ap()

    ctx = Ctx(nc)
    with ctx.es:
        xin_sb = ctx.es.enter_context(nc.sbuf_tensor("xin_sb", [128, CH * TLOC], F32))
        blk = ctx.es.enter_context(nc.Block())
        t_x = ctx.dma("sync", lambda h: h.dma_start(
            out=xin_sb[:], in_=d["xin"].rearrange("p c t -> p (c t)")), stream="x")
        out_toks = emit_l1ffn(ctx, xin_sb, [t_x], d, xout)
        ctx.wait("sync", *out_toks)
        ctx.finish(blk)
    return nc



def emit_l0ffn(ctx, xres, xin_toks, d):
    nc, es = ctx.nc, ctx.es
    Tt = 512
    OCG = ISH // 128
    OCR = IRT // 128
    KCD = OCG

    t_sp = ctx.dma("sync", lambda h: h.dma_start(out=d["x1_s"], in_=xres[:]),
                   waits=xin_toks, stream="x1s")
    m_sb = [es.enter_context(nc.sbuf_tensor("f0_m0", [128, OCG * Tt], BF16)),
            xres[:].bitcast(BF16)[:, 0:OCG * Tt]]
    ln_sb = es.enter_context(nc.sbuf_tensor("f0_ln", [128, CH * 2 * Tt], BF16))
    bo_sb = ln_sb
    mr_sb = es.enter_context(nc.sbuf_tensor("f0_mr", [128, 2 * OCR * Tt], BF16))
    g_sb = es.enter_context(nc.sbuf_tensor("f0_g", [128, Tt], BF16))
    lg_alias = g_sb
    crow = g_sb[0:1, :]
    wg = [es.enter_context(nc.sbuf_tensor(f"f0_wg{i}", [128, CH * 128], BF16))
          for i in range(2)]
    wu = [es.enter_context(nc.sbuf_tensor(f"f0_wu{i}", [128, CH * 128], BF16))
          for i in range(2)]
    wd = wg
    wrd = es.enter_context(nc.sbuf_tensor("f0_wrd0", [128, OCR * 128], BF16))
    wro = es.enter_context(nc.sbuf_tensor("f0_wro", [128, CH * 8], BF16))
    idn = es.enter_context(nc.sbuf_tensor("f0_idn", [128, 128], BF16))
    ones_bf = es.enter_context(nc.sbuf_tensor("f0_ones", [1, 128], BF16))
    lgt = es.enter_context(nc.sbuf_tensor("f0_lgt", [128, 32], BF16))
    cm1 = es.enter_context(nc.sbuf_tensor("f0_cm1", [128, 4], F32))
    cm2 = es.enter_context(nc.sbuf_tensor("f0_cm2", [128, 4], F32))
    ceq = es.enter_context(nc.sbuf_tensor("f0_ceq", [128, 32], BF16))
    cadj = es.enter_context(nc.sbuf_tensor("f0_cadj", [128, 32], F32))
    ce = es.enter_context(nc.sbuf_tensor("f0_ce", [128, 4], F32))
    cs = es.enter_context(nc.sbuf_tensor("f0_cs", [128, 4], F32))
    cr = es.enter_context(nc.sbuf_tensor("f0_cr", [128, 4], F32))
    ct = es.enter_context(nc.sbuf_tensor("f0_ct", [128, 4], F32))
    ccomb = es.enter_context(nc.sbuf_tensor("f0_ccomb", [128, 4], BF16))
    combB = es.enter_context(nc.sbuf_tensor("f0_combB", [128, 2 * Tt], BF16))

    pg = [es.enter_context(nc.psum_tensor(f"f0_pg{i}", [128, Tt], F32)) for i in range(2)]
    pu = [es.enter_context(nc.psum_tensor(f"f0_pu{i}", [128, Tt], F32)) for i in range(2)]
    pd = [es.enter_context(nc.psum_tensor(f"f0_pd{i}", [128, Tt], F32)) for i in range(2)]
    prt = es.enter_context(nc.psum_tensor("f0_prt", [128, Tt], F32))
    plg = prt[0:8, :]
    ptr = prt[:, 0:16].bitcast(BF16)
    ptc = prt[0:1, 128:384].bitcast(BF16)
    pcb = prt
    lns = LNScratch(ctx, Tt, "f0", xb_alias=m_sb[0][:, 0:CH * Tt])
    xr_f = m_sb[0][:, 0:2 * CH * Tt].bitcast(F32)[:, 0:CH * Tt]

    t_id = ctx.dma("gpsimd", lambda h: h.dma_start(out=idn[:], in_=d["ident"]), stream="id")
    t_on = ctx.op("vector", lambda v: v.memset(ones_bf[:], 1.0))
    t_wro = ctx.dma("gpsimd", lambda h: h.dma_start(
        out=wro[:].rearrange("p (kc o) -> p kc o", kc=CH),
        in_=d["routerT"].rearrange("(kc p) o -> p kc o", p=128)), stream="wro")

    def load_gu(which, dst, oc, waits, routed=False):
        if routed:
            src = d["r_guT"][0 if which == "g" else 1]
        else:
            src = d["sh_gateT" if which == "g" else "sh_upT"]
        ap = src[oc].rearrange("p kc o -> p kc o")
        dv = dst[:].rearrange("p (kc o) -> p kc o", kc=CH)
        return ctx.dma("gpsimd", lambda h, ap=ap, dv=dv: h.dma_start(out=dv, in_=ap),
                       waits=waits, stream="w" + which)

    def load_d(dst, oc, half, waits):
        ap = d["sh_dT"][oc, :, half * (KCD // 4):(half + 1) * (KCD // 4)]
        dv = dst[:].rearrange("p (kc o) -> p kc o", kc=KCD // 4)
        return ctx.dma("gpsimd", lambda h, ap=ap, dv=dv: h.dma_start(out=dv, in_=ap),
                       waits=waits, stream="wd")

    ln_toks = {}
    for tt in range(2):
        t0 = tt * Tt
        srcx = [xres[:, c * TLOC + t0: c * TLOC + t0 + Tt] for c in range(CH)]
        lnc = [ln_sb[:, (c * 2 + tt) * Tt:(c * 2 + tt + 1) * Tt] for c in range(CH)]
        toks, _ = emit_ln(ctx, lns, srcx, lnc, list(xin_toks))
        ln_toks[tt] = toks
    lnv = lambda kc, tt: ln_sb[:, (kc * 2 + tt) * Tt:(kc * 2 + tt + 1) * Tt]
    ln_done = ln_toks[1][CH - 1]

    ev_cB = {}
    cw_prev = []
    for tt in range(2):
        for kc in range(CH):
            tk_lg = ctx.op("tensor", lambda t, kc=kc, tt=tt: t.matmul(
                plg, wro[:, kc * 8:(kc + 1) * 8], lnv(kc, tt),
                start=(kc == 0), stop=(kc == CH - 1)),
                waits=[t_wro, ln_toks[tt][kc]] + cw_prev)
        ev_lg = ctx.op("scalar", lambda s: s.activation(
            out=lg_alias[0:8, :], in_=plg, func=AF.Copy), waits=[tk_lg])
        tr_toks = []
        for q in range(4):
            tk = ctx.op("tensor", lambda t, q=q: t.transpose(
                ptr[:, q * 8:(q + 1) * 8], lg_alias[0:8, q * 128:(q + 1) * 128],
                idn[0:8, 0:8]), waits=[ev_lg, t_id])
            tr_toks.append(tk)
        ev_lgt = ctx.op("scalar", lambda s: s.activation(
            out=lgt[:], in_=ptr, func=AF.Copy), waits=tr_toks)
        v_lgt = lgt[:].rearrange("p (q n) -> p q n", n=8)
        t_m1 = ctx.op("vector", lambda v: v.reduce_max(
            cm1[:].rearrange("p (q o) -> p q o", o=1), v_lgt, axis=AX.X), waits=[ev_lgt])
        m1b = cm1[:].rearrange("p (q o) -> p q o", o=1).broadcast_to([128, 4, 8])
        t_eq = ctx.op("vector", lambda v: v.tensor_tensor(
            out=ceq[:].rearrange("p (q n) -> p q n", n=8), in0=v_lgt, in1=m1b,
            op=ALU.is_ge), waits=[t_m1])
        t_adj = ctx.op("vector", lambda v: v.scalar_tensor_tensor(
            out=cadj[:], in0=ceq[:], scalar=-1e30, in1=lgt[:],
            op0=ALU.mult, op1=ALU.add), waits=[t_eq])
        t_m2 = ctx.op("vector", lambda v: v.reduce_max(
            cm2[:].rearrange("p (q o) -> p q o", o=1),
            cadj[:].rearrange("p (q n) -> p q n", n=8), axis=AX.X), waits=[t_adj])
        t_e = ctx.op("vector", lambda v: v.tensor_sub(ce[:], cm2[:], cm1[:]), waits=[t_m2])
        t_ee = ctx.op("scalar", lambda s: s.activation(
            out=ce[:], in_=ce[:], func=AF.Exp), waits=[t_e])
        t_s = ctx.op("vector", lambda v: v.tensor_scalar_add(cs[:], ce[:], 1.0), waits=[t_ee])
        t_r = ctx.op("vector", lambda v: v.reciprocal(cr[:], cs[:]), waits=[t_s])
        t_t = ctx.op("vector", lambda v: v.tensor_mul(ct[:], ce[:], cr[:]), waits=[t_r])
        t_cb = ctx.op("vector", lambda v: v.tensor_add(ccomb[:], ct[:], cr[:]), waits=[t_t])
        trc_toks = []
        for q in range(4):
            tk = ctx.op("tensor", lambda t, q=q: t.transpose(
                ptc[0:1, q * 128:(q + 1) * 128], ccomb[:, q:q + 1], idn[:]),
                waits=[t_cb, t_id, ev_lgt])
            trc_toks.append(tk)
        ev_cr = ctx.op("scalar", lambda s: s.activation(
            out=crow[:], in_=ptc[0:1, 0:Tt], func=AF.Copy), waits=trc_toks)
        t_cbm = ctx.op("tensor", lambda t: t.matmul(
            pcb[:, 0:Tt], ones_bf[:], crow[:], start=True, stop=True),
            waits=[ev_cr, t_on])
        ev_cB[tt] = ctx.op("scalar", lambda s, tt=tt: s.activation(
            out=combB[:, tt * Tt:(tt + 1) * Tt], in_=pcb[:, 0:Tt], func=AF.Copy),
            waits=[t_cbm])
        cw_prev = [ev_cB[tt]]

    gu_busy = {"g": [None, None], "u": [None, None]}
    lastr = {}
    m_toks = {}
    mr_toks = {}

    def gu_loop(n_oc, routed):
        ld_g = load_gu("g", wg[0], 0, [gu_busy["g"][0], t_sp], routed=routed)
        ld_u = load_gu("u", wu[0], 0, [gu_busy["u"][0], t_sp], routed=routed)
        for oc in range(n_oc):
            bi = oc % 2
            nld_g = nld_u = None
            if oc + 1 < n_oc:
                nld_g = load_gu("g", wg[1 - bi], oc + 1, [gu_busy["g"][1 - bi]], routed=routed)
                nld_u = load_gu("u", wu[1 - bi], oc + 1, [gu_busy["u"][1 - bi]], routed=routed)
            for tt in range(2):
                for kc in range(CH):
                    wt_g = wg[bi][:, kc * 128:(kc + 1) * 128]
                    wt_u = wu[bi][:, kc * 128:(kc + 1) * 128]
                    w = [ld_g, ln_toks[tt][kc]] + ([lastr.get(("g", tt))] if kc == 0 else [])
                    tkg = ctx.op("tensor", lambda t, wt=wt_g, rh=lnv(kc, tt), tt=tt, kc=kc: t.matmul(
                        pg[tt][:], wt, rh, start=(kc == 0), stop=(kc == CH - 1)), waits=w)
                    w = [ld_u, ln_toks[tt][kc]] + ([lastr.get(("u", tt))] if kc == 0 else [])
                    tku = ctx.op("tensor", lambda t, wt=wt_u, rh=lnv(kc, tt), tt=tt, kc=kc: t.matmul(
                        pu[tt][:], wt, rh, start=(kc == 0), stop=(kc == CH - 1)), waits=w)
                mdst = (mr_sb if routed else m_sb[tt])[
                    :, ((oc * 2 + tt) if routed else oc) * Tt:
                       (((oc * 2 + tt) if routed else oc) + 1) * Tt]
                mw = [tkg, ln_done]
                if not routed and tt == 1:
                    mw.append(t_sp)
                ev_g = ctx.op("scalar", lambda s, tt=tt, mdst=mdst: s.activation(
                    out=mdst, in_=pg[tt][:], func=AF.Silu), waits=mw)
                lastr[("g", tt)] = ev_g
                t_mu = ctx.op("vector", lambda v, tt=tt, mdst=mdst: v.tensor_tensor(
                    out=mdst, in0=pu[tt][:], in1=mdst, op=ALU.mult),
                    waits=[tku, ev_g])
                lastr[("u", tt)] = t_mu
                if routed:
                    mr_toks[(oc, tt)] = ctx.op("vector", lambda v, tt=tt, mdst=mdst: v.tensor_mul(
                        mdst, mdst, combB[:, tt * Tt:(tt + 1) * Tt]),
                        waits=[t_mu, ev_cB[tt]])
                else:
                    m_toks[(oc, tt)] = t_mu
            gu_busy["g"][bi] = tkg
            gu_busy["u"][bi] = tku
            if nld_g is not None:
                ld_g, ld_u = nld_g, nld_u

    gu_loop(OCR, True)
    gu_loop(OCG, False)

    down_ev = {}
    d_busy = [gu_busy["g"][0], gu_busy["g"][1]]
    rd_busy = [None]
    ld_d = load_d(wd[0], 0, 0, [d_busy[0]])
    for oc in range(CH):
        ld_rd = ctx.dma("gpsimd", lambda h, oc=oc: h.dma_start(
            out=wrd[:].rearrange("p (kc o) -> p kc o", kc=OCR),
            in_=d["r_dT"][oc]), waits=[rd_busy[0]], stream="wrd")
        for half in range(4):
            bi = (4 * oc + half) % 2
            nld = None
            if 4 * oc + half + 1 < 4 * CH:
                nxt = 4 * oc + half + 1
                nld = load_d(wd[1 - bi], nxt // 4, nxt % 4, [d_busy[1 - bi]])
            for tt in range(2):
                for kc2 in range(KCD // 4):
                    kc = half * (KCD // 4) + kc2
                    wt = wd[bi][:, kc2 * 128:(kc2 + 1) * 128]
                    rh = m_sb[tt][:, kc * Tt:(kc + 1) * Tt]
                    w = [ld_d, m_toks[(kc, tt)]]
                    if kc2 == 0 and half == 0:
                        w.append(down_ev.get((oc - 1, tt)))
                    tkd = ctx.op("tensor", lambda t, wt=wt, rh=rh,
                                 st=(kc2 == 0 and half == 0), tt=tt: t.matmul(
                        pd[tt][:], wt, rh, start=st, stop=False), waits=w)
                d_busy[bi] = tkd
            if nld is not None:
                ld_d = nld
        for tt in range(2):
            for kc in range(OCR):
                wt = wrd[:, kc * 128:(kc + 1) * 128]
                rh = mr_sb[:, (kc * 2 + tt) * Tt:(kc * 2 + tt + 1) * Tt]
                tkd2 = ctx.op("tensor", lambda t, wt=wt, rh=rh, kc=kc, tt=tt: t.matmul(
                    pd[tt][:], wt, rh, start=False, stop=(kc == OCR - 1)),
                    waits=[ld_rd, mr_toks[(kc, tt)]])
            rd_busy[0] = tkd2
            bodst = bo_sb[:, (oc * 2 + tt) * Tt:(oc * 2 + tt + 1) * Tt]
            down_ev[(oc, tt)] = ctx.op("scalar", lambda s, bodst=bodst, tt=tt: s.activation(
                out=bodst, in_=pd[tt][:], func=AF.Copy), waits=[tkd2])
    down_all = [down_ev[(CH - 1, 0)], down_ev[(CH - 1, 1)], rd_busy[0]]

    out_toks_all = []
    prev_st = []
    for tt in range(2):
        t_xr = ctx.dma("sync", lambda h, tt=tt: h.dma_start(
            out=xr_f.rearrange("p (c t) -> p c t", c=CH),
            in_=d["x1_s"].rearrange("p (c t) -> p c t", c=CH)[:, :, tt * Tt:(tt + 1) * Tt]),
            waits=down_all + [t_sp] + prev_st, stream="xrr0")
        bo = [bo_sb[:, (c * 2 + tt) * Tt:(c * 2 + tt + 1) * Tt] for c in range(CH)]
        xrs = [xr_f.rearrange("p (c t) -> p c t", c=CH)[:, c, :] for c in range(CH)]
        xdst = [xres[:, c * TLOC + tt * Tt: c * TLOC + tt * Tt + Tt] for c in range(CH)]
        toks, _ = emit_ln(ctx, lns, bo, xdst,
                          [down_ev[(c, tt)] for c in range(CH)] + [t_xr],
                          residual=(xrs, None))
        prev_st = toks
        out_toks_all.extend(toks)
    return out_toks_all


def build_prog_l0ffn():
    nc = bass.Bass("TRN2", target_bir_lowering=False)
    d = {}
    d["xin"] = nc.dram_tensor("xin", [128, CH, TLOC], F32, kind="ExternalInput").ap()
    d["x1_s"] = nc.dram_tensor("x1_s", [128, CH * TLOC], F32, kind="Internal").ap()
    d["sh_gateT"] = nc.dram_tensor("sh_gateT", [ISH // 128, 128, CH, 128], F32, kind="ExternalInput").ap()
    d["sh_upT"] = nc.dram_tensor("sh_upT", [ISH // 128, 128, CH, 128], F32, kind="ExternalInput").ap()
    d["sh_dT"] = nc.dram_tensor("sh_dT", [CH, 128, ISH // 128, 128], F32, kind="ExternalInput").ap()
    d["r_guT"] = nc.dram_tensor("r_guT", [2, IRT // 128, 128, CH, 128], F32, kind="ExternalInput").ap()
    d["r_dT"] = nc.dram_tensor("r_dT", [CH, 128, IRT // 128, 128], F32, kind="ExternalInput").ap()
    d["routerT"] = nc.dram_tensor("routerT", [D, 8], F32, kind="ExternalInput").ap()
    d["ident"] = nc.dram_tensor("ident", [128, 128], F32, kind="ExternalInput").ap()
    xout = nc.dram_tensor("xout", [128, CH, TLOC], F32, kind="ExternalOutput").ap()

    ctx = Ctx(nc)
    with ctx.es:
        xin_sb = ctx.es.enter_context(nc.sbuf_tensor("xin_sb", [128, CH * TLOC], F32))
        blk = ctx.es.enter_context(nc.Block())
        t_x = ctx.dma("sync", lambda h: h.dma_start(
            out=xin_sb[:], in_=d["xin"].rearrange("p c t -> p (c t)")), stream="x")
        out_toks = emit_l0ffn(ctx, xin_sb, [t_x], d)
        st = ctx.dma("sync", lambda h: h.dma_start(
            out=xout.rearrange("p c t -> p (c t)"), in_=xin_sb[:]),
            waits=out_toks, stream="xo")
        ctx.wait("sync", st)
        ctx.finish(blk)
    return nc



def emit_l0attn(ctx, d, xres, xres_tok):
    nc, es = ctx.nc, ctx.es
    TH = TLOC + WIN
    NW = TLOC // WIN
    NTC = TH // 128

    ln1 = es.enter_context(nc.sbuf_tensor("at_ln1", [128, CH * TH], BF16))
    qkstg = [es.enter_context(nc.sbuf_tensor(f"at_qks{i}", [128, 512], BF16))
             for i in range(2)]
    vstg = es.enter_context(nc.sbuf_tensor("at_vstg", [128, D], BF16))
    vt_sb = es.enter_context(nc.sbuf_tensor("at_vt", [128, 512], BF16))
    qwin = es.enter_context(nc.sbuf_tensor("at_qwin", [128, CH * WIN], BF16))
    kwin = es.enter_context(nc.sbuf_tensor("at_kwin", [128, CH * 512], BF16))
    vwin = es.enter_context(nc.sbuf_tensor("at_vwin", [128, 4 * D], BF16))
    wbuf = [es.enter_context(nc.sbuf_tensor(f"at_w{i}", [128, CH * 128], BF16))
            for i in range(2)]
    cs_sb = es.enter_context(nc.sbuf_tensor("at_cs", [64, 2 * TH], BF16))
    msk = es.enter_context(nc.sbuf_tensor("at_msk", [128, 4 * 512], BF16))
    ra = [es.enter_context(nc.sbuf_tensor(f"at_ra{i}", [64, 512], BF16))
          for i in range(2)]
    rb2 = [es.enter_context(nc.sbuf_tensor(f"at_rb{i}", [64, 512], BF16))
           for i in range(2)]
    p_sbs = [es.enter_context(nc.sbuf_tensor(f"at_p{i}", [128, 512], BF16))
             for i in range(3)]
    ph2 = [es.enter_context(nc.sbuf_tensor(f"at_ph2{i}", [64, 512], BF16))
           for i in range(2)]
    rp_sb = [es.enter_context(nc.sbuf_tensor(f"at_rp{i}", [64, 512], BF16))
             for i in range(2)]
    pt_sbs = [es.enter_context(nc.sbuf_tensor(f"at_pt{i}", [128, 512], BF16))
              for i in range(3)]
    cols = es.enter_context(nc.sbuf_tensor("at_cols", [128, 12], F32))
    idn = es.enter_context(nc.sbuf_tensor("at_idn", [128, 128], BF16))
    attn_sb = ln1[:, 0:2 * CH * WIN]
    bo_view = ln1[:, 2 * CH * WIN:4 * CH * WIN]

    PA = es.enter_context(nc.psum_tensor("at_PA", [128, 2048], F32))
    PB0 = es.enter_context(nc.psum_tensor("at_PB0", [128, 512], F32))
    PB1 = es.enter_context(nc.psum_tensor("at_PB1", [128, 512], F32))
    PD = es.enter_context(nc.psum_tensor("at_PD", [128, 512], F32))
    lns = LNScratch(ctx, WIN, "at")

    t_id = ctx.dma("gpsimd", lambda h: h.dma_start(out=idn[:], in_=d["ident"]), stream="id")
    t_cs = ctx.dma("gpsimd", lambda h: h.dma_start(out=cs_sb[:], in_=d["cossin"]), stream="cs")
    t_mk = ctx.dma("gpsimd", lambda h: h.dma_start(
        out=msk[:].rearrange("p (a t) -> p a t", a=4),
        in_=d["masks"].rearrange("a p t -> p a t")), stream="mk")

    ln_state = [None]
    ln_toks_all = [None] * (CH * 5)

    def wln(i):
        if i == 0:
            src_ap = d["halo"]
        else:
            src_ap = d["xin"][:, :, (i - 1) * WIN:i * WIN]
        t_x = ctx.dma("gpsimd", lambda h, src_ap=src_ap: h.dma_start(
            out=lns.xb[:].rearrange("p (c t) -> p c t", c=CH),
            in_=src_ap), waits=[ln_state[0]], stream="xw")
        src = [lns.xb[:, c * WIN:(c + 1) * WIN] for c in range(CH)]
        outc = [ln1[:, c * TH + i * WIN: c * TH + (i + 1) * WIN] for c in range(CH)]
        toks, ln_state[0] = emit_ln(ctx, lns, src, outc, [t_x])
        for c in range(CH):
            ln_toks_all[c * 5 + i] = toks[c]

    def load_w(name, oc, buf, waits):
        ap = d[name][oc]
        dv = buf[:].rearrange("p (kc o) -> p kc o", kc=CH)
        return ctx.dma("gpsimd", lambda h, ap=ap, dv=dv: h.dma_start(out=dv, in_=ap),
                       waits=waits, stream="wp")

    def rope_evac(proj, h_, ps, t0, F, mm_tok, war, sbi, sb_war):
        dst = qkstg[sbi]
        off = 0
        if proj == "v":
            ev = ctx.op("scalar", lambda s: s.activation(
                out=vstg[:, 0:F], in_=ps[:, 0:F], func=AF.Copy),
                waits=[mm_tok, sb_war, vt_war[0]])
            nq = F // 128
            vt_ps = PA[:, 0:256].bitcast(BF16)
            tts = []
            for q in range(nq):
                tk = ctx.op("tensor", lambda t, q=q: t.transpose(
                    vt_ps[:, q * 128:(q + 1) * 128],
                    vstg[:, q * 128:(q + 1) * 128], idn[:]),
                    waits=[ev, t_id, vt_war[1]])
                tts.append(tk)
            ev2 = ctx.op("scalar", lambda s, nq=nq: s.activation(
                out=vt_sb[:, 0:nq * 128], in_=vt_ps[:, 0:nq * 128], func=AF.Copy),
                waits=tts)
            vt_war[1] = ev2
            tc0 = t0 // 128
            dstv = d["v_s"][:].rearrange("p (tc o) -> p tc o", o=D)[
                :, tc0:tc0 + nq, h_ * 128:(h_ + 1) * 128]
            t_st = ctx.dma("scalar", lambda h, dstv=dstv, nq=nq: h.dma_start(
                out=dstv,
                in_=vt_sb[:, 0:nq * 128].rearrange("p (q o) -> p q o", o=128)),
                waits=[ev2], stream="qkst" + str(sbi))
            vt_war[0] = t_st
            return t_st, ev
        scale = float(DH) ** -0.5 if proj == "q" else 1.0
        rp_, ph_, ra_, rb_ = rp_sb[sbi], ph2[sbi], ra[sbi], rb2[sbi]
        ev_e = ctx.op("scalar", lambda s: s.activation(
            out=rp_[:, 0:F], in_=ps[0:64, 0:F], func=AF.Copy, scale=scale),
            waits=[mm_tok] + war[0:1])
        ev = ctx.op("scalar", lambda s: s.activation(
            out=ph_[:, 0:F], in_=ps[64:128, 0:F], func=AF.Copy, scale=scale),
            waits=[mm_tok] + war[0:1])
        ctok = t0
        ce = cs_sb[:, ctok:ctok + F]
        se = cs_sb[:, TH + ctok:TH + ctok + F]
        t_a = ctx.op("vector", lambda v: v.tensor_mul(ra_[:, 0:F], rp_[:, 0:F], ce),
                     waits=[ev_e, t_cs] + war[1:2])
        t_b = ctx.op("vector", lambda v: v.tensor_mul(rb_[:, 0:F], ph_[:, 0:F], se),
                     waits=[ev, t_cs] + war[2:3])
        o_r = ctx.op("vector", lambda v: v.tensor_sub(
            dst[0:64, off:off + F], ra_[:, 0:F], rb_[:, 0:F]), waits=[t_a, t_b])
        t_a2 = ctx.op("vector", lambda v: v.tensor_mul(ra_[:, 0:F], rp_[:, 0:F], se),
                      waits=[o_r])
        t_b2 = ctx.op("vector", lambda v: v.tensor_mul(rb_[:, 0:F], ph_[:, 0:F], ce),
                      waits=[o_r])
        o_i = ctx.op("vector", lambda v: v.tensor_add(
            dst[64:128, off:off + F], ra_[:, 0:F], rb_[:, 0:F]), waits=[t_a2, t_b2])
        if proj == "q":
            w0 = (t0 - WIN) // WIN
            dap = d["q_s"][:].rearrange("p (w c t) -> p w c t", w=NW, c=CH)[
                :, w0:w0 + F // WIN, h_, :]
        else:
            tc0 = t0 // WIN
            dap = d["k_s"][:].rearrange("p (tc c t) -> p tc c t", tc=TH // WIN, c=CH)[
                :, tc0:tc0 + F // WIN, h_, :]
        t_st = ctx.dma("scalar", lambda h, dap=dap, F=F: h.dma_start(
            out=dap, in_=dst[:, 0:F].rearrange("p (a t) -> p a t", t=WIN)),
            waits=[o_r, o_i], stream="qkst" + str(sbi))
        return t_st, ev

    wp_busy = [None, None]
    pq_busy = [None, None, None]
    qk_done = []
    vt_war = [None, None]
    ld = load_w("wqT", 0, wbuf[0], [wp_busy[0]])
    wln(1)
    wln(2)
    kvtiles = [(0, 512), (512, 512), (1024, 256)]
    seq = ([("q", "wqT", oc, [(WIN, 512)]) for oc in range(CH)]
           + [("q", "wqT", oc, [(WIN + 512, 512)]) for oc in range(CH)]
           + [("k", "wkT", oc, kvtiles) for oc in range(CH)]
           + [("v", "wvT", oc, kvtiles) for oc in range(CH)])
    hooks = {4: 3, 12: 4, 20: 0}
    for idx, (proj, wname, oc, ftiles) in enumerate(seq):
        if idx in hooks:
            wln(hooks[idx])
        bi = idx % 2
        nld = None
        if idx + 1 < len(seq):
            nproj, nw, noc, _nt = seq[idx + 1]
            nld = load_w(nw, noc, wbuf[1 - bi], [wp_busy[1 - bi]])
        for fi, (t0, F) in enumerate(ftiles):
            pbi = len(qk_done) % 3
            ps = (PB0, PB1, PD)[pbi]
            ivals = list(range(t0 // WIN, (t0 + F - 1) // WIN + 1))
            for kc in range(CH):
                rh = ln1[:, kc * TH + t0: kc * TH + t0 + F]
                w = [ld] + [ln_toks_all[kc * 5 + i] for i in ivals] + (
                    [pq_busy[pbi]] if kc == 0 else [])
                mm = ctx.op("tensor", lambda t, wt=wbuf[bi][:, kc * 128:(kc + 1) * 128],
                            rh=rh, kc=kc, ps=ps, F=F: t.matmul(
                    ps[:, 0:F], wt, rh, start=(kc == 0), stop=(kc == CH - 1)), waits=w)
            sbi = len(qk_done) % 2
            sb_war = qk_done[-2][0] if len(qk_done) >= 2 else None
            o_i, ev = rope_evac(proj, oc, ps, t0, F, mm,
                                [sb_war if sb_war else None] * 3, sbi, sb_war)
            pq_busy[pbi] = ev
            qk_done.append((o_i, ev))
        wp_busy[bi] = mm
        if nld is not None:
            ld = nld
    qk_all = qk_done[-1][0]
    qk_all2 = qk_done[-2][0]

    v_all = qk_all

    lns2 = LNScratch(ctx, 512, "at2", xb_alias=bo_view, ps_alias=lns.ps)
    out_final = []
    st_tok = {}
    win_war = [None, None, None]
    pend = []

    def attn_view(pp, kc):
        if pp == 0:
            return attn_sb[:, kc * 512:(kc + 1) * 512]
        if kc < 8:
            return ln1[:, 4 * CH * WIN + kc * 512: 4 * CH * WIN + (kc + 1) * 512]
        return lns.xb[:, (kc - 8) * 512:(kc - 7) * 512]

    def make_wo_steps(w_, pair_toks, attn_toks, pp):
        st = {"ld": ctx.dma("gpsimd", lambda h: h.dma_start(
            out=wbuf[0][:].rearrange("p (kc o) -> p kc o", kc=CH), in_=d["woT"][0]),
            waits=[wp_busy[0]], stream="wp")}
        ev_wo = {}

        def step_oc(oc):
            def run():
                bi = oc % 2
                nld = None
                if oc + 1 < CH:
                    nld = ctx.dma("gpsimd", lambda h, oc=oc, bi=bi: h.dma_start(
                        out=wbuf[1 - bi][:].rearrange("p (kc o) -> p kc o", kc=CH),
                        in_=d["woT"][oc + 1]),
                        waits=[wp_busy[1 - bi]], stream="wp2" if bi == 0 else "wp")
                for kc in range(CH):
                    rh = attn_view(pp, kc)
                    w = [st["ld"], attn_toks[-1]] + pair_toks[-1:] + (
                        [ev_wo.get(oc - 2)] if kc == 0 else [])
                    mm_o = ctx.op("tensor", lambda t, wt=wbuf[bi][:, kc * 128:(kc + 1) * 128],
                                  rh=rh, kc=kc, oc=oc: t.matmul(
                        PA[:, (oc % 2) * 512:(oc % 2) * 512 + 512], wt, rh,
                        start=(kc == 0), stop=(kc == CH - 1)), waits=w)
                wp_busy[bi] = mm_o
                ev_wo[oc] = ctx.op("scalar", lambda s, oc=oc: s.activation(
                    out=bo_view[:, oc * 512:(oc + 1) * 512],
                    in_=PA[:, (oc % 2) * 512:(oc % 2) * 512 + 512], func=AF.Copy),
                    waits=[mm_o])
                if nld is not None:
                    st["ld"] = nld
            return run

        def step_ln():
            t0p = (w_ - 1) * WIN
            bo = [bo_view[:, c * 512:(c + 1) * 512] for c in range(CH)]
            xrs = [xres[:, c * TLOC + t0p: c * TLOC + t0p + 512] for c in range(CH)]
            toks, _dn = emit_ln(ctx, lns2, bo, xrs,
                                [ev_wo[c] for c in range(CH)] + [xres_tok],
                                residual=(xrs, None))
            out_final.extend(toks)
        return [step_oc(oc) for oc in range(CH)] + [step_ln]

    pair_attn_toks = []
    for w_ in range(NW):
        pp = (w_ // 2) & 1
        t_qw = ctx.dma("sync", lambda h, w_=w_: h.dma_start(
            out=qwin[:], in_=d["q_s"][:, w_ * CH * WIN:(w_ + 1) * CH * WIN]),
            waits=[qk_all, qk_all2, win_war[0]], stream="qw")
        t_kw = ctx.dma("sync", lambda h, w_=w_: h.dma_start(
            out=kwin[:], in_=d["k_s"][:, w_ * CH * WIN: (w_ + 2) * CH * WIN]),
            waits=[qk_all, qk_all2, win_war[1]], stream="kw")
        t_vw = ctx.dma("sync", lambda h, w_=w_: h.dma_start(
            out=vwin[:], in_=d["v_s"][:, w_ * 2 * D: (w_ * 2 + 4) * D]),
            waits=[v_all, qk_all2, win_war[2]], stream="vw")
        ev_av = {}
        NIT = HEADS * 2
        g0 = w_ * NIT
        sc_ps = [PB0[:], PA[:, 1024:1536]]
        av_ps = [PD[:], PA[:, 1536:2048]]
        pt_ps = [PB1[:, 0:256].bitcast(BF16), PB1[:, 256:512].bitcast(BF16)]

        def stageA_mm(i):
            h_, qc = i // 2, i % 2
            ps_sc = sc_ps[i % 2]
            warp2 = st_tok.get(g0 + i - 2, {})
            krhs = kwin[:].rearrange("p (tc c t) -> p tc c t", tc=2, c=CH)[:, :, h_, :]
            mm_s = ctx.op("tensor", lambda t, h_=h_, qc=qc, ps_sc=ps_sc, krhs=krhs: t.matmul(
                ps_sc, qwin[:, h_ * WIN + qc * 128: h_ * WIN + qc * 128 + 128],
                krhs, start=True, stop=True),
                waits=[t_qw, t_kw, warp2.get("exp")])
            st_tok.setdefault(g0 + i, {})["mm_s"] = mm_s

        def stageA(i):
            h_, qc = i // 2, i % 2
            ri = i % 3
            P_sb = p_sbs[ri]
            ps_sc = sc_ps[i % 2]
            ncol = ri * 4
            warp = st_tok.get(g0 + i - 3, {})
            mm_s = st_tok[g0 + i]["mm_s"]
            t_p = ctx.op("scalar", lambda s, ps_sc=ps_sc, P_sb=P_sb: s.activation(
                out=P_sb[:], in_=ps_sc, func=AF.Exp),
                waits=[mm_s, warp.get("pt_done")])
            moff = ((0 if w_ == 0 else 1) * 2 + qc) * 512
            t_pz = ctx.op("vector", lambda v, moff=moff, P_sb=P_sb, ncol=ncol: v.scalar_tensor_tensor(
                out=P_sb[:], in0=P_sb[:], scalar=1.0, in1=msk[:, moff:moff + 512],
                op0=ALU.mult, op1=ALU.mult, accum_out=cols[:, ncol + 1:ncol + 2]),
                waits=[t_p, t_mk])
            t_rr = ctx.op("vector", lambda v, ncol=ncol: v.reciprocal(
                cols[:, ncol + 2:ncol + 3], cols[:, ncol + 1:ncol + 2]), waits=[t_pz])
            t_pn = ctx.op("vector", lambda v, P_sb=P_sb, ncol=ncol: v.tensor_scalar_mul(
                P_sb[:], P_sb[:], cols[:, ncol + 2:ncol + 3]), waits=[t_rr])
            st_tok[g0 + i].update(exp=t_p, pz=t_pz, pn=t_pn)

        def stageB(i):
            ri = i % 3
            P_sb = p_sbs[ri]
            ptps = pt_ps[i % 2]
            PT_s = pt_sbs[ri]
            warp = st_tok.get(g0 + i - 2, {})
            tts = []
            for kk in range(4):
                tk = ctx.op("tensor", lambda t, kk=kk, P_sb=P_sb, ptps=ptps: t.transpose(
                    ptps[:, kk * 128:(kk + 1) * 128],
                    P_sb[:, kk * 128:(kk + 1) * 128], idn[:]),
                    waits=[st_tok[g0 + i]["pn"], t_id, warp.get("pt_ev")])
                tts.append(tk)
            ev_pt = ctx.op("scalar", lambda s, ptps=ptps, PT_s=PT_s: s.activation(
                out=PT_s[:], in_=ptps[:, 0:512], func=AF.Copy),
                waits=tts + [st_tok.get(g0 + i - 3, {}).get("av_mm")])
            st_tok[g0 + i].update(pt_ev=ev_pt, pt_done=tts[-1])

        def stageC(i):
            h_, qc = i // 2, i % 2
            ri = i % 3
            PT_s = pt_sbs[ri]
            warp = st_tok.get(g0 + i - 2, {})
            pav = av_ps[i % 2]
            for kk in range(4):
                mm_av = ctx.op("tensor", lambda t, h_=h_, kk=kk, PT_s=PT_s, pav=pav: t.matmul(
                    pav[:, 0:128],
                    vwin[:, kk * D + h_ * 128: kk * D + h_ * 128 + 128],
                    PT_s[:, kk * 128:(kk + 1) * 128],
                    start=(kk == 0), stop=(kk == 3)),
                    waits=[t_vw, st_tok[g0 + i]["pt_ev"], warp.get("av_ev")])
            ccol = (w_ % 2) * WIN + qc * 128
            dstv = attn_view(pp, h_)[:, ccol:ccol + 128]
            ev = ctx.op("scalar", lambda s, dstv=dstv, pav=pav: s.activation(
                out=dstv, in_=pav[:, 0:128], func=AF.Copy),
                waits=[mm_av])
            st_tok[g0 + i].update(av_mm=mm_av, av_ev=ev)
            ev_av[(h_, qc)] = ev

        for i in range(NIT + 2):
            if i < NIT:
                stageA_mm(i)
            if 1 <= i < NIT + 1:
                stageB(i - 1)
            if i >= 2:
                stageC(i - 2)
            if i < NIT:
                stageA(i)
            if pend and i % 2 == 0:
                pend.pop(0)()
        while pend:
            pend.pop(0)()
        attn_toks = [ev_av[(h_, qc)] for h_ in range(HEADS) for qc in range(2)]
        win_war = [st_tok[g0 + NIT - 1]["exp"], st_tok[g0 + NIT - 1]["exp"],
                   st_tok[g0 + NIT - 1]["av_mm"]]
        if w_ % 2 == 0:
            pair_attn_toks = attn_toks
            continue
        pend = make_wo_steps(w_, pair_attn_toks, attn_toks, pp)
    while pend:
        pend.pop(0)()
    return out_final



def build_prog_l0attn():
    nc = bass.Bass("TRN2", target_bir_lowering=False)
    d = {}
    d["xin"] = nc.dram_tensor("xin", [128, CH, TLOC], F32, kind="ExternalInput").ap()
    d["halo"] = nc.dram_tensor("halo", [128, CH, WIN], F32, kind="ExternalInput").ap()
    d["wqT"] = nc.dram_tensor("wqT", [CH, 128, CH, 128], F32, kind="ExternalInput").ap()
    d["wkT"] = nc.dram_tensor("wkT", [CH, 128, CH, 128], F32, kind="ExternalInput").ap()
    d["wvT"] = nc.dram_tensor("wvT", [CH, 128, CH, 128], F32, kind="ExternalInput").ap()
    d["woT"] = nc.dram_tensor("woT", [CH, 128, CH, 128], F32, kind="ExternalInput").ap()
    d["cossin"] = nc.dram_tensor("cossin", [64, 2 * (TLOC + WIN)], F32, kind="ExternalInput").ap()
    d["masks"] = nc.dram_tensor("masks", [4, 128, 512], F32, kind="ExternalInput").ap()
    d["ident"] = nc.dram_tensor("ident", [128, 128], F32, kind="ExternalInput").ap()
    d["q_s"] = nc.dram_tensor("q_s", [128, CH * TLOC], BF16, kind="Internal").ap()
    d["k_s"] = nc.dram_tensor("k_s", [128, CH * (TLOC + WIN)], BF16, kind="Internal").ap()
    d["v_s"] = nc.dram_tensor("v_s", [128, CH * (TLOC + WIN)], BF16, kind="Internal").ap()
    d["xout"] = nc.dram_tensor("xout", [128, CH, TLOC], F32, kind="ExternalOutput").ap()

    ctx = Ctx(nc)
    with ctx.es:
        xres = ctx.es.enter_context(nc.sbuf_tensor("xres", [128, CH * TLOC], F32))
        blk = ctx.es.enter_context(nc.Block())
        t_x = ctx.dma("sync", lambda h: h.dma_start(
            out=xres[:], in_=d["xin"].rearrange("p c t -> p (c t)")), stream="x")
        toks = emit_l0attn(ctx, d, xres, t_x)
        st = ctx.dma("sync", lambda h: h.dma_start(
            out=d["xout"].rearrange("p c t -> p (c t)"), in_=xres[:]),
            waits=toks, stream="xo")
        ctx.wait("sync", st)
        ctx.finish(blk)
    return nc



PAIR_GROUPS = [[0, 1], [2, 3], [4, 5], [6, 7]]


def emit_l1ssm(ctx, xin_sb, xin_toks, d):
    nc, es = ctx.nc, ctx.es
    Tt = 512
    NDC = 8
    TB = 256
    NTB = S // TB
    XSC = 8 * TLOC
    PW1 = XSC + 2 * TLOC + 128

    wk = es.enter_context(nc.sbuf_tensor("ss_wk", [128, 12288], F32))
    wkb = wk[:].bitcast(BF16)
    xs_g = es.enter_context(nc.sbuf_tensor("ss_xsg", [128, NDC * S], BF16))
    dtr_g = es.enter_context(nc.sbuf_tensor("ss_dtrg", [128, 2 * S], BF16))
    dtr_m = es.enter_context(nc.sbuf_tensor("ss_dtrm", [128, 2 * TLOC], BF16))
    bct_m = es.enter_context(nc.sbuf_tensor("ss_bctm", [128, 128], BF16))
    bct_g = es.enter_context(nc.sbuf_tensor("ss_bctg", [128, 256], BF16))
    y_sb = xs_g
    Bb = [es.enter_context(nc.sbuf_tensor(f"ss_Bb{i}", [128, TB * DST], BF16)) for i in range(2)]
    Cb = [es.enter_context(nc.sbuf_tensor(f"ss_Cb{i}", [128, TB * DST], BF16)) for i in range(2)]
    xpj = es.enter_context(nc.sbuf_tensor("ss_xpj", [128, CH * 272], BF16))
    wdt = es.enter_context(nc.sbuf_tensor("ss_wdt", [128, 2 * TLOC], BF16))
    hc = es.enter_context(nc.sbuf_tensor("ss_hc", [128, NDC * DST], F32))
    A_sb = es.enter_context(nc.sbuf_tensor("ss_A", [128, NDC * DST], F32))
    vecs = es.enter_context(nc.sbuf_tensor("ss_vecs_sb", [128, 2 * NDC + NDC * DST + 1], F32))
    dxt = [es.enter_context(nc.sbuf_tensor(f"ss_dxt{i}", [128, TB], BF16)) for i in range(2)]
    yred = [es.enter_context(nc.sbuf_tensor(f"ss_yred{i}", [128, TB], F32)) for i in range(2)]
    dlt = [es.enter_context(nc.sbuf_tensor(f"ss_dlt{i}", [128, TB], F32)) for i in range(2)]
    sp1 = es.enter_context(nc.sbuf_tensor("ss_sp1", [128, TB], F32))
    sp2 = es.enter_context(nc.sbuf_tensor("ss_sp2", [128, TB], F32))
    sp1b = es.enter_context(nc.sbuf_tensor("ss_sp1b", [128, TB], F32))
    sp2b = es.enter_context(nc.sbuf_tensor("ss_sp2b", [128, TB], F32))
    idn = es.enter_context(nc.sbuf_tensor("ss_idn", [16, 16], BF16))
    bcev = es.enter_context(nc.sbuf_tensor("ss_bcev", [16, Tt], BF16))
    stg = None
    xs_m = wkb[:, 8192:8192 + NDC * TLOC]
    stg = xs_g[:, 0:NDC * Tt]
    a_v = [wk[:, 0:2048], wk[:, 4096:6144]]
    h_v = [wk[:, 2048:4096], wk[:, 6144:8192]]
    u_v = [wkb[:, 16384:18432], wkb[:, 20480:22528]]
    pr_v = [wkb[:, 18432:20480], wkb[:, 22528:24576]]

    pp = [es.enter_context(nc.psum_tensor(f"ss_pp{i}", [128, 512], F32)) for i in range(2)]
    pbc = es.enter_context(nc.psum_tensor("ss_pbc", [128, 16], BF16))
    lns = LNScratch(ctx, Tt, "ss", xb_alias=wkb[:, 0:CH * Tt])

    bdt = vecs[:, 0:NDC]
    Dcol = vecs[:, NDC:2 * NDC]
    Alog = vecs[:, 2 * NDC:2 * NDC + NDC * DST]
    msel = vecs[:, 2 * NDC + NDC * DST:2 * NDC + NDC * DST + 1]

    t_vec = ctx.dma("sync", lambda h: h.dma_start(out=vecs[:], in_=d["ss_vecs"]), stream="sv")
    t_xp = ctx.dma("gpsimd", lambda h: h.dma_start(
        out=xpj[:].rearrange("p (kc o) -> p kc o", kc=CH),
        in_=d["xprojT"]), stream="sxp")
    t_wd = ctx.dma("gpsimd", lambda h: h.dma_start(
        out=wdt[:].rearrange("p (kc o) -> p kc o", kc=2),
        in_=d["wdtT"].rearrange("kc p o -> p kc o")), stream="swd")
    t_idn = ctx.dma("gpsimd", lambda h: h.dma_start(out=idn[:], in_=d["ident"][0:16, 0:16]),
                    stream="sid")
    t_A0 = ctx.op("scalar", lambda s: s.activation(
        out=A_sb[:], in_=Alog, func=AF.Exp), waits=[t_vec])
    t_A = ctx.op("vector", lambda v: v.tensor_scalar_mul(A_sb[:], A_sb[:], -1.0),
                 waits=[t_A0])
    t_hc = ctx.op("vector", lambda v: v.memset(hc[:], 0.0))
    selbuf = es.enter_context(nc.sbuf_tensor("ss_selb", [128, 2], F32))
    selbi = es.enter_context(nc.sbuf_tensor("ss_selbi", [128, 2], mybir.dt.int32))
    t_s0 = ctx.op("vector", lambda v: v.scalar_tensor_tensor(
        out=selbuf[:, 0:1], in0=msel, scalar=-1.0, in1=msel,
        op0=ALU.mult, op1=ALU.bypass), waits=[t_vec])
    t_s1 = ctx.op("vector", lambda v: v.tensor_scalar(
        out=selbuf[:, 0:1], in0=msel, scalar1=-1.0, scalar2=1.0,
        op0=ALU.mult, op1=ALU.add), waits=[t_vec, t_s0])
    t_s2i = ctx.op("vector", lambda v: v.tensor_copy(
        out=selbi[:, 0:1], in_=selbuf[:, 0:1]), waits=[t_s1])
    t_s3i = ctx.op("vector", lambda v: v.tensor_copy(
        out=selbi[:, 1:2], in_=msel), waits=[t_vec, t_s2i])
    t_s1 = t_s3i
    i_am_s0 = selbi[:, 0:1]
    mselm = selbi[:, 1:2]

    cc1w = []
    dtr_ev = {}
    bct_tr = []
    for tt in range(2):
        t0 = tt * Tt
        src = [xin_sb[:, c * TLOC + t0: c * TLOC + t0 + Tt] for c in range(CH)]
        outc = []
        for c in range(CH):
            if c < NDC:
                outc.append(xs_m[:, c * TLOC + t0: c * TLOC + t0 + Tt])
            else:
                outc.append(stg[:, (c - NDC) * Tt:(c - NDC + 1) * Tt])
        ln_toks, _ = emit_ln(ctx, lns, src, outc, list(xin_toks) + cc1w)
        mm_last = {}
        for och in range(3):
            ps = pp[och % 2]
            hi = 128 if och < 2 else 16
            for kc in range(CH):
                w = [t_xp, ln_toks[kc]] + ([dtr_ev.get((tt, och - 2))] if kc == 0 else [])
                mm = ctx.op("tensor", lambda t, kc=kc, och=och, ps=ps, hi=hi, oc2=outc[kc]: t.matmul(
                    ps[0:hi, 0:Tt],
                    xpj[:, kc * 272 + och * 128: kc * 272 + och * 128 + hi],
                    oc2, start=(kc == 0), stop=(kc == CH - 1)), waits=w)
            mm_last[och] = mm
            if och < 2:
                dtr_ev[(tt, och)] = ctx.op("scalar", lambda s, t0=t0, och=och, ps=ps: s.activation(
                    out=dtr_m[:, och * TLOC + t0: och * TLOC + t0 + Tt],
                    in_=ps[:, 0:Tt], func=AF.Copy), waits=[mm])
            else:
                ev_bc = ctx.op("scalar", lambda s, ps=ps: s.activation(
                    out=bcev[:], in_=ps[0:16, 0:Tt], func=AF.Copy),
                    waits=[mm, bct_tr[-1] if bct_tr else None])
                for q in range(4):
                    tk = ctx.op("tensor", lambda t, q=q: t.transpose(
                        pbc[:], bcev[0:16, q * 128:(q + 1) * 128],
                        idn[:]), waits=[ev_bc, t_idn, bct_tr[-1] if bct_tr else None])
                    tk2 = ctx.op("scalar", lambda s, q=q, tt=tt: s.activation(
                        out=bct_m[:, (tt * 4 + q) * 16:(tt * 4 + q + 1) * 16],
                        in_=pbc[:], func=AF.Copy), waits=[tk])
                    bct_tr.append(tk2)
        cc1w.append(ctx.dma("sync", lambda h, t0=t0: h.dma_start(
            out=d["cc1_in"][:, 0:XSC].rearrange(
                "p (c t) -> p c t", c=NDC)[:, :, t0:t0 + Tt],
            in_=stg[:].rearrange("p (c t) -> p c t", c=NDC)),
            waits=[mm_last[0], mm_last[1], mm_last[2]], stream="c1x"))

    t_c1d = ctx.dma("sync", lambda h: h.dma_start(
        out=d["cc1_in"][:, XSC:XSC + 2 * TLOC], in_=dtr_m[:]),
        waits=[dtr_ev[(0, 0)], dtr_ev[(0, 1)], dtr_ev[(1, 0)], dtr_ev[(1, 1)]], stream="c1d")
    t_c1b = ctx.dma("sync", lambda h: h.dma_start(
        out=d["cc1_in"][:, XSC + 2 * TLOC:XSC + 2 * TLOC + 128], in_=bct_m[:]),
        waits=[bct_tr[-1]], stream="c1b")

    t_cc1 = ctx.op("gpsimd", lambda g: g.collective_compute(
        "AllGather", ALU.bypass, replica_groups=PAIR_GROUPS,
        ins=[d["cc1_in"].opt()], outs=[d["cc1_out"].opt()]),
        waits=cc1w + [t_c1d, t_c1b])

    def half_slices(buf, nseg, seglen):
        return [buf[:].rearrange("p (c s t) -> p c s t", c=nseg, s=2)[:, :, sl, :]
                for sl in range(2)]
    xsg_h = half_slices(xs_g, NDC, TLOC)
    def pred_fix(buf, nseg, local, mask_ap, sl, waits):
        toks = []
        for c in range(nseg):
            o = c * 2 * TLOC + sl * TLOC
            tk = ctx.op("vector", lambda v, o=o, c=c, mask_ap=mask_ap, buf=buf, local=local: v.copy_predicated(
                buf[:, o:o + TLOC], mask_ap.broadcast_to([128, TLOC]),
                local[:, c * TLOC:(c + 1) * TLOC]), waits=waits + toks[-1:])
            toks.append(tk)
        return toks[-1]
    t_g0 = ctx.dma("sync", lambda h: h.dma_start(
        out=xsg_h[0], in_=d["cc1_out"][0:128, 0:XSC].rearrange("p (c t) -> p c t", c=NDC)),
        waits=[t_cc1], stream="g0")
    t_g1 = ctx.dma("sync", lambda h: h.dma_start(
        out=xsg_h[1], in_=d["cc1_out"][128:256, 0:XSC].rearrange("p (c t) -> p c t", c=NDC)),
        waits=[t_cc1], stream="g1")
    t_fx0 = pred_fix(xs_g, NDC, xs_m, i_am_s0, 0, [t_g0, t_s1])
    t_fx1 = pred_fix(xs_g, NDC, xs_m, mselm, 1, [t_g1, t_s1, t_fx0])
    xs_g_tok = t_fx1
    dtg_h = half_slices(dtr_g, 2, TLOC)
    t_d0 = ctx.dma("sync", lambda h: h.dma_start(
        out=dtg_h[0], in_=d["cc1_out"][0:128, XSC:XSC + 2 * TLOC].rearrange(
            "p (c t) -> p c t", c=2)), waits=[t_cc1], stream="d0")
    t_d1 = ctx.dma("sync", lambda h: h.dma_start(
        out=dtg_h[1], in_=d["cc1_out"][128:256, XSC:XSC + 2 * TLOC].rearrange(
            "p (c t) -> p c t", c=2)), waits=[t_cc1], stream="d1")
    t_fd0 = pred_fix(dtr_g, 2, dtr_m[:], i_am_s0, 0, [t_d0, t_s1])
    t_fd1 = pred_fix(dtr_g, 2, dtr_m[:], mselm, 1, [t_d1, t_s1, t_fd0])
    dtr_g_tok = t_fd1
    btg_h = half_slices(bct_g, 1, 128)
    t_b0 = ctx.dma("sync", lambda h: h.dma_start(
        out=btg_h[0].squeeze(1), in_=d["cc1_out"][0:128, XSC + 2 * TLOC:XSC + 2 * TLOC + 128]),
        waits=[t_cc1], stream="b0")
    t_b1 = ctx.dma("sync", lambda h: h.dma_start(
        out=btg_h[1].squeeze(1), in_=d["cc1_out"][128:256, XSC + 2 * TLOC:XSC + 2 * TLOC + 128]),
        waits=[t_cc1], stream="b1")
    t_fb0 = ctx.op("vector", lambda v: v.copy_predicated(
        btg_h[0].squeeze(1), i_am_s0.broadcast_to([128, 128]), bct_m[:]),
        waits=[t_b0, t_s1])
    t_fb1 = ctx.op("vector", lambda v: v.copy_predicated(
        btg_h[1].squeeze(1), mselm.broadcast_to([128, 128]), bct_m[:]),
        waits=[t_b1, t_s1, t_fb0])
    t_bsB = ctx.dma("sync", lambda h: h.dma_start(
        out=d["bc_sB"].rearrange("(s tc p) n -> p s tc n", s=2, p=128),
        in_=bct_g[:].rearrange("p (s tc n) -> p s tc n", s=2, n=16)[:, :, :, 0:8]),
        waits=[t_fb1], stream="bs")
    t_bs = ctx.dma("sync", lambda h: h.dma_start(
        out=d["bc_sC"].rearrange("(s tc p) n -> p s tc n", s=2, p=128),
        in_=bct_g[:].rearrange("p (s tc n) -> p s tc n", s=2, n=16)[:, :, :, 8:16]),
        waits=[t_fb1, t_bsB], stream="bs")

    one_ap, one_tok = ctx.constf128(1.0)
    y_toks = {}
    NIT2 = NTB * NDC
    sst = {}
    ld_bc = {}
    seed_war = [xs_g_tok, dtr_g_tok]

    def it_of(tb, dc):
        return tb * NDC + dc

    def ld_stage(tb):
        pw = sst.get(it_of(tb - 2, NDC - 1), {})
        ld_b = ctx.dma("gpsimd", lambda h, tb=tb: h.dma_start(
            out=Bb[tb % 2][:],
            in_=d["bc_sB"][tb * TB:(tb + 1) * TB, :].rearrange("t n -> (t n)")
            .unsqueeze(0).broadcast_to([128, TB * DST])),
            waits=[t_bs, pw.get("u")], stream="ldb")
        ld_c = ctx.dma("gpsimd", lambda h, tb=tb: h.dma_start(
            out=Cb[tb % 2][:],
            in_=d["bc_sC"][tb * TB:(tb + 1) * TB, :].rearrange("t n -> (t n)")
            .unsqueeze(0).broadcast_to([128, TB * DST])),
            waits=[t_bs, pw.get("pr")], stream="ldc")
        ld_bc[tb] = (ld_b, ld_c)

    def s0(it):
        tb, dc = it // NDC, it % NDC
        pi = it % 2
        pw = sst.get(it - 2, {})
        ps_d = pp[pi]
        for kc in range(2):
            w = [t_wd, dtr_g_tok] + ([pw.get("sp")] if kc == 0 else [])
            mm_d = ctx.op("tensor", lambda t, kc=kc, dc=dc, tb=tb, ps_d=ps_d: t.matmul(
                ps_d[:, 0:TB],
                wdt[:, kc * TLOC + dc * 128: kc * TLOC + dc * 128 + 128],
                dtr_g[:, kc * S + tb * TB: kc * S + tb * TB + TB],
                start=(kc == 0), stop=(kc == 1)), waits=w)
        s1_, s2_ = (sp1, sp2) if pi == 0 else (sp1b, sp2b)
        t_ab = ctx.op("scalar", lambda s, dc=dc, s1_=s1_, ps_d=ps_d: s.activation(
            out=s1_[:], in_=ps_d[:, 0:TB], func=AF.Abs,
            bias=bdt[:, dc:dc + 1]), waits=[mm_d, t_vec, pw.get("dl")])
        t_rl = ctx.op("scalar", lambda s, dc=dc, s2_=s2_, ps_d=ps_d: s.activation(
            out=s2_[:], in_=ps_d[:, 0:TB], func=AF.Relu,
            bias=bdt[:, dc:dc + 1]), waits=[mm_d, t_vec, pw.get("dl")])
        t_ex = ctx.op("scalar", lambda s, s1_=s1_: s.activation(
            out=s1_[:], in_=s1_[:], func=AF.Exp, scale=-1.0), waits=[t_ab])
        t_ln2 = ctx.op("scalar", lambda s, s1_=s1_: s.activation(
            out=s1_[:], in_=s1_[:], func=AF.Ln, bias=one_ap), waits=[t_ex, one_tok])
        t_dl = ctx.op("gpsimd", lambda g, pi=pi, s1_=s1_, s2_=s2_: g.tensor_add(
            dlt[pi][:], s1_[:], s2_[:]), waits=[t_ln2, t_rl, pw.get("ax"), pw.get("dx")])
        sst.setdefault(it, {}).update(sp=t_rl, dl=t_dl)
        a3 = a_v[pi].rearrange("p (t n) -> p t n", n=DST)
        a_toks = []
        for n in range(DST):
            tk = ctx.op("scalar", lambda s, n=n, dc=dc, a3=a3, pi=pi: s.activation(
                out=a3[:, :, n], in_=dlt[pi][:], func=AF.Exp,
                scale=A_sb[:, dc * DST + n:dc * DST + n + 1]),
                waits=[t_dl, t_A] + (seed_war if it < 2 else [pw.get("sc")]))
            a_toks.append(tk)
        t_dx = ctx.op("gpsimd", lambda g, dc=dc, tb=tb, pi=pi: g.tensor_mul(
            dxt[pi][:], dlt[pi][:],
            xs_g[:, dc * S + tb * TB: dc * S + tb * TB + TB]),
            waits=[t_dl, xs_g_tok])
        sst[it].update(ax=a_toks[-1], a_toks=a_toks, dx=t_dx)

    def s1(it):
        tb, dc = it // NDC, it % NDC
        pi = it % 2
        pw = sst.get(it - 2, {})
        t_u = ctx.op("gpsimd", lambda g, pi=pi: g.tensor_tensor(
            out=u_v[pi].rearrange("p (t n) -> p t n", n=DST),
            in0=dxt[pi][:].unsqueeze(2).broadcast_to([128, TB, DST]),
            in1=Bb[tb % 2][:].rearrange("p (t n) -> p t n", n=DST), op=ALU.mult),
            waits=[sst[it]["dx"], ld_bc[tb][0]] + ([pw.get("sc")] if pw else seed_war))
        sst[it]["u"] = t_u

    def s2(it):
        tb, dc = it // NDC, it % NDC
        pi = it % 2
        h3 = h_v[pi].rearrange("p (t n) -> p t n", n=DST)
        u3 = u_v[pi].rearrange("p (t n) -> p t n", n=DST)
        a3 = a_v[pi].rearrange("p (t n) -> p t n", n=DST)
        pw = sst.get(it - 2, {})
        sc_toks = []
        for n in range(DST):
            tk = ctx.op("vector", lambda v, n=n, dc=dc, h3=h3, u3=u3, a3=a3: v.tensor_tensor_scan(
                out=h3[:, :, n], data0=a3[:, :, n], data1=u3[:, :, n],
                initial=hc[:, dc * DST + n:dc * DST + n + 1],
                op0=ALU.mult, op1=ALU.add),
                waits=[sst[it]["a_toks"][n], sst[it]["u"], t_hc]
                + ([sst.get(it - NDC, {}).get("cu")] if n == 0 else [])
                + ([pw.get("pr"), pw.get("cu")] if n == 0 and pw else
                   (seed_war if n == 0 else [])))
            sc_toks.append(tk)
        t_cu = ctx.op("scalar", lambda s, dc=dc, h3=h3: s.activation(
            out=hc[:, dc * DST:(dc + 1) * DST], in_=h3[:, TB - 1, :], func=AF.Copy),
            waits=sc_toks)
        sst[it].update(sc=sc_toks[-1], sc_toks=sc_toks, cu=t_cu)

    def s3(it):
        tb, dc = it // NDC, it % NDC
        pi = it % 2
        h3 = h_v[pi].rearrange("p (t n) -> p t n", n=DST)
        t_pr = ctx.op("gpsimd", lambda g, h3=h3, pi=pi, tb=tb: g.tensor_tensor(
            out=pr_v[pi].rearrange("p (t n) -> p t n", n=DST), in0=h3,
            in1=Cb[tb % 2][:].rearrange("p (t n) -> p t n", n=DST), op=ALU.mult),
            waits=sst[it]["sc_toks"] + [ld_bc[tb][1], sst.get(it - 2, {}).get("yr")])
        sst[it]["pr"] = t_pr

    def s4(it):
        tb, dc = it // NDC, it % NDC
        pi = it % 2
        t_yr = ctx.op("vector", lambda v, pi=pi: v.reduce_sum(
            yred[pi][:].unsqueeze(2),
            pr_v[pi].rearrange("p (t n) -> p t n", n=DST), axis=AX.X),
            waits=[sst[it]["pr"]])
        sst[it]["yr"] = t_yr
        y_toks[(tb, dc)] = ctx.op("vector", lambda v, dc=dc, tb=tb, pi=pi: v.scalar_tensor_tensor(
            out=y_sb[:, dc * S + tb * TB: dc * S + tb * TB + TB],
            in0=xs_g[:, dc * S + tb * TB: dc * S + tb * TB + TB],
            scalar=Dcol[:, dc:dc + 1], in1=yred[pi][:],
            op0=ALU.mult, op1=ALU.add), waits=[t_yr, xs_g_tok, t_vec])

    ld_stage(0)
    ld_stage(1)
    for it in range(NIT2 + 2):
        if it < NIT2:
            s0(it)
        if 1 <= it < NIT2 + 1:
            s1(it - 1)
            s2(it - 1)
        if it >= 2:
            s3(it - 2)
            s4(it - 2)
            if it % NDC == 1 and it > NDC:
                tbn = (it - 1) // NDC + 1
                if 2 <= tbn <= NTB - 1:
                    ld_stage(tbn)
    scan_done = [y_toks[(NTB - 1, dc)] for dc in range(NDC)]
    if "dbg_y" in d:
        scan_done = scan_done + [
            ctx.dma("sync", lambda h: h.dma_start(out=d["dbg_y"], in_=y_sb[:]),
                    waits=list(y_toks.values()), stream="dbgy"),
            ctx.dma("sync", lambda h: h.dma_start(out=d["dbg_xsg"], in_=xs_g[:]),
                    waits=[xs_g_tok], stream="dbgx"),
            ctx.dma("sync", lambda h: h.dma_start(out=d["dbg_dtg"], in_=dtr_g[:]),
                    waits=[dtr_g_tok], stream="dbgd")]

    ypar = wkb[:, 0:NDC * TLOC]
    t_yp = None
    for c in range(NDC):
        tk0 = ctx.op("vector", lambda v, c=c: v.tensor_copy(
            out=ypar[:, c * TLOC:(c + 1) * TLOC],
            in_=y_sb[:, c * S + TLOC: c * S + 2 * TLOC]),
            waits=scan_done + [t_s1, sst[NIT2 - 1]["cu"]] + ([t_yp] if t_yp else []))
        t_yp = ctx.op("vector", lambda v, c=c: v.copy_predicated(
            ypar[:, c * TLOC:(c + 1) * TLOC], mselm.broadcast_to([128, TLOC]),
            y_sb[:, c * S: c * S + TLOC]), waits=[tk0])
    t_c2 = ctx.dma("sync", lambda h: h.dma_start(
        out=d["cc2_in"].rearrange("p (c t) -> p c t", c=NDC), in_=ypar),
        waits=[t_yp], stream="c2")
    t_cc2 = ctx.op("gpsimd", lambda g: g.collective_compute(
        "AllGather", ALU.bypass, replica_groups=PAIR_GROUPS,
        ins=[d["cc2_in"].opt()], outs=[d["cc2_out"].opt()]), waits=[t_c2])
    ylm_b = wkb[:, 0:NDC * TLOC]
    t_ylm = None
    for c in range(NDC):
        tk0 = ctx.op("vector", lambda v, c=c: v.tensor_copy(
            out=ylm_b[:, c * TLOC:(c + 1) * TLOC],
            in_=y_sb[:, c * S: c * S + TLOC]),
            waits=[t_c2] + scan_done + ([t_ylm] if t_ylm else []))
        t_ylm = ctx.op("vector", lambda v, c=c: v.copy_predicated(
            ylm_b[:, c * TLOC:(c + 1) * TLOC], mselm.broadcast_to([128, TLOC]),
            y_sb[:, c * S + TLOC: c * S + 2 * TLOC]), waits=[tk0])
    ylo = wkb[:, NDC * TLOC:2 * NDC * TLOC]
    t_r0 = ctx.dma("sync", lambda h: h.dma_start(
        out=ylo, in_=d["cc2_out"][0:128, :]), waits=[t_cc2], stream="yr0")
    t_r1 = ctx.dma("sync", lambda h: h.dma_start(
        out=y_sb[:, 0:NDC * TLOC], in_=d["cc2_out"][128:256, :]),
        waits=[t_cc2, t_ylm] + scan_done, stream="yr1")
    t_yc = ctx.op("vector", lambda v: v.copy_predicated(
        ylo, i_am_s0.broadcast_to([128, NDC * TLOC]), y_sb[:, 0:NDC * TLOC]),
        waits=[t_r0, t_r1, t_s1])
    y_loc_tok = [t_ylm, t_yc]

    out_toks = None
    ln_war = []
    for tt in range(2):
        t0 = tt * Tt
        src = []
        for c in range(CH):
            if c < NDC:
                src.append(wkb[:, c * TLOC + t0: c * TLOC + t0 + Tt])
            else:
                src.append(wkb[:, NDC * TLOC + (c - NDC) * TLOC + t0:
                               NDC * TLOC + (c - NDC) * TLOC + t0 + Tt])
        xiw = [xin_sb[:, c * TLOC + t0: c * TLOC + t0 + Tt] for c in range(CH)]
        out_toks, _ = emit_ln(ctx, lns, src, xiw, y_loc_tok + ln_war,
                              residual=(xiw, None))
        ln_war = out_toks[:]
    return out_toks


def build_prog_l1ssm():
    nc = bass.Bass("TRN2", target_bir_lowering=False)
    d = {}
    d["xin"] = nc.dram_tensor("xin", [128, CH, TLOC], F32, kind="ExternalInput").ap()
    d["xprojT"] = nc.dram_tensor("xprojT", [128, CH, 272], F32, kind="ExternalInput").ap()
    d["wdtT"] = nc.dram_tensor("wdtT", [2, 128, TLOC], F32, kind="ExternalInput").ap()
    d["ss_vecs"] = nc.dram_tensor("ss_vecs", [128, 2 * 8 + 8 * DST + 1], F32, kind="ExternalInput").ap()
    d["ident"] = nc.dram_tensor("ident", [128, 128], F32, kind="ExternalInput").ap()
    PW1 = 8 * TLOC + 2 * TLOC + 128
    d["q_s"] = nc.dram_tensor("q_s", [128, CH * TLOC], BF16, kind="Internal").ap()
    d["k_s"] = nc.dram_tensor("k_s", [128, CH * (TLOC + WIN)], BF16, kind="Internal").ap()
    d["v_s"] = nc.dram_tensor("v_s", [128, CH * (TLOC + WIN)], BF16, kind="Internal").ap()
    d["cc1_in"] = nc.dram_tensor("cc1_in", [128, PW1], BF16, kind="Internal").ap()
    d["cc1_out"] = nc.dram_tensor("cc1_out", [256, PW1], BF16, kind="Internal").ap()
    d["cc2_in"] = nc.dram_tensor("cc2_in", [128, 8 * TLOC], BF16, kind="Internal").ap()
    d["cc2_out"] = nc.dram_tensor("cc2_out", [256, 8 * TLOC], BF16, kind="Internal").ap()
    d["bc_sB"] = nc.dram_tensor("bc_sB", [S, 8], BF16, kind="Internal").ap()
    d["bc_sC"] = nc.dram_tensor("bc_sC", [S, 8], BF16, kind="Internal").ap()
    xout = nc.dram_tensor("xout", [128, CH, TLOC], F32, kind="ExternalOutput").ap()

    import os
    dbg = os.environ.get("SSM_DEBUG") == "1"
    if dbg:
        d["dbg_y"] = nc.dram_tensor("dbg_y", [128, 8 * S], BF16, kind="ExternalOutput").ap()
        d["dbg_xsg"] = nc.dram_tensor("dbg_xsg", [128, 8 * S], BF16, kind="ExternalOutput").ap()
        d["dbg_dtg"] = nc.dram_tensor("dbg_dtg", [128, 2 * S], BF16, kind="ExternalOutput").ap()
    ctx = Ctx(nc)
    with ctx.es:
        xin_sb = ctx.es.enter_context(nc.sbuf_tensor("xin_sb", [128, CH * TLOC], F32))
        blk = ctx.es.enter_context(nc.Block())
        t_x = ctx.dma("sync", lambda h: h.dma_start(
            out=xin_sb[:], in_=d["xin"].rearrange("p c t -> p (c t)")), stream="x")
        out_toks = emit_l1ssm(ctx, xin_sb, [t_x], d)
        st = ctx.dma("sync", lambda h: h.dma_start(
            out=xout.rearrange("p c t -> p (c t)"), in_=xin_sb[:]),
            waits=out_toks, stream="xo")
        ctx.wait("sync", st)
        ctx.finish(blk)
    return nc



def _perm_headwise(wT):
    out = wT.reshape(D, HEADS, DH)
    perm = np.concatenate([np.arange(0, DH, 2), np.arange(1, DH, 2)])
    out = out[:, :, perm]
    return np.ascontiguousarray(out.reshape(D, D))


def _rot(fm, h):
    if h == 0:
        return fm
    return np.ascontiguousarray(np.concatenate([fm[:, 8:], fm[:, :8]], axis=1))


def _rot_ax(a, ax):
    n = a.shape[ax]
    h = n // 2
    idx = [slice(None)] * a.ndim
    idx1, idx2 = list(idx), list(idx)
    idx1[ax] = slice(h, None)
    idx2[ax] = slice(0, h)
    return np.ascontiguousarray(np.concatenate([a[tuple(idx1)], a[tuple(idx2)]], axis=ax))


FUSED = True


def _host_prep(inp):
    ident = np.eye(128, dtype=np.float32)
    wqT0 = host_wT_oc(_perm_headwise(np.ascontiguousarray(inp["L0_attn_Wq"].T)))
    wkT0 = host_wT_oc(_perm_headwise(np.ascontiguousarray(inp["L0_attn_Wk"].T)))
    wvT0 = host_wT_oc(np.ascontiguousarray(inp["L0_attn_Wv"].T))
    woT0 = host_wT_oc(np.ascontiguousarray(inp["L0_attn_Wo"].T))
    inv = 1.0 / (10000.0 ** (np.arange(0, DH, 2, dtype=np.float32) / DH))
    tril = np.tril(np.ones((WIN, WIN), np.float32))
    onesw = np.ones((WIN, WIN), np.float32)
    shgT0 = host_wT_oc(np.ascontiguousarray(inp["L0_ffn_sh_gate"].T))
    shuT0 = host_wT_oc(np.ascontiguousarray(inp["L0_ffn_sh_up"].T))
    shdT0 = host_wT_oc(np.ascontiguousarray(inp["L0_ffn_sh_down"].T))
    rguT0 = np.stack([host_wT_oc(np.ascontiguousarray(inp["L0_ffn_r_gate"].T)),
                      host_wT_oc(np.ascontiguousarray(inp["L0_ffn_r_up"].T))])
    rdT0 = host_wT_oc(np.ascontiguousarray(inp["L0_ffn_r_down"].T))
    routT0 = np.ascontiguousarray(inp["L0_ffn_router_weight"].T)
    xpT_full = np.ascontiguousarray(inp["L1_attn_W_xproj"].T)
    l1g0 = host_wT_oc(inp["L1_ffn_gate_w"])
    l1u0 = host_wT_oc(inp["L1_ffn_up_w"])
    l1d0 = host_wT_oc(inp["L1_ffn_down_w"])
    gsc = host_colvec(inp["L1_ffn_gate_scale"])
    usc = host_colvec(inp["L1_ffn_up_scale"])
    dsc0 = host_colvec(inp["L1_ffn_down_scale"])

    per_h = {}
    for h in (0, 1):
        t0 = h * TLOC
        pos = np.arange(t0 - WIN, t0 + TLOC, dtype=np.float32)
        fr = np.outer(pos, inv).astype(np.float32)
        cs = np.concatenate([np.cos(fr).T, np.sin(fr).T], axis=1).astype(np.float32)
        left = np.zeros((WIN, WIN), np.float32) if h == 0 else onesw
        m0 = np.concatenate([left, tril], axis=1)
        mr = np.concatenate([onesw, tril], axis=1)
        masks = np.stack([m0[0:128], m0[128:256], mr[0:128], mr[128:256]])
        myd = slice(h * 1024, (h + 1) * 1024)
        xp = xpT_full.reshape(CH, 128, 272)
        if h == 1:
            xp = np.concatenate([xp[8:], xp[:8]], axis=0)
        v = np.zeros((128, 81), np.float32)
        v[:, 0:8] = inp["L1_attn_b_dt"][myd].reshape(8, 128).T
        v[:, 8:16] = inp["L1_attn_D_param"][myd].reshape(8, 128).T
        v[:, 16:80] = inp["L1_attn_A_log"][myd].reshape(8, 128, DST).transpose(1, 0, 2).reshape(128, 64)
        v[:, 80] = float(h)
        R = (lambda a, ax: a) if h == 0 else _rot_ax
        per_h[h] = {
            "wqT": R(wqT0, 2), "wkT": R(wkT0, 2), "wvT": R(wvT0, 2), "woT": R(woT0, 0),
            "cossin": cs, "masks": masks, "ident": ident,
            "sh_gateT": R(shgT0, 2), "sh_upT": R(shuT0, 2), "sh_dT": R(shdT0, 0),
            "r_guT": R(rguT0, 3), "r_dT": R(rdT0, 0), "routerT": R(routT0, 0),
            "xprojT": np.ascontiguousarray(np.transpose(xp, (1, 0, 2))),
            "wdtT": np.ascontiguousarray(inp["L1_attn_W_dt"].T[:, myd].reshape(2, 128, TLOC)),
            "ss_vecs": v,
            "l1_gateT": R(l1g0, 2), "l1_upT": R(l1u0, 2), "l1_downT": R(l1d0, 0),
            "l1_scales": np.concatenate([gsc, usc, R(dsc0, 1)], axis=1),
        }
    x = inp["L0_attn_x"]
    in_maps = []
    for c in range(NCORE):
        b, h = c // 2, c % 2
        t0 = h * TLOC
        halo = (np.zeros((128, CH, WIN), np.float32) if h == 0
                else _rot(host_fm(x[b, t0 - WIN:t0]), h))
        m = dict(per_h[h])
        m["xin"] = _rot(host_fm(x[b, t0:t0 + TLOC]), h)
        m["halo"] = halo
        in_maps.append(m)
    return in_maps


def kernel(**inp):
    inp = {k: np.asarray(v, dtype=np.float32) for k, v in inp.items()}
    in_maps = _host_prep(inp)
    if FUSED:
        nc = build_prog_fused()
        res = run_bass_kernel_spmd(nc, in_maps, core_ids=list(range(NCORE)))
        out = np.empty((B, S, D), np.float32)
        for c in range(NCORE):
            b, h = c // 2, c % 2
            out[b, h * TLOC:(h + 1) * TLOC] = host_unfm(
                _rot(np.asarray(res.results[c]["xout"], np.float32), h))
        return out
    KEYS1 = ["xin", "halo", "wqT", "wkT", "wvT", "woT", "cossin", "masks", "ident"]
    nc = build_prog_l0attn()
    res = run_bass_kernel_spmd(nc, [{k: m[k] for k in KEYS1} for m in in_maps],
                               core_ids=list(range(NCORE)))
    xs1 = [np.asarray(res.results[c]["xout"], np.float32) for c in range(NCORE)]
    KEYS2 = ["sh_gateT", "sh_upT", "sh_dT", "r_guT", "r_dT", "routerT", "ident"]
    nc = build_prog_l0ffn()
    res = run_bass_kernel_spmd(nc, [dict({k: m[k] for k in KEYS2}, xin=xs1[c])
                                    for c, m in enumerate(in_maps)],
                               core_ids=list(range(NCORE)))
    xs2 = [np.asarray(res.results[c]["xout"], np.float32) for c in range(NCORE)]
    KEYS3 = ["xprojT", "wdtT", "ss_vecs", "ident"]
    nc = build_prog_l1ssm()
    res = run_bass_kernel_spmd(nc, [dict({k: m[k] for k in KEYS3}, xin=xs2[c])
                                    for c, m in enumerate(in_maps)],
                               core_ids=list(range(NCORE)))
    xs3 = [np.asarray(res.results[c]["xout"], np.float32) for c in range(NCORE)]
    KEYS4 = ["l1_gateT", "l1_upT", "l1_downT", "l1_scales"]
    nc = build_prog_l1ffn()
    res = run_bass_kernel_spmd(nc, [dict({k: m[k] for k in KEYS4}, xin=xs3[c])
                                    for c, m in enumerate(in_maps)],
                               core_ids=list(range(NCORE)))
    out = np.empty((B, S, D), np.float32)
    for c in range(NCORE):
        b, h = c // 2, c % 2
        out[b, h * TLOC:(h + 1) * TLOC] = host_unfm(
            _rot(np.asarray(res.results[c]["xout"], np.float32), h))
    return out



def build_prog_fused():
    nc = bass.Bass("TRN2", target_bir_lowering=False)
    d = {}
    EI = lambda n, sh: nc.dram_tensor(n, sh, F32, kind="ExternalInput").ap()
    d["xin"] = EI("xin", [128, CH, TLOC])
    d["halo"] = EI("halo", [128, CH, WIN])
    d["wqT"] = EI("wqT", [CH, 128, CH, 128])
    d["wkT"] = EI("wkT", [CH, 128, CH, 128])
    d["wvT"] = EI("wvT", [CH, 128, CH, 128])
    d["woT"] = EI("woT", [CH, 128, CH, 128])
    d["cossin"] = EI("cossin", [64, 2 * (TLOC + WIN)])
    d["masks"] = EI("masks", [4, 128, 512])
    d["ident"] = EI("ident", [128, 128])
    d["sh_gateT"] = EI("sh_gateT", [ISH // 128, 128, CH, 128])
    d["sh_upT"] = EI("sh_upT", [ISH // 128, 128, CH, 128])
    d["sh_dT"] = EI("sh_dT", [CH, 128, ISH // 128, 128])
    d["r_guT"] = EI("r_guT", [2, IRT // 128, 128, CH, 128])
    d["r_dT"] = EI("r_dT", [CH, 128, IRT // 128, 128])
    d["routerT"] = EI("routerT", [D, 8])
    d["xprojT"] = EI("xprojT", [128, CH, 272])
    d["wdtT"] = EI("wdtT", [2, 128, TLOC])
    d["ss_vecs"] = EI("ss_vecs", [128, 2 * 8 + 8 * DST + 1])
    d["l1_gateT"] = EI("l1_gateT", [IL1 // 128, 128, CH, 128])
    d["l1_upT"] = EI("l1_upT", [IL1 // 128, 128, CH, 128])
    d["l1_downT"] = EI("l1_downT", [CH, 128, IL1 // 128, 128])
    d["l1_scales"] = EI("l1_scales", [128, 2 * (IL1 // 128) + CH])
    PW1 = 8 * TLOC + 2 * TLOC + 128
    d["q_s"] = nc.dram_tensor("q_s", [128, CH * TLOC], BF16, kind="Internal").ap()
    d["k_s"] = nc.dram_tensor("k_s", [128, CH * (TLOC + WIN)], BF16, kind="Internal").ap()
    d["v_s"] = nc.dram_tensor("v_s", [128, CH * (TLOC + WIN)], BF16, kind="Internal").ap()
    d["cc1_in"] = nc.dram_tensor("cc1_in", [128, PW1], BF16, kind="Internal").ap()
    d["cc1_out"] = nc.dram_tensor("cc1_out", [256, PW1], BF16, kind="Internal").ap()
    d["cc2_in"] = nc.dram_tensor("cc2_in", [128, 8 * TLOC], BF16, kind="Internal").ap()
    d["cc2_out"] = nc.dram_tensor("cc2_out", [256, 8 * TLOC], BF16, kind="Internal").ap()
    d["bc_sB"] = nc.dram_tensor("bc_sB", [S, 8], BF16, kind="Internal").ap()
    d["bc_sC"] = nc.dram_tensor("bc_sC", [S, 8], BF16, kind="Internal").ap()
    d["x3_s"] = nc.dram_tensor("x3_s", [128, CH * TLOC], F32, kind="Internal").ap()
    d["x1_s"] = nc.dram_tensor("x1_s", [128, CH * TLOC], F32, kind="Internal").ap()
    xout = nc.dram_tensor("xout", [128, CH, TLOC], F32, kind="ExternalOutput").ap()

    ctx = Ctx(nc)
    with ctx.es:
        xres = ctx.es.enter_context(nc.sbuf_tensor("xres", [128, CH * TLOC], F32))
        blk = ctx.es.enter_context(nc.Block())
        t_x0 = ctx.dma("sync", lambda h: h.dma_start(
            out=xres[:], in_=d["xin"].rearrange("p c t -> p (c t)")), stream="x")
        with ctx.scope():
            toks = emit_l0attn(ctx, d, xres, t_x0)
        ctx.barrier()
        with ctx.scope():
            toks = emit_l0ffn(ctx, xres, toks, d)
        ctx.barrier()
        with ctx.scope():
            toks = emit_l1ssm(ctx, xres, toks, d)
        ctx.barrier()
        with ctx.scope():
            toks = emit_l1ffn(ctx, xres, toks, d, xout)
        ctx.wait("sync", *toks)
        ctx.finish(blk)
    return nc


# version 19
# speedup vs baseline: 1.0368x; 1.0368x over previous
import contextlib
import numpy as np
import ml_dtypes
import concourse.bass as bass
import concourse.mybir as mybir
from concourse.bass_utils import run_bass_kernel_spmd

F32 = mybir.dt.float32
BF16 = mybir.dt.bfloat16
FP8 = mybir.dt.float8e4
AF = mybir.ActivationFunctionType
ALU = mybir.AluOpType
AX = mybir.AxisListType

NCORE = 8
B, S, D = 4, 2048, 2048
CH = D // 128
TLOC = 1024
HEADS, DH, WIN = 16, 128, 256
ISH, IRT, IL1 = 8192, 1024, 6144
DTR, DST = 256, 8
EPS = 1e-5

ENGS = ("sync", "scalar", "vector", "tensor", "gpsimd")
ESEM = {"sync": "qs", "scalar": "ac", "vector": "ve", "tensor": "pe", "gpsimd": "po"}
QSEM = {"sync": "qs_d", "scalar": "ac_d", "gpsimd": "po_d", "vector": "ve_d"}


class Ctx:

    def __init__(self, nc):
        self.nc = nc
        self.es = contextlib.ExitStack()
        self.sem_es = self.es
        self.sems = {}
        for s in set(ESEM.values()) | set(QSEM.values()):
            self.sems[s] = self.sem_es.enter_context(nc.semaphore("S_" + s))
        self.cnt = {s: 0 for s in self.sems}
        self.ops = {e: [] for e in ENGS}
        self.waited = {e: {} for e in ENGS}
        self._cpool = self.sem_es.enter_context(nc.sbuf_tensor("constpool", [1, 32], mybir.dt.float32))
        self._cpool128 = self.sem_es.enter_context(
            nc.sbuf_tensor("constpool128", [128, 8], mybir.dt.float32))
        self._n128 = 0
        self._cmap = {}

    def dma(self, eng, f, waits=(), stream="misc"):
        if stream not in self.sems:
            self.sems[stream] = self.sem_es.enter_context(self.nc.semaphore("S_" + stream))
            self.cnt[stream] = 0
        prev = (stream, self.cnt[stream]) if self.cnt[stream] else None
        sem = stream
        need = []
        for t in list(waits) + [prev]:
            if t is None:
                continue
            s, v = t
            if self.waited[eng].get(s, -1) < v:
                need.append((s, v))
                self.waited[eng][s] = v
        self.cnt[sem] += 16
        tok = (sem, self.cnt[sem])

        def closure(h, need=tuple(need), f=f, tok=tok):
            for s, v in need:
                h.wait_ge(self.sems[s], v)
            f(h).then_inc(self.sems[tok[0]], 16)
        self.ops[eng].append(closure)
        return tok

    def constf(self, v):
        v = float(v)
        if v not in self._cmap:
            i = len(self._cmap)
            ap = self._cpool[0:1, i:i + 1]
            tok = self.op("vector", lambda h, ap=ap, v=v: h.memset(ap, v))
            self._cmap[v] = (ap, tok)
        return self._cmap[v]

    def constf128(self, v):
        v = float(v)
        key = ("p128", v)
        if key not in self._cmap:
            i = self._n128
            self._n128 += 1
            ap = self._cpool128[:, i:i + 1]
            tok = self.op("vector", lambda h, ap=ap, v=v: h.memset(ap, v))
            self._cmap[key] = (ap, tok)
        return self._cmap[key]

    def op(self, eng, f, waits=(), inc=True, dma=False):
        sem = QSEM[eng] if dma else ESEM[eng]
        need = []
        for t in waits:
            if t is None:
                continue
            s, v = t
            if self.waited[eng].get(s, -1) < v:
                need.append((s, v))
                self.waited[eng][s] = v
        tok = None
        if inc:
            self.cnt[sem] += 16 if dma else 1
            tok = (sem, self.cnt[sem])

        def closure(h, need=tuple(need), f=f, tok=tok, dma=dma):
            for s, v in need:
                h.wait_ge(self.sems[s], v)
            ins = f(h)
            if tok is not None:
                ins.then_inc(self.sems[tok[0]], 16 if dma else 1)
        self.ops[eng].append(closure)
        return tok

    def scope(self):
        class _Scope:
            def __init__(s2):
                s2.prev = None
                s2.es = contextlib.ExitStack()
            def __enter__(s2):
                s2.prev = self.es
                self.es = s2.es
                return s2
            def __exit__(s2, *a):
                s2.es.close()
                self.es = s2.prev
                return False
        return _Scope()

    def barrier(self):
        snap = [(s, v) for s, v in self.cnt.items() if v > 0]
        for e in ENGS:
            self.wait(e, *[(s, v) for s, v in snap])

    def wait(self, eng, *toks):
        need = []
        for t in toks:
            if t is None:
                continue
            s, v = t
            if self.waited[eng].get(s, -1) < v:
                need.append((s, v))
                self.waited[eng][s] = v
        if not need:
            return

        def closure(h, need=tuple(need)):
            for s, v in need:
                h.wait_ge(self.sems[s], v)
        self.ops[eng].append(closure)

    def finish(self, block):
        for e in ENGS:
            fs = self.ops[e]
            if not fs:
                continue
            section = getattr(block, e)

            def body(h, fs=fs):
                for f in fs:
                    f(h)
            section(body)



def host_fm(a):
    T = a.shape[0]
    return np.ascontiguousarray(np.transpose(a.reshape(T, CH, 128), (2, 1, 0)))


def host_unfm(a):
    return np.ascontiguousarray(np.transpose(a, (2, 1, 0))).reshape(a.shape[2], -1)


def host_wT_oc(wT, ocs=128):
    Din, Dout = wT.shape
    a = np.transpose(wT.reshape(Din // 128, 128, Dout // ocs, ocs), (2, 1, 0, 3))
    return np.ascontiguousarray(a)


def host_colvec(v):
    n = v.shape[0] // 128
    return np.ascontiguousarray(v.reshape(n, 128).T)



class LNScratch:
    def __init__(self, ctx, Tt, tag="", xb_alias=None, ps_alias=None, pool=None):
        nc, es = ctx.nc, ctx.es
        self.Tt = Tt
        g = tag
        assert Tt <= 512
        self.xb = xb_alias
        if self.xb is None:
            self.xb = es.enter_context(nc.sbuf_tensor("ln_xb" + g, [128, CH * Tt], BF16))
        if pool is not None:
            self.t1 = pool[:, 0:Tt]
        else:
            self.t1 = es.enter_context(nc.sbuf_tensor("ln_t1" + g, [128, Tt], BF16))
        self.sq = self.t1
        self.ps = ps_alias if ps_alias is not None else es.enter_context(
            nc.psum_tensor("ln_ps" + g, [128, 512], F32))
        if pool is not None:
            pf = pool.bitcast(F32)
            rows1 = pf[:, 3 * Tt // 2: 3 * Tt // 2 + Tt]
            rows2 = pf[:, 3 * Tt // 2 + Tt: 3 * Tt // 2 + 2 * Tt]
        else:
            rows1 = es.enter_context(nc.sbuf_tensor("ln_rows1" + g, [128, Tt], F32))
            rows2 = es.enter_context(nc.sbuf_tensor("ln_rows2" + g, [128, Tt], F32))
        self.r0 = rows1[0:1, :]
        self.r1 = rows1[32:33, :]
        self.r2 = rows1[64:65, :]
        self.r3 = rows2[0:1, :]
        self.r4 = rows1[0:1, :]
        if pool is not None:
            self.bc = pool[:, Tt:3 * Tt]
        else:
            self.bc = es.enter_context(nc.sbuf_tensor("ln_bc" + g, [128, 2 * Tt], BF16))
        self.ones_col = es.enter_context(nc.sbuf_tensor("ones_col" + g, [128, 1], BF16))
        self.ones_row = es.enter_context(nc.sbuf_tensor("ones_row" + g, [1, 128], F32))
        t0 = ctx.op("vector", lambda v: v.memset(self.ones_col[:], 1.0))
        t1 = ctx.op("vector", lambda v: v.memset(self.ones_row[:], 1.0))
        self.ones_tok = [t0, t1]
        self.busy = None


def emit_ln(ctx, lns, src_chunks, out_chunks, src_tok, residual=None):
    Tt = lns.Tt
    waits = list(src_tok) + lns.ones_tok
    if lns.busy is not None:
        waits.append(lns.busy)

    xb_toks = {}
    xb_aps = []
    for c in range(CH):
        ap = src_chunks[c]
        if ap.dtype == BF16:
            xb_aps.append(ap)
        else:
            dst = lns.xb[:, c * Tt:(c + 1) * Tt]
            xb_toks[c] = ctx.op(
                "vector", lambda v, ap=ap, dst=dst: v.tensor_copy(out=dst, in_=ap),
                waits=waits)
            xb_aps.append(dst)

    ps_s = lns.ps[0:1, 0:Tt]
    for c in range(CH):
        w = waits + ([xb_toks[c]] if c in xb_toks else [])
        tk = ctx.op("tensor", lambda t, c=c, ap=xb_aps[c]: t.matmul(
            ps_s, lns.ones_col[:], ap,
            start=(c == 0), stop=(c == CH - 1)), waits=w)
    t_s1 = ctx.op("scalar", lambda s: s.activation(
        out=lns.r0, in_=ps_s, func=AF.Copy), waits=[tk])
    sq_dep = [t_s1]
    for c in range(CH):
        w = waits + ([xb_toks[c]] if c in xb_toks else [])
        sqt = ctx.op("scalar", lambda s, ap=xb_aps[c]: s.activation(
            out=lns.sq[:], in_=ap, func=AF.Square), waits=w + sq_dep)
        tk = ctx.op("tensor", lambda t, c=c: t.matmul(
            ps_s, lns.ones_col[:], lns.sq[:],
            start=(c == 0), stop=(c == CH - 1)), waits=[sqt, t_s1])
        sq_dep = [tk]
        if c == CH - 1:
            t_s2 = tk

    inv_d = 1.0 / D
    t_msq = ctx.op("scalar", lambda s: s.activation(
        out=lns.r1, in_=lns.r0, func=AF.Square, scale=inv_d), waits=[t_s1])
    t_var = ctx.op("vector", lambda v: v.scalar_tensor_tensor(
        out=lns.r2, in0=ps_s, scalar=inv_d, in1=lns.r1[:],
        op0=ALU.mult, op1=ALU.subtract), waits=[t_s2, t_msq])
    eps_ap, eps_tok = ctx.constf(EPS)
    t_sd = ctx.op("scalar", lambda s: s.activation(
        out=lns.r1, in_=lns.r2, func=AF.Sqrt, bias=eps_ap),
        waits=[t_var, t_msq, eps_tok])
    t_r = ctx.op("vector", lambda v: v.reciprocal(
        out=lns.r3, in_=lns.r1), waits=[t_sd])
    t_mr = ctx.op("vector", lambda v: v.scalar_tensor_tensor(
        out=lns.r4, in0=lns.r0, scalar=inv_d, in1=lns.r3,
        op0=ALU.mult, op1=ALU.mult), waits=[t_r])

    t_b1 = ctx.op("tensor", lambda t: t.matmul(
        lns.ps[:, 0:Tt], lns.ones_row[:], lns.r3, start=True, stop=True),
        waits=waits + [t_r, t_var])
    t_rb = ctx.op("scalar", lambda s: s.activation(
        out=lns.bc[:, 0:Tt], in_=lns.ps[:, 0:Tt], func=AF.Copy), waits=[t_b1])
    t_b2 = ctx.op("tensor", lambda t: t.matmul(
        lns.ps[:, 0:Tt], lns.ones_row[:], lns.r4, start=True, stop=True),
        waits=[t_mr, t_rb])
    t_bc = ctx.op("scalar", lambda s: s.activation(
        out=lns.bc[:, Tt:2 * Tt], in_=lns.ps[:, 0:Tt], func=AF.Copy), waits=[t_b2])
    rb = lns.bc[:, 0:Tt]
    mrb = lns.bc[:, Tt:2 * Tt]

    out_toks = []
    prev_read = [t_s2]
    for c in range(CH):
        w = [t_rb, t_bc] + ([xb_toks[c]] if c in xb_toks else []) + waits + prev_read
        tt1 = ctx.op("vector", lambda v, ap=xb_aps[c]: v.tensor_mul(
            lns.t1[:], ap, rb), waits=w)
        if residual is None:
            tk = ctx.op("vector", lambda v, c=c: v.tensor_sub(
                out_chunks[c], lns.t1[:], mrb), waits=[tt1])
        else:
            xin_chunks, xin_toks = residual
            tt2 = ctx.op("vector", lambda v: v.tensor_sub(
                lns.t1[:], lns.t1[:], mrb), waits=[tt1])
            xw = [tt2] + ([xin_toks[c]] if xin_toks else [])
            tk = ctx.op("vector", lambda v, c=c: v.tensor_add(
                out_chunks[c], xin_chunks[c], lns.t1[:]), waits=xw)
        prev_read = [tk]
        out_toks.append(tk)
    lns.busy = out_toks[-1]
    return out_toks, out_toks[-1]


def emit_ln2(ctx, specs):
    St = []
    for (lns, src_chunks, out_chunks, src_tok, residual) in specs:
        waits = list(src_tok) + lns.ones_tok
        if lns.busy is not None:
            waits.append(lns.busy)
        St.append(dict(lns=lns, src=src_chunks, out=out_chunks, res=residual,
                       waits=waits, xb_toks={}, xb_aps=[]))
    for s in St:
        lns, Tt = s["lns"], s["lns"].Tt
        for c in range(CH):
            ap = s["src"][c]
            if ap.dtype == BF16:
                s["xb_aps"].append(ap)
            else:
                dst = lns.xb[:, c * Tt:(c + 1) * Tt]
                s["xb_toks"][c] = ctx.op(
                    "vector", lambda v, ap=ap, dst=dst: v.tensor_copy(out=dst, in_=ap),
                    waits=s["waits"])
                s["xb_aps"].append(dst)
    for s in St:
        lns, Tt = s["lns"], s["lns"].Tt
        ps_s = lns.ps[0:1, 0:Tt]
        for c in range(CH):
            w = s["waits"] + ([s["xb_toks"][c]] if c in s["xb_toks"] else [])
            tk = ctx.op("tensor", lambda t, ps_s=ps_s, lns=lns, c=c, ap=s["xb_aps"][c]: t.matmul(
                ps_s, lns.ones_col[:], ap,
                start=(c == 0), stop=(c == CH - 1)), waits=w)
        s["t_s1"] = ctx.op("scalar", lambda sc, lns=lns, ps_s=ps_s: sc.activation(
            out=lns.r0, in_=ps_s, func=AF.Copy), waits=[tk])
        s["ps_s"] = ps_s
    for s in St:
        lns, Tt, ps_s = s["lns"], s["lns"].Tt, s["ps_s"]
        sq_dep = [s["t_s1"]]
        for c in range(CH):
            w = s["waits"] + ([s["xb_toks"][c]] if c in s["xb_toks"] else [])
            sqt = ctx.op("scalar", lambda sc, lns=lns, ap=s["xb_aps"][c]: sc.activation(
                out=lns.sq[:], in_=ap, func=AF.Square), waits=w + sq_dep)
            tk = ctx.op("tensor", lambda t, lns=lns, ps_s=ps_s, c=c: t.matmul(
                ps_s, lns.ones_col[:], lns.sq[:],
                start=(c == 0), stop=(c == CH - 1)), waits=[sqt, s["t_s1"]])
            sq_dep = [tk]
        s["t_s2"] = tk
    inv_d = 1.0 / D
    for s in St:
        lns = s["lns"]
        s["t_msq"] = ctx.op("scalar", lambda sc, lns=lns: sc.activation(
            out=lns.r1, in_=lns.r0, func=AF.Square, scale=inv_d), waits=[s["t_s1"]])
    for s in St:
        lns, ps_s = s["lns"], s["ps_s"]
        s["t_var"] = ctx.op("vector", lambda v, lns=lns, ps_s=ps_s: v.scalar_tensor_tensor(
            out=lns.r2, in0=ps_s, scalar=inv_d, in1=lns.r1[:],
            op0=ALU.mult, op1=ALU.subtract), waits=[s["t_s2"], s["t_msq"]])
    eps_ap, eps_tok = ctx.constf(EPS)
    for s in St:
        lns = s["lns"]
        s["t_sd"] = ctx.op("scalar", lambda sc, lns=lns: sc.activation(
            out=lns.r1, in_=lns.r2, func=AF.Sqrt, bias=eps_ap),
            waits=[s["t_var"], s["t_msq"], eps_tok])
    for s in St:
        lns = s["lns"]
        s["t_r"] = ctx.op("vector", lambda v, lns=lns: v.reciprocal(
            out=lns.r3, in_=lns.r1), waits=[s["t_sd"]])
    for s in St:
        lns = s["lns"]
        s["t_mr"] = ctx.op("vector", lambda v, lns=lns: v.scalar_tensor_tensor(
            out=lns.r4, in0=lns.r0, scalar=inv_d, in1=lns.r3,
            op0=ALU.mult, op1=ALU.mult), waits=[s["t_r"]])
    for s in St:
        lns, Tt = s["lns"], s["lns"].Tt
        s["t_b1"] = ctx.op("tensor", lambda t, lns=lns, Tt=Tt: t.matmul(
            lns.ps[:, 0:Tt], lns.ones_row[:], lns.r3, start=True, stop=True),
            waits=s["waits"] + [s["t_r"], s["t_var"]])
    for s in St:
        lns, Tt = s["lns"], s["lns"].Tt
        s["t_rb"] = ctx.op("scalar", lambda sc, lns=lns, Tt=Tt: sc.activation(
            out=lns.bc[:, 0:Tt], in_=lns.ps[:, 0:Tt], func=AF.Copy), waits=[s["t_b1"]])
    for s in St:
        lns, Tt = s["lns"], s["lns"].Tt
        s["t_b2"] = ctx.op("tensor", lambda t, lns=lns, Tt=Tt: t.matmul(
            lns.ps[:, 0:Tt], lns.ones_row[:], lns.r4, start=True, stop=True),
            waits=[s["t_mr"], s["t_rb"]])
    for s in St:
        lns, Tt = s["lns"], s["lns"].Tt
        s["t_bc"] = ctx.op("scalar", lambda sc, lns=lns, Tt=Tt: sc.activation(
            out=lns.bc[:, Tt:2 * Tt], in_=lns.ps[:, 0:Tt], func=AF.Copy), waits=[s["t_b2"]])
    rets = []
    for s in St:
        lns, Tt = s["lns"], s["lns"].Tt
        rb = lns.bc[:, 0:Tt]
        mrb = lns.bc[:, Tt:2 * Tt]
        out_toks = []
        prev_read = [s["t_s2"]]
        for c in range(CH):
            w = [s["t_rb"], s["t_bc"]] + ([s["xb_toks"][c]] if c in s["xb_toks"] else []) \
                + s["waits"] + prev_read
            tt1 = ctx.op("vector", lambda v, lns=lns, ap=s["xb_aps"][c], rb=rb: v.tensor_mul(
                lns.t1[:], ap, rb), waits=w)
            if s["res"] is None:
                tk = ctx.op("vector", lambda v, lns=lns, oc=s["out"][c], mrb=mrb: v.tensor_sub(
                    oc, lns.t1[:], mrb), waits=[tt1])
            else:
                xin_chunks, xin_toks = s["res"]
                tt2 = ctx.op("vector", lambda v, lns=lns, mrb=mrb: v.tensor_sub(
                    lns.t1[:], lns.t1[:], mrb), waits=[tt1])
                xw = [tt2] + ([xin_toks[c]] if xin_toks else [])
                tk = ctx.op("vector", lambda v, lns=lns, oc=s["out"][c],
                            xc=xin_chunks[c]: v.tensor_add(
                    oc, xc, lns.t1[:]), waits=xw)
            prev_read = [tk]
            out_toks.append(tk)
        lns.busy = out_toks[-1]
        rets.append((out_toks, out_toks[-1]))
    return rets



def emit_l1ffn(ctx, xres, xin_toks, d, xout_ap):
    nc, es = ctx.nc, ctx.es
    Tt = 512
    OCG = IL1 // 128
    SLAB = 4
    NSL = OCG // SLAB
    KCD = IL1 // 128

    t_sp = ctx.dma("sync", lambda h: h.dma_start(out=d["x3_s"], in_=xres[:]),
                   waits=xin_toks, stream="x3s")
    m_sb = [es.enter_context(nc.sbuf_tensor("f1_m0", [128, OCG * Tt], BF16)),
            xres[:].bitcast(BF16)[:, 0:OCG * Tt]]
    ln_sb = es.enter_context(nc.sbuf_tensor("f1_ln", [128, CH * 2 * Tt], BF16))
    g_sb = es.enter_context(nc.sbuf_tensor("f1_g", [128, Tt], BF16))
    u_sb = es.enter_context(nc.sbuf_tensor("f1_u", [128, Tt], BF16))
    bo_sb = ln_sb
    m0f = m_sb[0][:].bitcast(F32)
    xrsf = xres[:].bitcast(F32)
    xr_f = m0f[:, 0:CH * Tt]
    wg8 = [es.enter_context(nc.sbuf_tensor(f"f1_wg{i}", [128, SLAB * CH * 128], FP8))
           for i in range(2)]
    wu8 = [es.enter_context(nc.sbuf_tensor(f"f1_wu{i}", [128, SLAB * CH * 128], FP8))
           for i in range(2)]
    wd8 = [es.enter_context(nc.sbuf_tensor(f"f1_wd{i}", [128, KCD * 128], FP8))
           for i in range(2)]
    scs = es.enter_context(nc.sbuf_tensor("f1_scs", [128, 2 * OCG + CH], F32))
    pg = [es.enter_context(nc.psum_tensor(f"f1_pg{i}", [128, Tt], F32)) for i in range(2)]
    pu = [es.enter_context(nc.psum_tensor(f"f1_pu{i}", [128, Tt], F32)) for i in range(2)]
    pd = [es.enter_context(nc.psum_tensor(f"f1_pd{i}", [128, Tt], F32)) for i in range(2)]
    lns = LNScratch(ctx, Tt, "f1", xb_alias=m_sb[0][:, 0:CH * Tt])
    lns_b = LNScratch(ctx, Tt, "f1b", xb_alias=m_sb[0][:, CH * Tt:2 * CH * Tt])

    t_sc = ctx.dma("sync", lambda h: h.dma_start(out=scs[:], in_=d["l1_scales"]), stream="sc")
    gsc = lambda oc: scs[:, oc:oc + 1]
    usc = lambda oc: scs[:, OCG + oc:OCG + oc + 1]
    dsc = lambda oc: scs[:, 2 * OCG + oc:2 * OCG + oc + 1]

    def load_gu(which, dst, sl, waits):
        src = d["l1_gateT" if which == "g" else "l1_upT"]
        ap = src[sl * SLAB:(sl + 1) * SLAB].rearrange("oc p kc o -> p oc kc o")
        dv = dst[:].rearrange("p (oc kc o) -> p oc kc o", oc=SLAB, kc=CH)
        return ctx.dma("gpsimd", lambda h, ap=ap, dv=dv: h.dma_start(out=dv, in_=ap),
                       waits=waits, stream="w" + which)

    def load_d(dst, oc, waits):
        ap = d["l1_downT"][oc]
        dv = dst[:].rearrange("p (kc o) -> p kc o", kc=KCD)
        return ctx.dma("gpsimd", lambda h, ap=ap, dv=dv: h.dma_start(out=dv, in_=ap),
                       waits=waits, stream="wd")

    specs = []
    for tt in range(2):
        t0 = tt * Tt
        srcx = [xres[:, c * TLOC + t0: c * TLOC + t0 + Tt] for c in range(CH)]
        lnc = [ln_sb[:, (c * 2 + tt) * Tt:(c * 2 + tt + 1) * Tt] for c in range(CH)]
        specs.append(((lns, lns_b)[tt], srcx, lnc, list(xin_toks), None))
    rr = emit_ln2(ctx, specs)
    ln_toks = {0: rr[0][0], 1: rr[1][0]}
    lnv = lambda kc, tt: ln_sb[:, (kc * 2 + tt) * Tt:(kc * 2 + tt + 1) * Tt]

    gu_busy = [[None, None], [None, None]]
    d_busy = [None, None]
    m_toks = {}
    gev = {}
    g_reader = [None]
    ld_g = load_gu("g", wg8[0], 0, [gu_busy[0][0], t_sp])
    ld_u = load_gu("u", wu8[0], 0, [gu_busy[1][0], t_sp])
    for sl in range(NSL):
        bi = sl % 2
        nld_g = nld_u = None
        if sl + 1 < NSL:
            nld_g = load_gu("g", wg8[1 - bi], sl + 1, [gu_busy[0][1 - bi]])
            nld_u = load_gu("u", wu8[1 - bi], sl + 1, [gu_busy[1][1 - bi]])
        for os_ in range(SLAB):
            oc = sl * SLAB + os_
            for tt in range(2):
                for kc in range(CH):
                    wt_g = wg8[bi][:, (os_ * CH + kc) * 128:(os_ * CH + kc + 1) * 128]
                    wt_u = wu8[bi][:, (os_ * CH + kc) * 128:(os_ * CH + kc + 1) * 128]
                    w = [ld_g, ln_toks[tt][kc]] + ([gev.get(("g", oc - 1, tt))] if kc == 0 else [])
                    tkg = ctx.op("tensor", lambda t, wt=wt_g, rh=lnv(kc, tt), tt=tt, kc=kc: t.matmul(
                        pg[tt][:], wt, rh, start=(kc == 0), stop=(kc == CH - 1)), waits=w)
                    w = [ld_u, ln_toks[tt][kc]] + ([gev.get(("u", oc - 1, tt))] if kc == 0 else [])
                    tku = ctx.op("tensor", lambda t, wt=wt_u, rh=lnv(kc, tt), tt=tt, kc=kc: t.matmul(
                        pu[tt][:], wt, rh, start=(kc == 0), stop=(kc == CH - 1)), waits=w)
                ev_g = ctx.op("scalar", lambda s, oc=oc, tt=tt: s.activation(
                    out=g_sb[:], in_=pg[tt][:], func=AF.Silu, scale=gsc(oc)),
                    waits=[tkg, t_sc, g_reader[0]])
                ev_u = ctx.op("scalar", lambda s, oc=oc, tt=tt: s.activation(
                    out=u_sb[:], in_=pu[tt][:], func=AF.Identity, scale=usc(oc)),
                    waits=[tku, t_sc, g_reader[0]])
                gev[("g", oc, tt)] = ev_g
                gev[("u", oc, tt)] = ev_u
                mdst = m_sb[tt][:, oc * Tt:(oc + 1) * Tt]
                mw = [ev_g, ev_u]
                if tt == 1:
                    mw.append(t_sp)
                m_toks[(oc, tt)] = ctx.op("vector", lambda v, mdst=mdst: v.tensor_mul(
                    mdst, g_sb[:], u_sb[:]), waits=mw)
                g_reader[0] = m_toks[(oc, tt)]
        gu_busy[0][bi] = tkg
        gu_busy[1][bi] = tku
        if nld_g is not None:
            ld_g, ld_u = nld_g, nld_u

    down_ev = {}
    ld_d = load_d(wd8[0], 0, [d_busy[0]])
    for oc in range(CH):
        bi = oc % 2
        nld = None
        if oc + 1 < CH:
            nld = load_d(wd8[1 - bi], oc + 1, [d_busy[1 - bi]])
        for tt in range(2):
            for kc in range(KCD):
                wt = wd8[bi][:, kc * 128:(kc + 1) * 128]
                rh = m_sb[tt][:, kc * Tt:(kc + 1) * Tt]
                w = [ld_d, m_toks[(kc, tt)]] + ([down_ev.get((oc - 1, tt))] if kc == 0 else [])
                tkd = ctx.op("tensor", lambda t, wt=wt, rh=rh, kc=kc, tt=tt: t.matmul(
                    pd[tt][:], wt, rh, start=(kc == 0), stop=(kc == KCD - 1)), waits=w)
            d_busy[bi] = tkd
            bodst = bo_sb[:, (oc * 2 + tt) * Tt:(oc * 2 + tt + 1) * Tt]
            down_ev[(oc, tt)] = ctx.op("scalar", lambda s, oc=oc, bodst=bodst, tt=tt: s.activation(
                out=bodst, in_=pd[tt][:], func=AF.Identity, scale=dsc(oc)),
                waits=[tkd, t_sc])
        if nld is not None:
            ld_d = nld
    down_all = [d_busy[0], d_busy[1]]

    x3v = d["x3_s"].rearrange("p (c t) -> p c t", c=CH)
    specs = []
    for tt in range(2):
        if tt == 0:
            xrs = [xr_f.rearrange("p (c t) -> p c t", c=CH)[:, c, :] for c in range(CH)]
            t_xr = ctx.dma("sync", lambda h: h.dma_start(
                out=xr_f.rearrange("p (c t) -> p c t", c=CH),
                in_=x3v[:, :, 0:Tt]),
                waits=down_all + [t_sp], stream="xrr0")
            xr_toks = [t_xr]
        else:
            xrs = ([m0f[:, CH * Tt + c * Tt: CH * Tt + (c + 1) * Tt] for c in range(8)]
                   + [xrsf[:, 12288 + (c - 8) * Tt: 12288 + (c - 7) * Tt] for c in range(8, CH)])
            t_xa = ctx.dma("sync", lambda h: h.dma_start(
                out=m0f[:, CH * Tt:CH * Tt + 8 * Tt].rearrange("p (c t) -> p c t", c=8),
                in_=x3v[:, 0:8, Tt:2 * Tt]),
                waits=down_all + [t_sp], stream="xrr1")
            t_xb = ctx.dma("sync", lambda h: h.dma_start(
                out=xrsf[:, 12288:12288 + 8 * Tt].rearrange("p (c t) -> p c t", c=8),
                in_=x3v[:, 8:CH, Tt:2 * Tt]),
                waits=down_all + [t_sp], stream="xrr2")
            xr_toks = [t_xa, t_xb]
        bo = [bo_sb[:, (c * 2 + tt) * Tt:(c * 2 + tt + 1) * Tt] for c in range(CH)]
        specs.append(((lns, lns_b)[tt], bo, xrs,
                      [down_ev[(c, tt)] for c in range(CH)] + xr_toks,
                      (xrs, None)))
    rr = emit_ln2(ctx, specs)
    st0 = ctx.dma("sync", lambda h: h.dma_start(
        out=xout_ap[:, :, 0:Tt],
        in_=xr_f.rearrange("p (c t) -> p c t", c=CH)),
        waits=rr[0][0], stream="xo4")
    st1 = ctx.dma("sync", lambda h: h.dma_start(
        out=xout_ap[:, 0:8, Tt:2 * Tt],
        in_=m0f[:, CH * Tt:CH * Tt + 8 * Tt].rearrange("p (c t) -> p c t", c=8)),
        waits=rr[1][0], stream="xo5")
    st2 = ctx.dma("sync", lambda h: h.dma_start(
        out=xout_ap[:, 8:CH, Tt:2 * Tt],
        in_=xrsf[:, 12288:12288 + 8 * Tt].rearrange("p (c t) -> p c t", c=8)),
        waits=rr[1][0], stream="xo6")
    return [st0, st1, st2]


def build_prog_l1ffn():
    nc = bass.Bass("TRN2", target_bir_lowering=False)
    d = {}
    d["xin"] = nc.dram_tensor("xin", [128, CH, TLOC], F32, kind="ExternalInput").ap()
    d["x3_s"] = nc.dram_tensor("x3_s", [128, CH * TLOC], F32, kind="Internal").ap()
    d["l1_gateT"] = nc.dram_tensor("l1_gateT", [IL1 // 128, 128, CH, 128], F32, kind="ExternalInput").ap()
    d["l1_upT"] = nc.dram_tensor("l1_upT", [IL1 // 128, 128, CH, 128], F32, kind="ExternalInput").ap()
    d["l1_downT"] = nc.dram_tensor("l1_downT", [CH, 128, IL1 // 128, 128], F32, kind="ExternalInput").ap()
    d["l1_scales"] = nc.dram_tensor("l1_scales", [128, 2 * (IL1 // 128) + CH], F32, kind="ExternalInput").ap()
    xout = nc.dram_tensor("xout", [128, CH, TLOC], F32, kind="ExternalOutput").ap()

    ctx = Ctx(nc)
    with ctx.es:
        xin_sb = ctx.es.enter_context(nc.sbuf_tensor("xin_sb", [128, CH * TLOC], F32))
        blk = ctx.es.enter_context(nc.Block())
        t_x = ctx.dma("sync", lambda h: h.dma_start(
            out=xin_sb[:], in_=d["xin"].rearrange("p c t -> p (c t)")), stream="x")
        out_toks = emit_l1ffn(ctx, xin_sb, [t_x], d, xout)
        ctx.wait("sync", *out_toks)
        ctx.finish(blk)
    return nc



def emit_l0ffn(ctx, xres, xin_toks, d):
    nc, es = ctx.nc, ctx.es
    Tt = 512
    OCG = ISH // 128
    OCR = IRT // 128
    KCD = OCG

    t_sp = ctx.dma("sync", lambda h: h.dma_start(out=d["x1_s"], in_=xres[:]),
                   waits=xin_toks, stream="x1s")
    m_sb = [es.enter_context(nc.sbuf_tensor("f0_m0", [128, OCG * Tt], BF16)),
            xres[:].bitcast(BF16)[:, 0:OCG * Tt]]
    ln_sb = es.enter_context(nc.sbuf_tensor("f0_ln", [128, CH * 2 * Tt], BF16))
    bo_sb = ln_sb
    mr_sb = es.enter_context(nc.sbuf_tensor("f0_mr", [128, 2 * OCR * Tt], BF16))
    g_sb = es.enter_context(nc.sbuf_tensor("f0_g", [128, Tt], BF16))
    lg_alias = g_sb
    crow = g_sb[0:1, :]
    wg = [es.enter_context(nc.sbuf_tensor(f"f0_wg{i}", [128, CH * 128], BF16))
          for i in range(2)]
    wu = [es.enter_context(nc.sbuf_tensor(f"f0_wu{i}", [128, CH * 128], BF16))
          for i in range(2)]
    wd = wg
    wrd = es.enter_context(nc.sbuf_tensor("f0_wrd0", [128, OCR * 128], BF16))
    wro = es.enter_context(nc.sbuf_tensor("f0_wro", [128, CH * 8], BF16))
    idn = es.enter_context(nc.sbuf_tensor("f0_idn", [128, 128], BF16))
    ones_bf = es.enter_context(nc.sbuf_tensor("f0_ones", [1, 128], BF16))
    lgt = es.enter_context(nc.sbuf_tensor("f0_lgt", [128, 32], BF16))
    cm1 = es.enter_context(nc.sbuf_tensor("f0_cm1", [128, 4], F32))
    cm2 = es.enter_context(nc.sbuf_tensor("f0_cm2", [128, 4], F32))
    ceq = es.enter_context(nc.sbuf_tensor("f0_ceq", [128, 32], BF16))
    cadj = es.enter_context(nc.sbuf_tensor("f0_cadj", [128, 32], F32))
    ce = es.enter_context(nc.sbuf_tensor("f0_ce", [128, 4], F32))
    cs = es.enter_context(nc.sbuf_tensor("f0_cs", [128, 4], F32))
    cr = es.enter_context(nc.sbuf_tensor("f0_cr", [128, 4], F32))
    ct = es.enter_context(nc.sbuf_tensor("f0_ct", [128, 4], F32))
    ccomb = es.enter_context(nc.sbuf_tensor("f0_ccomb", [128, 4], BF16))
    combB = es.enter_context(nc.sbuf_tensor("f0_combB", [128, 2 * Tt], BF16))

    pg = [es.enter_context(nc.psum_tensor(f"f0_pg{i}", [128, Tt], F32)) for i in range(2)]
    pu = [es.enter_context(nc.psum_tensor(f"f0_pu{i}", [128, Tt], F32)) for i in range(2)]
    pd = [es.enter_context(nc.psum_tensor(f"f0_pd{i}", [128, Tt], F32)) for i in range(2)]
    prt = es.enter_context(nc.psum_tensor("f0_prt", [128, Tt], F32))
    plg = prt[0:8, :]
    ptr = prt[:, 0:16].bitcast(BF16)
    ptc = prt[0:1, 128:384].bitcast(BF16)
    pcb = prt
    lns = LNScratch(ctx, Tt, "f0", xb_alias=m_sb[0][:, 0:CH * Tt])
    xr_f = m_sb[0][:, 0:2 * CH * Tt].bitcast(F32)[:, 0:CH * Tt]

    t_id = ctx.dma("gpsimd", lambda h: h.dma_start(out=idn[:], in_=d["ident"]), stream="id")
    t_on = ctx.op("vector", lambda v: v.memset(ones_bf[:], 1.0))
    t_wro = ctx.dma("gpsimd", lambda h: h.dma_start(
        out=wro[:].rearrange("p (kc o) -> p kc o", kc=CH),
        in_=d["routerT"].rearrange("(kc p) o -> p kc o", p=128)), stream="wro")

    def load_gu(which, dst, oc, waits, routed=False):
        if routed:
            src = d["r_guT"][0 if which == "g" else 1]
        else:
            src = d["sh_gateT" if which == "g" else "sh_upT"]
        ap = src[oc].rearrange("p kc o -> p kc o")
        dv = dst[:].rearrange("p (kc o) -> p kc o", kc=CH)
        return ctx.dma("gpsimd", lambda h, ap=ap, dv=dv: h.dma_start(out=dv, in_=ap),
                       waits=waits, stream="w" + which)

    def load_d(dst, oc, half, waits):
        ap = d["sh_dT"][oc, :, half * (KCD // 4):(half + 1) * (KCD // 4)]
        dv = dst[:].rearrange("p (kc o) -> p kc o", kc=KCD // 4)
        return ctx.dma("gpsimd", lambda h, ap=ap, dv=dv: h.dma_start(out=dv, in_=ap),
                       waits=waits, stream="wd")

    ln_toks = {}
    for tt in range(2):
        t0 = tt * Tt
        srcx = [xres[:, c * TLOC + t0: c * TLOC + t0 + Tt] for c in range(CH)]
        lnc = [ln_sb[:, (c * 2 + tt) * Tt:(c * 2 + tt + 1) * Tt] for c in range(CH)]
        toks, _ = emit_ln(ctx, lns, srcx, lnc, list(xin_toks))
        ln_toks[tt] = toks
    lnv = lambda kc, tt: ln_sb[:, (kc * 2 + tt) * Tt:(kc * 2 + tt + 1) * Tt]
    ln_done = ln_toks[1][CH - 1]

    ev_cB = {}
    cw_prev = []
    for tt in range(2):
        for kc in range(CH):
            tk_lg = ctx.op("tensor", lambda t, kc=kc, tt=tt: t.matmul(
                plg, wro[:, kc * 8:(kc + 1) * 8], lnv(kc, tt),
                start=(kc == 0), stop=(kc == CH - 1)),
                waits=[t_wro, ln_toks[tt][kc]] + cw_prev)
        ev_lg = ctx.op("scalar", lambda s: s.activation(
            out=lg_alias[0:8, :], in_=plg, func=AF.Copy), waits=[tk_lg])
        tr_toks = []
        for q in range(4):
            tk = ctx.op("tensor", lambda t, q=q: t.transpose(
                ptr[:, q * 8:(q + 1) * 8], lg_alias[0:8, q * 128:(q + 1) * 128],
                idn[0:8, 0:8]), waits=[ev_lg, t_id])
            tr_toks.append(tk)
        ev_lgt = ctx.op("scalar", lambda s: s.activation(
            out=lgt[:], in_=ptr, func=AF.Copy), waits=tr_toks)
        v_lgt = lgt[:].rearrange("p (q n) -> p q n", n=8)
        t_m1 = ctx.op("vector", lambda v: v.reduce_max(
            cm1[:].rearrange("p (q o) -> p q o", o=1), v_lgt, axis=AX.X), waits=[ev_lgt])
        m1b = cm1[:].rearrange("p (q o) -> p q o", o=1).broadcast_to([128, 4, 8])
        t_eq = ctx.op("vector", lambda v: v.tensor_tensor(
            out=ceq[:].rearrange("p (q n) -> p q n", n=8), in0=v_lgt, in1=m1b,
            op=ALU.is_ge), waits=[t_m1])
        t_adj = ctx.op("vector", lambda v: v.scalar_tensor_tensor(
            out=cadj[:], in0=ceq[:], scalar=-1e30, in1=lgt[:],
            op0=ALU.mult, op1=ALU.add), waits=[t_eq])
        t_m2 = ctx.op("vector", lambda v: v.reduce_max(
            cm2[:].rearrange("p (q o) -> p q o", o=1),
            cadj[:].rearrange("p (q n) -> p q n", n=8), axis=AX.X), waits=[t_adj])
        t_e = ctx.op("vector", lambda v: v.tensor_sub(ce[:], cm2[:], cm1[:]), waits=[t_m2])
        t_ee = ctx.op("scalar", lambda s: s.activation(
            out=ce[:], in_=ce[:], func=AF.Exp), waits=[t_e])
        t_s = ctx.op("vector", lambda v: v.tensor_scalar_add(cs[:], ce[:], 1.0), waits=[t_ee])
        t_r = ctx.op("vector", lambda v: v.reciprocal(cr[:], cs[:]), waits=[t_s])
        t_t = ctx.op("vector", lambda v: v.tensor_mul(ct[:], ce[:], cr[:]), waits=[t_r])
        t_cb = ctx.op("vector", lambda v: v.tensor_add(ccomb[:], ct[:], cr[:]), waits=[t_t])
        trc_toks = []
        for q in range(4):
            tk = ctx.op("tensor", lambda t, q=q: t.transpose(
                ptc[0:1, q * 128:(q + 1) * 128], ccomb[:, q:q + 1], idn[:]),
                waits=[t_cb, t_id, ev_lgt])
            trc_toks.append(tk)
        ev_cr = ctx.op("scalar", lambda s: s.activation(
            out=crow[:], in_=ptc[0:1, 0:Tt], func=AF.Copy), waits=trc_toks)
        t_cbm = ctx.op("tensor", lambda t: t.matmul(
            pcb[:, 0:Tt], ones_bf[:], crow[:], start=True, stop=True),
            waits=[ev_cr, t_on])
        ev_cB[tt] = ctx.op("scalar", lambda s, tt=tt: s.activation(
            out=combB[:, tt * Tt:(tt + 1) * Tt], in_=pcb[:, 0:Tt], func=AF.Copy),
            waits=[t_cbm])
        cw_prev = [ev_cB[tt]]

    gu_busy = {"g": [None, None], "u": [None, None]}
    lastr = {}
    m_toks = {}
    mr_toks = {}

    def gu_loop(n_oc, routed):
        ld_g = load_gu("g", wg[0], 0, [gu_busy["g"][0], t_sp], routed=routed)
        ld_u = load_gu("u", wu[0], 0, [gu_busy["u"][0], t_sp], routed=routed)
        for oc in range(n_oc):
            bi = oc % 2
            nld_g = nld_u = None
            if oc + 1 < n_oc:
                nld_g = load_gu("g", wg[1 - bi], oc + 1, [gu_busy["g"][1 - bi]], routed=routed)
                nld_u = load_gu("u", wu[1 - bi], oc + 1, [gu_busy["u"][1 - bi]], routed=routed)
            for tt in range(2):
                for kc in range(CH):
                    wt_g = wg[bi][:, kc * 128:(kc + 1) * 128]
                    wt_u = wu[bi][:, kc * 128:(kc + 1) * 128]
                    w = [ld_g, ln_toks[tt][kc]] + ([lastr.get(("g", tt))] if kc == 0 else [])
                    tkg = ctx.op("tensor", lambda t, wt=wt_g, rh=lnv(kc, tt), tt=tt, kc=kc: t.matmul(
                        pg[tt][:], wt, rh, start=(kc == 0), stop=(kc == CH - 1)), waits=w)
                    w = [ld_u, ln_toks[tt][kc]] + ([lastr.get(("u", tt))] if kc == 0 else [])
                    tku = ctx.op("tensor", lambda t, wt=wt_u, rh=lnv(kc, tt), tt=tt, kc=kc: t.matmul(
                        pu[tt][:], wt, rh, start=(kc == 0), stop=(kc == CH - 1)), waits=w)
                mdst = (mr_sb if routed else m_sb[tt])[
                    :, ((oc * 2 + tt) if routed else oc) * Tt:
                       (((oc * 2 + tt) if routed else oc) + 1) * Tt]
                mw = [tkg, ln_done]
                if not routed and tt == 1:
                    mw.append(t_sp)
                ev_g = ctx.op("scalar", lambda s, tt=tt, mdst=mdst: s.activation(
                    out=mdst, in_=pg[tt][:], func=AF.Silu), waits=mw)
                lastr[("g", tt)] = ev_g
                t_mu = ctx.op("vector", lambda v, tt=tt, mdst=mdst: v.tensor_tensor(
                    out=mdst, in0=pu[tt][:], in1=mdst, op=ALU.mult),
                    waits=[tku, ev_g])
                lastr[("u", tt)] = t_mu
                if routed:
                    mr_toks[(oc, tt)] = ctx.op("vector", lambda v, tt=tt, mdst=mdst: v.tensor_mul(
                        mdst, mdst, combB[:, tt * Tt:(tt + 1) * Tt]),
                        waits=[t_mu, ev_cB[tt]])
                else:
                    m_toks[(oc, tt)] = t_mu
            gu_busy["g"][bi] = tkg
            gu_busy["u"][bi] = tku
            if nld_g is not None:
                ld_g, ld_u = nld_g, nld_u

    gu_loop(OCR, True)
    gu_loop(OCG, False)

    down_ev = {}
    d_busy = [gu_busy["g"][0], gu_busy["g"][1]]
    rd_busy = [None]
    ld_d = load_d(wd[0], 0, 0, [d_busy[0]])
    for oc in range(CH):
        ld_rd = ctx.dma("gpsimd", lambda h, oc=oc: h.dma_start(
            out=wrd[:].rearrange("p (kc o) -> p kc o", kc=OCR),
            in_=d["r_dT"][oc]), waits=[rd_busy[0]], stream="wrd")
        for half in range(4):
            bi = (4 * oc + half) % 2
            nld = None
            if 4 * oc + half + 1 < 4 * CH:
                nxt = 4 * oc + half + 1
                nld = load_d(wd[1 - bi], nxt // 4, nxt % 4, [d_busy[1 - bi]])
            for tt in range(2):
                for kc2 in range(KCD // 4):
                    kc = half * (KCD // 4) + kc2
                    wt = wd[bi][:, kc2 * 128:(kc2 + 1) * 128]
                    rh = m_sb[tt][:, kc * Tt:(kc + 1) * Tt]
                    w = [ld_d, m_toks[(kc, tt)]]
                    if kc2 == 0 and half == 0:
                        w.append(down_ev.get((oc - 1, tt)))
                    tkd = ctx.op("tensor", lambda t, wt=wt, rh=rh,
                                 st=(kc2 == 0 and half == 0), tt=tt: t.matmul(
                        pd[tt][:], wt, rh, start=st, stop=False), waits=w)
                d_busy[bi] = tkd
            if nld is not None:
                ld_d = nld
        for tt in range(2):
            for kc in range(OCR):
                wt = wrd[:, kc * 128:(kc + 1) * 128]
                rh = mr_sb[:, (kc * 2 + tt) * Tt:(kc * 2 + tt + 1) * Tt]
                tkd2 = ctx.op("tensor", lambda t, wt=wt, rh=rh, kc=kc, tt=tt: t.matmul(
                    pd[tt][:], wt, rh, start=False, stop=(kc == OCR - 1)),
                    waits=[ld_rd, mr_toks[(kc, tt)]])
            rd_busy[0] = tkd2
            bodst = bo_sb[:, (oc * 2 + tt) * Tt:(oc * 2 + tt + 1) * Tt]
            down_ev[(oc, tt)] = ctx.op("scalar", lambda s, bodst=bodst, tt=tt: s.activation(
                out=bodst, in_=pd[tt][:], func=AF.Copy), waits=[tkd2])
    down_all = [down_ev[(CH - 1, 0)], down_ev[(CH - 1, 1)], rd_busy[0]]

    out_toks_all = []
    prev_st = []
    for tt in range(2):
        t_xr = ctx.dma("sync", lambda h, tt=tt: h.dma_start(
            out=xr_f.rearrange("p (c t) -> p c t", c=CH),
            in_=d["x1_s"].rearrange("p (c t) -> p c t", c=CH)[:, :, tt * Tt:(tt + 1) * Tt]),
            waits=down_all + [t_sp] + prev_st, stream="xrr0")
        bo = [bo_sb[:, (c * 2 + tt) * Tt:(c * 2 + tt + 1) * Tt] for c in range(CH)]
        xrs = [xr_f.rearrange("p (c t) -> p c t", c=CH)[:, c, :] for c in range(CH)]
        xdst = [xres[:, c * TLOC + tt * Tt: c * TLOC + tt * Tt + Tt] for c in range(CH)]
        toks, _ = emit_ln(ctx, lns, bo, xdst,
                          [down_ev[(c, tt)] for c in range(CH)] + [t_xr],
                          residual=(xrs, None))
        prev_st = toks
        out_toks_all.extend(toks)
    return out_toks_all


def build_prog_l0ffn():
    nc = bass.Bass("TRN2", target_bir_lowering=False)
    d = {}
    d["xin"] = nc.dram_tensor("xin", [128, CH, TLOC], F32, kind="ExternalInput").ap()
    d["x1_s"] = nc.dram_tensor("x1_s", [128, CH * TLOC], F32, kind="Internal").ap()
    d["sh_gateT"] = nc.dram_tensor("sh_gateT", [ISH // 128, 128, CH, 128], F32, kind="ExternalInput").ap()
    d["sh_upT"] = nc.dram_tensor("sh_upT", [ISH // 128, 128, CH, 128], F32, kind="ExternalInput").ap()
    d["sh_dT"] = nc.dram_tensor("sh_dT", [CH, 128, ISH // 128, 128], F32, kind="ExternalInput").ap()
    d["r_guT"] = nc.dram_tensor("r_guT", [2, IRT // 128, 128, CH, 128], F32, kind="ExternalInput").ap()
    d["r_dT"] = nc.dram_tensor("r_dT", [CH, 128, IRT // 128, 128], F32, kind="ExternalInput").ap()
    d["routerT"] = nc.dram_tensor("routerT", [D, 8], F32, kind="ExternalInput").ap()
    d["ident"] = nc.dram_tensor("ident", [128, 128], F32, kind="ExternalInput").ap()
    xout = nc.dram_tensor("xout", [128, CH, TLOC], F32, kind="ExternalOutput").ap()

    ctx = Ctx(nc)
    with ctx.es:
        xin_sb = ctx.es.enter_context(nc.sbuf_tensor("xin_sb", [128, CH * TLOC], F32))
        blk = ctx.es.enter_context(nc.Block())
        t_x = ctx.dma("sync", lambda h: h.dma_start(
            out=xin_sb[:], in_=d["xin"].rearrange("p c t -> p (c t)")), stream="x")
        out_toks = emit_l0ffn(ctx, xin_sb, [t_x], d)
        st = ctx.dma("sync", lambda h: h.dma_start(
            out=xout.rearrange("p c t -> p (c t)"), in_=xin_sb[:]),
            waits=out_toks, stream="xo")
        ctx.wait("sync", st)
        ctx.finish(blk)
    return nc



def emit_l0attn(ctx, d, xres, xres_tok):
    nc, es = ctx.nc, ctx.es
    TH = TLOC + WIN
    NW = TLOC // WIN
    NTC = TH // 128

    ln1 = es.enter_context(nc.sbuf_tensor("at_ln1", [128, CH * TH], BF16))
    qkstg = [es.enter_context(nc.sbuf_tensor(f"at_qks{i}", [128, 512], BF16))
             for i in range(2)]
    vstg = es.enter_context(nc.sbuf_tensor("at_vstg", [128, D], BF16))
    vt_sb = es.enter_context(nc.sbuf_tensor("at_vt", [128, 512], BF16))
    qwin = es.enter_context(nc.sbuf_tensor("at_qwin", [128, CH * WIN], BF16))
    kwin = es.enter_context(nc.sbuf_tensor("at_kwin", [128, CH * 512], BF16))
    vwin = es.enter_context(nc.sbuf_tensor("at_vwin", [128, 4 * D], BF16))
    wbuf = [es.enter_context(nc.sbuf_tensor(f"at_w{i}", [128, CH * 128], BF16))
            for i in range(2)]
    cs_sb = es.enter_context(nc.sbuf_tensor("at_cs", [64, 2 * TH], BF16))
    msk = es.enter_context(nc.sbuf_tensor("at_msk", [128, 4 * 512], BF16))
    ra = [es.enter_context(nc.sbuf_tensor(f"at_ra{i}", [64, 512], BF16))
          for i in range(2)]
    rb2 = [es.enter_context(nc.sbuf_tensor(f"at_rb{i}", [64, 512], BF16))
           for i in range(2)]
    p_sbs = [es.enter_context(nc.sbuf_tensor(f"at_p{i}", [128, 512], BF16))
             for i in range(3)]
    ph2 = [es.enter_context(nc.sbuf_tensor(f"at_ph2{i}", [64, 512], BF16))
           for i in range(2)]
    rp_sb = [es.enter_context(nc.sbuf_tensor(f"at_rp{i}", [64, 512], BF16))
             for i in range(2)]
    pt_sbs = [es.enter_context(nc.sbuf_tensor(f"at_pt{i}", [128, 512], BF16))
              for i in range(3)]
    cols = es.enter_context(nc.sbuf_tensor("at_cols", [128, 12], F32))
    idn = es.enter_context(nc.sbuf_tensor("at_idn", [128, 128], BF16))
    attn_sb = ln1[:, 0:2 * CH * WIN]
    bo_view = ln1[:, 2 * CH * WIN:4 * CH * WIN]

    PA = es.enter_context(nc.psum_tensor("at_PA", [128, 2048], F32))
    PB0 = es.enter_context(nc.psum_tensor("at_PB0", [128, 512], F32))
    PB1 = es.enter_context(nc.psum_tensor("at_PB1", [128, 512], F32))
    PD = es.enter_context(nc.psum_tensor("at_PD", [128, 512], F32))
    lns = LNScratch(ctx, WIN, "at")

    t_id = ctx.dma("gpsimd", lambda h: h.dma_start(out=idn[:], in_=d["ident"]), stream="id")
    t_cs = ctx.dma("gpsimd", lambda h: h.dma_start(out=cs_sb[:], in_=d["cossin"]), stream="cs")
    t_mk = ctx.dma("gpsimd", lambda h: h.dma_start(
        out=msk[:].rearrange("p (a t) -> p a t", a=4),
        in_=d["masks"].rearrange("a p t -> p a t")), stream="mk")

    ln_state = [None]
    ln_toks_all = [None] * (CH * 5)

    def wln(i):
        if i == 0:
            src_ap = d["halo"]
        else:
            src_ap = d["xin"][:, :, (i - 1) * WIN:i * WIN]
        t_x = ctx.dma("gpsimd", lambda h, src_ap=src_ap: h.dma_start(
            out=lns.xb[:].rearrange("p (c t) -> p c t", c=CH),
            in_=src_ap), waits=[ln_state[0]], stream="xw")
        src = [lns.xb[:, c * WIN:(c + 1) * WIN] for c in range(CH)]
        outc = [ln1[:, c * TH + i * WIN: c * TH + (i + 1) * WIN] for c in range(CH)]
        toks, ln_state[0] = emit_ln(ctx, lns, src, outc, [t_x])
        for c in range(CH):
            ln_toks_all[c * 5 + i] = toks[c]

    def load_w(name, oc, buf, waits):
        ap = d[name][oc]
        dv = buf[:].rearrange("p (kc o) -> p kc o", kc=CH)
        return ctx.dma("gpsimd", lambda h, ap=ap, dv=dv: h.dma_start(out=dv, in_=ap),
                       waits=waits, stream="wp")

    def rope_evac(proj, h_, ps, t0, F, mm_tok, war, sbi, sb_war):
        dst = qkstg[sbi]
        off = 0
        if proj == "v":
            ev = ctx.op("scalar", lambda s: s.activation(
                out=vstg[:, 0:F], in_=ps[:, 0:F], func=AF.Copy),
                waits=[mm_tok, sb_war, vt_war[0]])
            nq = F // 128
            vt_ps = PA[:, 0:256].bitcast(BF16)
            tts = []
            for q in range(nq):
                tk = ctx.op("tensor", lambda t, q=q: t.transpose(
                    vt_ps[:, q * 128:(q + 1) * 128],
                    vstg[:, q * 128:(q + 1) * 128], idn[:]),
                    waits=[ev, t_id, vt_war[1]])
                tts.append(tk)
            ev2 = ctx.op("scalar", lambda s, nq=nq: s.activation(
                out=vt_sb[:, 0:nq * 128], in_=vt_ps[:, 0:nq * 128], func=AF.Copy),
                waits=tts)
            vt_war[1] = ev2
            tc0 = t0 // 128
            dstv = d["v_s"][:].rearrange("p (tc o) -> p tc o", o=D)[
                :, tc0:tc0 + nq, h_ * 128:(h_ + 1) * 128]
            t_st = ctx.dma("scalar", lambda h, dstv=dstv, nq=nq: h.dma_start(
                out=dstv,
                in_=vt_sb[:, 0:nq * 128].rearrange("p (q o) -> p q o", o=128)),
                waits=[ev2], stream="qkst" + str(sbi))
            vt_war[0] = t_st
            return t_st, ev
        scale = float(DH) ** -0.5 if proj == "q" else 1.0
        rp_, ph_, ra_, rb_ = rp_sb[sbi], ph2[sbi], ra[sbi], rb2[sbi]
        ev_e = ctx.op("scalar", lambda s: s.activation(
            out=rp_[:, 0:F], in_=ps[0:64, 0:F], func=AF.Copy, scale=scale),
            waits=[mm_tok] + war[0:1])
        ev = ctx.op("scalar", lambda s: s.activation(
            out=ph_[:, 0:F], in_=ps[64:128, 0:F], func=AF.Copy, scale=scale),
            waits=[mm_tok] + war[0:1])
        ctok = t0
        ce = cs_sb[:, ctok:ctok + F]
        se = cs_sb[:, TH + ctok:TH + ctok + F]
        t_a = ctx.op("vector", lambda v: v.tensor_mul(ra_[:, 0:F], rp_[:, 0:F], ce),
                     waits=[ev_e, t_cs] + war[1:2])
        t_b = ctx.op("vector", lambda v: v.tensor_mul(rb_[:, 0:F], ph_[:, 0:F], se),
                     waits=[ev, t_cs] + war[2:3])
        o_r = ctx.op("vector", lambda v: v.tensor_sub(
            dst[0:64, off:off + F], ra_[:, 0:F], rb_[:, 0:F]), waits=[t_a, t_b])
        t_a2 = ctx.op("vector", lambda v: v.tensor_mul(ra_[:, 0:F], rp_[:, 0:F], se),
                      waits=[o_r])
        t_b2 = ctx.op("vector", lambda v: v.tensor_mul(rb_[:, 0:F], ph_[:, 0:F], ce),
                      waits=[o_r])
        o_i = ctx.op("vector", lambda v: v.tensor_add(
            dst[64:128, off:off + F], ra_[:, 0:F], rb_[:, 0:F]), waits=[t_a2, t_b2])
        if proj == "q":
            w0 = (t0 - WIN) // WIN
            dap = d["q_s"][:].rearrange("p (w c t) -> p w c t", w=NW, c=CH)[
                :, w0:w0 + F // WIN, h_, :]
        else:
            tc0 = t0 // WIN
            dap = d["k_s"][:].rearrange("p (tc c t) -> p tc c t", tc=TH // WIN, c=CH)[
                :, tc0:tc0 + F // WIN, h_, :]
        t_st = ctx.dma("scalar", lambda h, dap=dap, F=F: h.dma_start(
            out=dap, in_=dst[:, 0:F].rearrange("p (a t) -> p a t", t=WIN)),
            waits=[o_r, o_i], stream="qkst" + str(sbi))
        return t_st, ev

    wp_busy = [None, None]
    pq_busy = [None, None, None]
    qk_done = []
    vt_war = [None, None]
    ld = load_w("wqT", 0, wbuf[0], [wp_busy[0]])
    wln(1)
    wln(2)
    kvtiles = [(0, 512), (512, 512), (1024, 256)]
    seq = ([("q", "wqT", oc, [(WIN, 512)]) for oc in range(CH)]
           + [("q", "wqT", oc, [(WIN + 512, 512)]) for oc in range(CH)]
           + [("k", "wkT", oc, kvtiles) for oc in range(CH)]
           + [("v", "wvT", oc, kvtiles) for oc in range(CH)])
    hooks = {4: 3, 12: 4, 20: 0}
    for idx, (proj, wname, oc, ftiles) in enumerate(seq):
        if idx in hooks:
            wln(hooks[idx])
        bi = idx % 2
        nld = None
        if idx + 1 < len(seq):
            nproj, nw, noc, _nt = seq[idx + 1]
            nld = load_w(nw, noc, wbuf[1 - bi], [wp_busy[1 - bi]])
        for fi, (t0, F) in enumerate(ftiles):
            pbi = len(qk_done) % 3
            ps = (PB0, PB1, PD)[pbi]
            ivals = list(range(t0 // WIN, (t0 + F - 1) // WIN + 1))
            for kc in range(CH):
                rh = ln1[:, kc * TH + t0: kc * TH + t0 + F]
                w = [ld] + [ln_toks_all[kc * 5 + i] for i in ivals] + (
                    [pq_busy[pbi]] if kc == 0 else [])
                mm = ctx.op("tensor", lambda t, wt=wbuf[bi][:, kc * 128:(kc + 1) * 128],
                            rh=rh, kc=kc, ps=ps, F=F: t.matmul(
                    ps[:, 0:F], wt, rh, start=(kc == 0), stop=(kc == CH - 1)), waits=w)
            sbi = len(qk_done) % 2
            sb_war = qk_done[-2][0] if len(qk_done) >= 2 else None
            o_i, ev = rope_evac(proj, oc, ps, t0, F, mm,
                                [sb_war if sb_war else None] * 3, sbi, sb_war)
            pq_busy[pbi] = ev
            qk_done.append((o_i, ev))
        wp_busy[bi] = mm
        if nld is not None:
            ld = nld
    qk_all = qk_done[-1][0]
    qk_all2 = qk_done[-2][0]

    v_all = qk_all

    lns2 = LNScratch(ctx, 512, "at2", xb_alias=bo_view, ps_alias=lns.ps)
    out_final = []
    st_tok = {}
    win_war = [None, None, None]
    pend = []

    def attn_view(pp, kc):
        if pp == 0:
            return attn_sb[:, kc * 512:(kc + 1) * 512]
        if kc < 8:
            return ln1[:, 4 * CH * WIN + kc * 512: 4 * CH * WIN + (kc + 1) * 512]
        return lns.xb[:, (kc - 8) * 512:(kc - 7) * 512]

    def make_wo_steps(w_, pair_toks, attn_toks, pp):
        st = {"ld": ctx.dma("gpsimd", lambda h: h.dma_start(
            out=wbuf[0][:].rearrange("p (kc o) -> p kc o", kc=CH), in_=d["woT"][0]),
            waits=[wp_busy[0]], stream="wp")}
        ev_wo = {}

        def step_oc(oc):
            def run():
                bi = oc % 2
                nld = None
                if oc + 1 < CH:
                    nld = ctx.dma("gpsimd", lambda h, oc=oc, bi=bi: h.dma_start(
                        out=wbuf[1 - bi][:].rearrange("p (kc o) -> p kc o", kc=CH),
                        in_=d["woT"][oc + 1]),
                        waits=[wp_busy[1 - bi]], stream="wp2" if bi == 0 else "wp")
                for kc in range(CH):
                    rh = attn_view(pp, kc)
                    w = [st["ld"], attn_toks[-1]] + pair_toks[-1:] + (
                        [ev_wo.get(oc - 2)] if kc == 0 else [])
                    mm_o = ctx.op("tensor", lambda t, wt=wbuf[bi][:, kc * 128:(kc + 1) * 128],
                                  rh=rh, kc=kc, oc=oc: t.matmul(
                        PA[:, (oc % 2) * 512:(oc % 2) * 512 + 512], wt, rh,
                        start=(kc == 0), stop=(kc == CH - 1)), waits=w)
                wp_busy[bi] = mm_o
                ev_wo[oc] = ctx.op("scalar", lambda s, oc=oc: s.activation(
                    out=bo_view[:, oc * 512:(oc + 1) * 512],
                    in_=PA[:, (oc % 2) * 512:(oc % 2) * 512 + 512], func=AF.Copy),
                    waits=[mm_o])
                if nld is not None:
                    st["ld"] = nld
            return run

        def step_ln():
            t0p = (w_ - 1) * WIN
            bo = [bo_view[:, c * 512:(c + 1) * 512] for c in range(CH)]
            xrs = [xres[:, c * TLOC + t0p: c * TLOC + t0p + 512] for c in range(CH)]
            toks, _dn = emit_ln(ctx, lns2, bo, xrs,
                                [ev_wo[c] for c in range(CH)] + [xres_tok],
                                residual=(xrs, None))
            out_final.extend(toks)
        return [step_oc(oc) for oc in range(CH)] + [step_ln]

    pair_attn_toks = []
    for w_ in range(NW):
        pp = (w_ // 2) & 1
        t_qw = ctx.dma("sync", lambda h, w_=w_: h.dma_start(
            out=qwin[:], in_=d["q_s"][:, w_ * CH * WIN:(w_ + 1) * CH * WIN]),
            waits=[qk_all, qk_all2, win_war[0]], stream="qw")
        t_kw = ctx.dma("sync", lambda h, w_=w_: h.dma_start(
            out=kwin[:], in_=d["k_s"][:, w_ * CH * WIN: (w_ + 2) * CH * WIN]),
            waits=[qk_all, qk_all2, win_war[1]], stream="kw")
        t_vw = ctx.dma("sync", lambda h, w_=w_: h.dma_start(
            out=vwin[:], in_=d["v_s"][:, w_ * 2 * D: (w_ * 2 + 4) * D]),
            waits=[v_all, qk_all2, win_war[2]], stream="vw")
        ev_av = {}
        NIT = HEADS * 2
        g0 = w_ * NIT
        sc_ps = [PB0[:], PA[:, 1024:1536]]
        av_ps = [PD[:], PA[:, 1536:2048]]
        pt_ps = [PB1[:, 0:256].bitcast(BF16), PB1[:, 256:512].bitcast(BF16)]

        def stageA_mm(i):
            h_, qc = i // 2, i % 2
            ps_sc = sc_ps[i % 2]
            warp2 = st_tok.get(g0 + i - 2, {})
            krhs = kwin[:].rearrange("p (tc c t) -> p tc c t", tc=2, c=CH)[:, :, h_, :]
            mm_s = ctx.op("tensor", lambda t, h_=h_, qc=qc, ps_sc=ps_sc, krhs=krhs: t.matmul(
                ps_sc, qwin[:, h_ * WIN + qc * 128: h_ * WIN + qc * 128 + 128],
                krhs, start=True, stop=True),
                waits=[t_qw, t_kw, warp2.get("exp")])
            st_tok.setdefault(g0 + i, {})["mm_s"] = mm_s

        def stageA(i):
            h_, qc = i // 2, i % 2
            ri = i % 3
            P_sb = p_sbs[ri]
            ps_sc = sc_ps[i % 2]
            ncol = ri * 4
            warp = st_tok.get(g0 + i - 3, {})
            mm_s = st_tok[g0 + i]["mm_s"]
            t_p = ctx.op("scalar", lambda s, ps_sc=ps_sc, P_sb=P_sb: s.activation(
                out=P_sb[:], in_=ps_sc, func=AF.Exp),
                waits=[mm_s, warp.get("pt_done")])
            moff = ((0 if w_ == 0 else 1) * 2 + qc) * 512
            t_pz = ctx.op("vector", lambda v, moff=moff, P_sb=P_sb, ncol=ncol: v.scalar_tensor_tensor(
                out=P_sb[:], in0=P_sb[:], scalar=1.0, in1=msk[:, moff:moff + 512],
                op0=ALU.mult, op1=ALU.mult, accum_out=cols[:, ncol + 1:ncol + 2]),
                waits=[t_p, t_mk])
            t_rr = ctx.op("vector", lambda v, ncol=ncol: v.reciprocal(
                cols[:, ncol + 2:ncol + 3], cols[:, ncol + 1:ncol + 2]), waits=[t_pz])
            t_pn = ctx.op("vector", lambda v, P_sb=P_sb, ncol=ncol: v.tensor_scalar_mul(
                P_sb[:], P_sb[:], cols[:, ncol + 2:ncol + 3]), waits=[t_rr])
            st_tok[g0 + i].update(exp=t_p, pz=t_pz, pn=t_pn)

        def stageB(i):
            ri = i % 3
            P_sb = p_sbs[ri]
            ptps = pt_ps[i % 2]
            PT_s = pt_sbs[ri]
            warp = st_tok.get(g0 + i - 2, {})
            tts = []
            for kk in range(4):
                tk = ctx.op("tensor", lambda t, kk=kk, P_sb=P_sb, ptps=ptps: t.transpose(
                    ptps[:, kk * 128:(kk + 1) * 128],
                    P_sb[:, kk * 128:(kk + 1) * 128], idn[:]),
                    waits=[st_tok[g0 + i]["pn"], t_id, warp.get("pt_ev")])
                tts.append(tk)
            ev_pt = ctx.op("scalar", lambda s, ptps=ptps, PT_s=PT_s: s.activation(
                out=PT_s[:], in_=ptps[:, 0:512], func=AF.Copy),
                waits=tts + [st_tok.get(g0 + i - 3, {}).get("av_mm")])
            st_tok[g0 + i].update(pt_ev=ev_pt, pt_done=tts[-1])

        def stageC(i):
            h_, qc = i // 2, i % 2
            ri = i % 3
            PT_s = pt_sbs[ri]
            warp = st_tok.get(g0 + i - 2, {})
            pav = av_ps[i % 2]
            for kk in range(4):
                mm_av = ctx.op("tensor", lambda t, h_=h_, kk=kk, PT_s=PT_s, pav=pav: t.matmul(
                    pav[:, 0:128],
                    vwin[:, kk * D + h_ * 128: kk * D + h_ * 128 + 128],
                    PT_s[:, kk * 128:(kk + 1) * 128],
                    start=(kk == 0), stop=(kk == 3)),
                    waits=[t_vw, st_tok[g0 + i]["pt_ev"], warp.get("av_ev")])
            ccol = (w_ % 2) * WIN + qc * 128
            dstv = attn_view(pp, h_)[:, ccol:ccol + 128]
            ev = ctx.op("scalar", lambda s, dstv=dstv, pav=pav: s.activation(
                out=dstv, in_=pav[:, 0:128], func=AF.Copy),
                waits=[mm_av])
            st_tok[g0 + i].update(av_mm=mm_av, av_ev=ev)
            ev_av[(h_, qc)] = ev

        for i in range(NIT + 2):
            if i < NIT:
                stageA_mm(i)
            if 1 <= i < NIT + 1:
                stageB(i - 1)
            if i >= 2:
                stageC(i - 2)
            if i < NIT:
                stageA(i)
            if pend and i % 2 == 0:
                pend.pop(0)()
        while pend:
            pend.pop(0)()
        attn_toks = [ev_av[(h_, qc)] for h_ in range(HEADS) for qc in range(2)]
        win_war = [st_tok[g0 + NIT - 1]["exp"], st_tok[g0 + NIT - 1]["exp"],
                   st_tok[g0 + NIT - 1]["av_mm"]]
        if w_ % 2 == 0:
            pair_attn_toks = attn_toks
            continue
        pend = make_wo_steps(w_, pair_attn_toks, attn_toks, pp)
    while pend:
        pend.pop(0)()
    return out_final



def build_prog_l0attn():
    nc = bass.Bass("TRN2", target_bir_lowering=False)
    d = {}
    d["xin"] = nc.dram_tensor("xin", [128, CH, TLOC], F32, kind="ExternalInput").ap()
    d["halo"] = nc.dram_tensor("halo", [128, CH, WIN], F32, kind="ExternalInput").ap()
    d["wqT"] = nc.dram_tensor("wqT", [CH, 128, CH, 128], F32, kind="ExternalInput").ap()
    d["wkT"] = nc.dram_tensor("wkT", [CH, 128, CH, 128], F32, kind="ExternalInput").ap()
    d["wvT"] = nc.dram_tensor("wvT", [CH, 128, CH, 128], F32, kind="ExternalInput").ap()
    d["woT"] = nc.dram_tensor("woT", [CH, 128, CH, 128], F32, kind="ExternalInput").ap()
    d["cossin"] = nc.dram_tensor("cossin", [64, 2 * (TLOC + WIN)], F32, kind="ExternalInput").ap()
    d["masks"] = nc.dram_tensor("masks", [4, 128, 512], F32, kind="ExternalInput").ap()
    d["ident"] = nc.dram_tensor("ident", [128, 128], F32, kind="ExternalInput").ap()
    d["q_s"] = nc.dram_tensor("q_s", [128, CH * TLOC], BF16, kind="Internal").ap()
    d["k_s"] = nc.dram_tensor("k_s", [128, CH * (TLOC + WIN)], BF16, kind="Internal").ap()
    d["v_s"] = nc.dram_tensor("v_s", [128, CH * (TLOC + WIN)], BF16, kind="Internal").ap()
    d["xout"] = nc.dram_tensor("xout", [128, CH, TLOC], F32, kind="ExternalOutput").ap()

    ctx = Ctx(nc)
    with ctx.es:
        xres = ctx.es.enter_context(nc.sbuf_tensor("xres", [128, CH * TLOC], F32))
        blk = ctx.es.enter_context(nc.Block())
        t_x = ctx.dma("sync", lambda h: h.dma_start(
            out=xres[:], in_=d["xin"].rearrange("p c t -> p (c t)")), stream="x")
        toks = emit_l0attn(ctx, d, xres, t_x)
        st = ctx.dma("sync", lambda h: h.dma_start(
            out=d["xout"].rearrange("p c t -> p (c t)"), in_=xres[:]),
            waits=toks, stream="xo")
        ctx.wait("sync", st)
        ctx.finish(blk)
    return nc



PAIR_GROUPS = [[0, 1], [2, 3], [4, 5], [6, 7]]


def emit_l1ssm(ctx, xin_sb, xin_toks, d):
    nc, es = ctx.nc, ctx.es
    Tt = 512
    NDC = 8
    TB = 256
    NTB = S // TB
    XSC = 8 * TLOC
    PW1 = XSC + 2 * TLOC + 128

    wk = es.enter_context(nc.sbuf_tensor("ss_wk", [128, 12288], F32))
    wkb = wk[:].bitcast(BF16)
    xs_g = es.enter_context(nc.sbuf_tensor("ss_xsg", [128, NDC * S], BF16))
    dtr_g = es.enter_context(nc.sbuf_tensor("ss_dtrg", [128, 2 * S], BF16))
    dtr_m = es.enter_context(nc.sbuf_tensor("ss_dtrm", [128, 2 * TLOC], BF16))
    bct_m = es.enter_context(nc.sbuf_tensor("ss_bctm", [128, 128], BF16))
    bct_g = es.enter_context(nc.sbuf_tensor("ss_bctg", [128, 256], BF16))
    y_sb = xs_g
    Bb = [es.enter_context(nc.sbuf_tensor(f"ss_Bb{i}", [128, TB * DST], BF16)) for i in range(2)]
    Cb = [es.enter_context(nc.sbuf_tensor(f"ss_Cb{i}", [128, TB * DST], BF16)) for i in range(2)]
    xpj = es.enter_context(nc.sbuf_tensor("ss_xpj", [128, CH * 272], BF16))
    wdt = es.enter_context(nc.sbuf_tensor("ss_wdt", [128, 2 * TLOC], BF16))
    hc = es.enter_context(nc.sbuf_tensor("ss_hc", [128, NDC * DST], F32))
    A_sb = es.enter_context(nc.sbuf_tensor("ss_A", [128, NDC * DST], F32))
    vecs = es.enter_context(nc.sbuf_tensor("ss_vecs_sb", [128, 2 * NDC + NDC * DST + 1], F32))
    dxt = [es.enter_context(nc.sbuf_tensor(f"ss_dxt{i}", [128, TB], BF16)) for i in range(2)]
    yred = [es.enter_context(nc.sbuf_tensor(f"ss_yred{i}", [128, TB], F32)) for i in range(2)]
    dlt = [es.enter_context(nc.sbuf_tensor(f"ss_dlt{i}", [128, TB], F32)) for i in range(2)]
    sp1 = es.enter_context(nc.sbuf_tensor("ss_sp1", [128, TB], F32))
    sp2 = es.enter_context(nc.sbuf_tensor("ss_sp2", [128, TB], F32))
    sp1b = es.enter_context(nc.sbuf_tensor("ss_sp1b", [128, TB], F32))
    sp2b = es.enter_context(nc.sbuf_tensor("ss_sp2b", [128, TB], F32))
    idn = es.enter_context(nc.sbuf_tensor("ss_idn", [16, 16], BF16))
    bcev = es.enter_context(nc.sbuf_tensor("ss_bcev", [16, Tt], BF16))
    stg = None
    xs_m = wkb[:, 8192:8192 + NDC * TLOC]
    stg = xs_g[:, 0:NDC * Tt]
    a_v = [wk[:, 0:2048], wk[:, 4096:6144]]
    h_v = [wk[:, 2048:4096], wk[:, 6144:8192]]
    u_v = [wkb[:, 16384:18432], wkb[:, 20480:22528]]
    pr_v = [wkb[:, 18432:20480], wkb[:, 22528:24576]]

    pp = [es.enter_context(nc.psum_tensor(f"ss_pp{i}", [128, 512], F32)) for i in range(2)]
    pbc = es.enter_context(nc.psum_tensor("ss_pbc", [128, 16], BF16))
    lns = LNScratch(ctx, Tt, "ss", xb_alias=wkb[:, 0:CH * Tt])

    bdt = vecs[:, 0:NDC]
    Dcol = vecs[:, NDC:2 * NDC]
    Alog = vecs[:, 2 * NDC:2 * NDC + NDC * DST]
    msel = vecs[:, 2 * NDC + NDC * DST:2 * NDC + NDC * DST + 1]

    t_vec = ctx.dma("sync", lambda h: h.dma_start(out=vecs[:], in_=d["ss_vecs"]), stream="sv")
    t_xp = ctx.dma("gpsimd", lambda h: h.dma_start(
        out=xpj[:].rearrange("p (kc o) -> p kc o", kc=CH),
        in_=d["xprojT"]), stream="sxp")
    t_wd = ctx.dma("gpsimd", lambda h: h.dma_start(
        out=wdt[:].rearrange("p (kc o) -> p kc o", kc=2),
        in_=d["wdtT"].rearrange("kc p o -> p kc o")), stream="swd")
    t_idn = ctx.dma("gpsimd", lambda h: h.dma_start(out=idn[:], in_=d["ident"][0:16, 0:16]),
                    stream="sid")
    t_A0 = ctx.op("scalar", lambda s: s.activation(
        out=A_sb[:], in_=Alog, func=AF.Exp), waits=[t_vec])
    t_A = ctx.op("vector", lambda v: v.tensor_scalar_mul(A_sb[:], A_sb[:], -1.0),
                 waits=[t_A0])
    t_hc = ctx.op("vector", lambda v: v.memset(hc[:], 0.0))
    selbuf = es.enter_context(nc.sbuf_tensor("ss_selb", [128, 2], F32))
    selbi = es.enter_context(nc.sbuf_tensor("ss_selbi", [128, 2], mybir.dt.int32))
    t_s0 = ctx.op("vector", lambda v: v.scalar_tensor_tensor(
        out=selbuf[:, 0:1], in0=msel, scalar=-1.0, in1=msel,
        op0=ALU.mult, op1=ALU.bypass), waits=[t_vec])
    t_s1 = ctx.op("vector", lambda v: v.tensor_scalar(
        out=selbuf[:, 0:1], in0=msel, scalar1=-1.0, scalar2=1.0,
        op0=ALU.mult, op1=ALU.add), waits=[t_vec, t_s0])
    t_s2i = ctx.op("vector", lambda v: v.tensor_copy(
        out=selbi[:, 0:1], in_=selbuf[:, 0:1]), waits=[t_s1])
    t_s3i = ctx.op("vector", lambda v: v.tensor_copy(
        out=selbi[:, 1:2], in_=msel), waits=[t_vec, t_s2i])
    t_s1 = t_s3i
    i_am_s0 = selbi[:, 0:1]
    mselm = selbi[:, 1:2]

    cc1w = []
    dtr_ev = {}
    bct_tr = []
    for tt in range(2):
        t0 = tt * Tt
        src = [xin_sb[:, c * TLOC + t0: c * TLOC + t0 + Tt] for c in range(CH)]
        outc = []
        for c in range(CH):
            if c < NDC:
                outc.append(xs_m[:, c * TLOC + t0: c * TLOC + t0 + Tt])
            else:
                outc.append(stg[:, (c - NDC) * Tt:(c - NDC + 1) * Tt])
        ln_toks, _ = emit_ln(ctx, lns, src, outc, list(xin_toks) + cc1w)
        mm_last = {}
        for och in range(3):
            ps = pp[och % 2]
            hi = 128 if och < 2 else 16
            for kc in range(CH):
                w = [t_xp, ln_toks[kc]] + ([dtr_ev.get((tt, och - 2))] if kc == 0 else [])
                mm = ctx.op("tensor", lambda t, kc=kc, och=och, ps=ps, hi=hi, oc2=outc[kc]: t.matmul(
                    ps[0:hi, 0:Tt],
                    xpj[:, kc * 272 + och * 128: kc * 272 + och * 128 + hi],
                    oc2, start=(kc == 0), stop=(kc == CH - 1)), waits=w)
            mm_last[och] = mm
            if och < 2:
                dtr_ev[(tt, och)] = ctx.op("scalar", lambda s, t0=t0, och=och, ps=ps: s.activation(
                    out=dtr_m[:, och * TLOC + t0: och * TLOC + t0 + Tt],
                    in_=ps[:, 0:Tt], func=AF.Copy), waits=[mm])
            else:
                ev_bc = ctx.op("scalar", lambda s, ps=ps: s.activation(
                    out=bcev[:], in_=ps[0:16, 0:Tt], func=AF.Copy),
                    waits=[mm, bct_tr[-1] if bct_tr else None])
                for q in range(4):
                    tk = ctx.op("tensor", lambda t, q=q: t.transpose(
                        pbc[:], bcev[0:16, q * 128:(q + 1) * 128],
                        idn[:]), waits=[ev_bc, t_idn, bct_tr[-1] if bct_tr else None])
                    tk2 = ctx.op("scalar", lambda s, q=q, tt=tt: s.activation(
                        out=bct_m[:, (tt * 4 + q) * 16:(tt * 4 + q + 1) * 16],
                        in_=pbc[:], func=AF.Copy), waits=[tk])
                    bct_tr.append(tk2)
        cc1w.append(ctx.dma("sync", lambda h, t0=t0: h.dma_start(
            out=d["cc1_in"][:, 0:XSC].rearrange(
                "p (c t) -> p c t", c=NDC)[:, :, t0:t0 + Tt],
            in_=stg[:].rearrange("p (c t) -> p c t", c=NDC)),
            waits=[mm_last[0], mm_last[1], mm_last[2]], stream="c1x"))

    t_c1d = ctx.dma("sync", lambda h: h.dma_start(
        out=d["cc1_in"][:, XSC:XSC + 2 * TLOC], in_=dtr_m[:]),
        waits=[dtr_ev[(0, 0)], dtr_ev[(0, 1)], dtr_ev[(1, 0)], dtr_ev[(1, 1)]], stream="c1d")
    t_c1b = ctx.dma("sync", lambda h: h.dma_start(
        out=d["cc1_in"][:, XSC + 2 * TLOC:XSC + 2 * TLOC + 128], in_=bct_m[:]),
        waits=[bct_tr[-1]], stream="c1b")

    t_cc1 = ctx.op("gpsimd", lambda g: g.collective_compute(
        "AllGather", ALU.bypass, replica_groups=PAIR_GROUPS,
        ins=[d["cc1_in"].opt()], outs=[d["cc1_out"].opt()]),
        waits=cc1w + [t_c1d, t_c1b])

    def half_slices(buf, nseg, seglen):
        return [buf[:].rearrange("p (c s t) -> p c s t", c=nseg, s=2)[:, :, sl, :]
                for sl in range(2)]
    xsg_h = half_slices(xs_g, NDC, TLOC)
    def pred_fix(buf, nseg, local, mask_ap, sl, waits):
        toks = []
        for c in range(nseg):
            o = c * 2 * TLOC + sl * TLOC
            tk = ctx.op("vector", lambda v, o=o, c=c, mask_ap=mask_ap, buf=buf, local=local: v.copy_predicated(
                buf[:, o:o + TLOC], mask_ap.broadcast_to([128, TLOC]),
                local[:, c * TLOC:(c + 1) * TLOC]), waits=waits + toks[-1:])
            toks.append(tk)
        return toks[-1]
    t_g0 = ctx.dma("sync", lambda h: h.dma_start(
        out=xsg_h[0], in_=d["cc1_out"][0:128, 0:XSC].rearrange("p (c t) -> p c t", c=NDC)),
        waits=[t_cc1], stream="g0")
    t_g1 = ctx.dma("sync", lambda h: h.dma_start(
        out=xsg_h[1], in_=d["cc1_out"][128:256, 0:XSC].rearrange("p (c t) -> p c t", c=NDC)),
        waits=[t_cc1], stream="g1")
    t_fx0 = pred_fix(xs_g, NDC, xs_m, i_am_s0, 0, [t_g0, t_s1])
    t_fx1 = pred_fix(xs_g, NDC, xs_m, mselm, 1, [t_g1, t_s1, t_fx0])
    xs_g_tok = t_fx1
    dtg_h = half_slices(dtr_g, 2, TLOC)
    t_d0 = ctx.dma("sync", lambda h: h.dma_start(
        out=dtg_h[0], in_=d["cc1_out"][0:128, XSC:XSC + 2 * TLOC].rearrange(
            "p (c t) -> p c t", c=2)), waits=[t_cc1], stream="d0")
    t_d1 = ctx.dma("sync", lambda h: h.dma_start(
        out=dtg_h[1], in_=d["cc1_out"][128:256, XSC:XSC + 2 * TLOC].rearrange(
            "p (c t) -> p c t", c=2)), waits=[t_cc1], stream="d1")
    t_fd0 = pred_fix(dtr_g, 2, dtr_m[:], i_am_s0, 0, [t_d0, t_s1])
    t_fd1 = pred_fix(dtr_g, 2, dtr_m[:], mselm, 1, [t_d1, t_s1, t_fd0])
    dtr_g_tok = t_fd1
    btg_h = half_slices(bct_g, 1, 128)
    t_b0 = ctx.dma("sync", lambda h: h.dma_start(
        out=btg_h[0].squeeze(1), in_=d["cc1_out"][0:128, XSC + 2 * TLOC:XSC + 2 * TLOC + 128]),
        waits=[t_cc1], stream="b0")
    t_b1 = ctx.dma("sync", lambda h: h.dma_start(
        out=btg_h[1].squeeze(1), in_=d["cc1_out"][128:256, XSC + 2 * TLOC:XSC + 2 * TLOC + 128]),
        waits=[t_cc1], stream="b1")
    t_fb0 = ctx.op("vector", lambda v: v.copy_predicated(
        btg_h[0].squeeze(1), i_am_s0.broadcast_to([128, 128]), bct_m[:]),
        waits=[t_b0, t_s1])
    t_fb1 = ctx.op("vector", lambda v: v.copy_predicated(
        btg_h[1].squeeze(1), mselm.broadcast_to([128, 128]), bct_m[:]),
        waits=[t_b1, t_s1, t_fb0])
    t_bsB = ctx.dma("sync", lambda h: h.dma_start(
        out=d["bc_sB"].rearrange("(s tc p) n -> p s tc n", s=2, p=128),
        in_=bct_g[:].rearrange("p (s tc n) -> p s tc n", s=2, n=16)[:, :, :, 0:8]),
        waits=[t_fb1], stream="bs")
    t_bs = ctx.dma("sync", lambda h: h.dma_start(
        out=d["bc_sC"].rearrange("(s tc p) n -> p s tc n", s=2, p=128),
        in_=bct_g[:].rearrange("p (s tc n) -> p s tc n", s=2, n=16)[:, :, :, 8:16]),
        waits=[t_fb1, t_bsB], stream="bs")

    one_ap, one_tok = ctx.constf128(1.0)
    y_toks = {}
    NIT2 = NTB * NDC
    sst = {}
    ld_bc = {}
    seed_war = [xs_g_tok, dtr_g_tok]

    def it_of(tb, dc):
        return tb * NDC + dc

    def ld_stage(tb):
        pw = sst.get(it_of(tb - 2, NDC - 1), {})
        ld_b = ctx.dma("gpsimd", lambda h, tb=tb: h.dma_start(
            out=Bb[tb % 2][:],
            in_=d["bc_sB"][tb * TB:(tb + 1) * TB, :].rearrange("t n -> (t n)")
            .unsqueeze(0).broadcast_to([128, TB * DST])),
            waits=[t_bs, pw.get("u")], stream="ldb")
        ld_c = ctx.dma("gpsimd", lambda h, tb=tb: h.dma_start(
            out=Cb[tb % 2][:],
            in_=d["bc_sC"][tb * TB:(tb + 1) * TB, :].rearrange("t n -> (t n)")
            .unsqueeze(0).broadcast_to([128, TB * DST])),
            waits=[t_bs, pw.get("pr")], stream="ldc")
        ld_bc[tb] = (ld_b, ld_c)

    def s0(it):
        tb, dc = it // NDC, it % NDC
        pi = it % 2
        pw = sst.get(it - 2, {})
        ps_d = pp[pi]
        for kc in range(2):
            w = [t_wd, dtr_g_tok] + ([pw.get("sp")] if kc == 0 else [])
            mm_d = ctx.op("tensor", lambda t, kc=kc, dc=dc, tb=tb, ps_d=ps_d: t.matmul(
                ps_d[:, 0:TB],
                wdt[:, kc * TLOC + dc * 128: kc * TLOC + dc * 128 + 128],
                dtr_g[:, kc * S + tb * TB: kc * S + tb * TB + TB],
                start=(kc == 0), stop=(kc == 1)), waits=w)
        s1_, s2_ = (sp1, sp2) if pi == 0 else (sp1b, sp2b)
        t_ab = ctx.op("scalar", lambda s, dc=dc, s1_=s1_, ps_d=ps_d: s.activation(
            out=s1_[:], in_=ps_d[:, 0:TB], func=AF.Abs,
            bias=bdt[:, dc:dc + 1]), waits=[mm_d, t_vec, pw.get("dl")])
        t_rl = ctx.op("scalar", lambda s, dc=dc, s2_=s2_, ps_d=ps_d: s.activation(
            out=s2_[:], in_=ps_d[:, 0:TB], func=AF.Relu,
            bias=bdt[:, dc:dc + 1]), waits=[mm_d, t_vec, pw.get("dl")])
        t_ex = ctx.op("scalar", lambda s, s1_=s1_: s.activation(
            out=s1_[:], in_=s1_[:], func=AF.Exp, scale=-1.0), waits=[t_ab])
        t_ln2 = ctx.op("scalar", lambda s, s1_=s1_: s.activation(
            out=s1_[:], in_=s1_[:], func=AF.Ln, bias=one_ap), waits=[t_ex, one_tok])
        t_dl = ctx.op("vector", lambda v, pi=pi, s1_=s1_, s2_=s2_: v.tensor_add(
            dlt[pi][:], s1_[:], s2_[:]), waits=[t_ln2, t_rl, pw.get("ax"), pw.get("dx")])
        sst.setdefault(it, {}).update(sp=t_rl, dl=t_dl)
        a3 = a_v[pi].rearrange("p (t n) -> p t n", n=DST)
        a_toks = []
        for n in range(DST):
            tk = ctx.op("scalar", lambda s, n=n, dc=dc, a3=a3, pi=pi: s.activation(
                out=a3[:, :, n], in_=dlt[pi][:], func=AF.Exp,
                scale=A_sb[:, dc * DST + n:dc * DST + n + 1]),
                waits=[t_dl, t_A] + (seed_war if it < 2 else [pw.get("sc")]))
            a_toks.append(tk)
        t_dx = ctx.op("vector", lambda v, dc=dc, tb=tb, pi=pi: v.tensor_mul(
            dxt[pi][:], dlt[pi][:],
            xs_g[:, dc * S + tb * TB: dc * S + tb * TB + TB]),
            waits=[t_dl, xs_g_tok])
        sst[it].update(ax=a_toks[-1], a_toks=a_toks, dx=t_dx)

    def s1(it):
        tb, dc = it // NDC, it % NDC
        pi = it % 2
        pw = sst.get(it - 2, {})
        t_u = ctx.op("gpsimd", lambda g, pi=pi: g.tensor_tensor(
            out=u_v[pi].rearrange("p (t n) -> p t n", n=DST),
            in0=dxt[pi][:].unsqueeze(2).broadcast_to([128, TB, DST]),
            in1=Bb[tb % 2][:].rearrange("p (t n) -> p t n", n=DST), op=ALU.mult),
            waits=[sst[it]["dx"], ld_bc[tb][0]] + ([pw.get("sc")] if pw else seed_war))
        sst[it]["u"] = t_u

    def s2(it):
        tb, dc = it // NDC, it % NDC
        pi = it % 2
        h3 = h_v[pi].rearrange("p (t n) -> p t n", n=DST)
        u3 = u_v[pi].rearrange("p (t n) -> p t n", n=DST)
        a3 = a_v[pi].rearrange("p (t n) -> p t n", n=DST)
        pw = sst.get(it - 2, {})
        sc_toks = []
        for n in range(DST):
            tk = ctx.op("vector", lambda v, n=n, dc=dc, h3=h3, u3=u3, a3=a3: v.tensor_tensor_scan(
                out=h3[:, :, n], data0=a3[:, :, n], data1=u3[:, :, n],
                initial=hc[:, dc * DST + n:dc * DST + n + 1],
                op0=ALU.mult, op1=ALU.add),
                waits=[sst[it]["a_toks"][n], sst[it]["u"], t_hc]
                + ([pw.get("pr"), pw.get("cu")] if n == 0 and pw else
                   (seed_war if n == 0 else [])))
            sc_toks.append(tk)
        t_cu = ctx.op("vector", lambda v, dc=dc, h3=h3: v.tensor_copy(
            out=hc[:, dc * DST:(dc + 1) * DST], in_=h3[:, TB - 1, :]), waits=sc_toks)
        sst[it].update(sc=sc_toks[-1], sc_toks=sc_toks, cu=t_cu)

    def s3(it):
        tb, dc = it // NDC, it % NDC
        pi = it % 2
        h3 = h_v[pi].rearrange("p (t n) -> p t n", n=DST)
        t_pr = ctx.op("gpsimd", lambda g, h3=h3, pi=pi, tb=tb: g.tensor_tensor(
            out=pr_v[pi].rearrange("p (t n) -> p t n", n=DST), in0=h3,
            in1=Cb[tb % 2][:].rearrange("p (t n) -> p t n", n=DST), op=ALU.mult),
            waits=sst[it]["sc_toks"] + [ld_bc[tb][1], sst.get(it - 2, {}).get("yr")])
        sst[it]["pr"] = t_pr

    def s4(it):
        tb, dc = it // NDC, it % NDC
        pi = it % 2
        t_yr = ctx.op("vector", lambda v, pi=pi: v.reduce_sum(
            yred[pi][:].unsqueeze(2),
            pr_v[pi].rearrange("p (t n) -> p t n", n=DST), axis=AX.X),
            waits=[sst[it]["pr"]])
        sst[it]["yr"] = t_yr
        y_toks[(tb, dc)] = ctx.op("vector", lambda v, dc=dc, tb=tb, pi=pi: v.scalar_tensor_tensor(
            out=y_sb[:, dc * S + tb * TB: dc * S + tb * TB + TB],
            in0=xs_g[:, dc * S + tb * TB: dc * S + tb * TB + TB],
            scalar=Dcol[:, dc:dc + 1], in1=yred[pi][:],
            op0=ALU.mult, op1=ALU.add), waits=[t_yr, xs_g_tok, t_vec])

    ld_stage(0)
    ld_stage(1)
    for it in range(NIT2 + 2):
        if it < NIT2:
            s0(it)
        if 1 <= it < NIT2 + 1:
            s1(it - 1)
            s2(it - 1)
        if it >= 2:
            s3(it - 2)
            s4(it - 2)
            if it % NDC == 1 and it > NDC:
                tbn = (it - 1) // NDC + 1
                if 2 <= tbn <= NTB - 1:
                    ld_stage(tbn)
    scan_done = [y_toks[(NTB - 1, dc)] for dc in range(NDC)]
    if "dbg_y" in d:
        scan_done = scan_done + [
            ctx.dma("sync", lambda h: h.dma_start(out=d["dbg_y"], in_=y_sb[:]),
                    waits=list(y_toks.values()), stream="dbgy"),
            ctx.dma("sync", lambda h: h.dma_start(out=d["dbg_xsg"], in_=xs_g[:]),
                    waits=[xs_g_tok], stream="dbgx"),
            ctx.dma("sync", lambda h: h.dma_start(out=d["dbg_dtg"], in_=dtr_g[:]),
                    waits=[dtr_g_tok], stream="dbgd")]

    ypar = wkb[:, 0:NDC * TLOC]
    t_yp = None
    for c in range(NDC):
        tk0 = ctx.op("vector", lambda v, c=c: v.tensor_copy(
            out=ypar[:, c * TLOC:(c + 1) * TLOC],
            in_=y_sb[:, c * S + TLOC: c * S + 2 * TLOC]),
            waits=scan_done + [t_s1] + ([t_yp] if t_yp else []))
        t_yp = ctx.op("vector", lambda v, c=c: v.copy_predicated(
            ypar[:, c * TLOC:(c + 1) * TLOC], mselm.broadcast_to([128, TLOC]),
            y_sb[:, c * S: c * S + TLOC]), waits=[tk0])
    t_c2 = ctx.dma("sync", lambda h: h.dma_start(
        out=d["cc2_in"].rearrange("p (c t) -> p c t", c=NDC), in_=ypar),
        waits=[t_yp], stream="c2")
    t_cc2 = ctx.op("gpsimd", lambda g: g.collective_compute(
        "AllGather", ALU.bypass, replica_groups=PAIR_GROUPS,
        ins=[d["cc2_in"].opt()], outs=[d["cc2_out"].opt()]), waits=[t_c2])
    ylm_b = wkb[:, 0:NDC * TLOC]
    t_ylm = None
    for c in range(NDC):
        tk0 = ctx.op("vector", lambda v, c=c: v.tensor_copy(
            out=ylm_b[:, c * TLOC:(c + 1) * TLOC],
            in_=y_sb[:, c * S: c * S + TLOC]),
            waits=[t_c2] + scan_done + ([t_ylm] if t_ylm else []))
        t_ylm = ctx.op("vector", lambda v, c=c: v.copy_predicated(
            ylm_b[:, c * TLOC:(c + 1) * TLOC], mselm.broadcast_to([128, TLOC]),
            y_sb[:, c * S + TLOC: c * S + 2 * TLOC]), waits=[tk0])
    ylo = wkb[:, NDC * TLOC:2 * NDC * TLOC]
    t_r0 = ctx.dma("sync", lambda h: h.dma_start(
        out=ylo, in_=d["cc2_out"][0:128, :]), waits=[t_cc2], stream="yr0")
    t_r1 = ctx.dma("sync", lambda h: h.dma_start(
        out=y_sb[:, 0:NDC * TLOC], in_=d["cc2_out"][128:256, :]),
        waits=[t_cc2, t_ylm] + scan_done, stream="yr1")
    t_yc = ctx.op("vector", lambda v: v.copy_predicated(
        ylo, i_am_s0.broadcast_to([128, NDC * TLOC]), y_sb[:, 0:NDC * TLOC]),
        waits=[t_r0, t_r1, t_s1])
    y_loc_tok = [t_ylm, t_yc]

    out_toks = None
    ln_war = []
    for tt in range(2):
        t0 = tt * Tt
        src = []
        for c in range(CH):
            if c < NDC:
                src.append(wkb[:, c * TLOC + t0: c * TLOC + t0 + Tt])
            else:
                src.append(wkb[:, NDC * TLOC + (c - NDC) * TLOC + t0:
                               NDC * TLOC + (c - NDC) * TLOC + t0 + Tt])
        xiw = [xin_sb[:, c * TLOC + t0: c * TLOC + t0 + Tt] for c in range(CH)]
        out_toks, _ = emit_ln(ctx, lns, src, xiw, y_loc_tok + ln_war,
                              residual=(xiw, None))
        ln_war = out_toks[:]
    return out_toks


def build_prog_l1ssm():
    nc = bass.Bass("TRN2", target_bir_lowering=False)
    d = {}
    d["xin"] = nc.dram_tensor("xin", [128, CH, TLOC], F32, kind="ExternalInput").ap()
    d["xprojT"] = nc.dram_tensor("xprojT", [128, CH, 272], F32, kind="ExternalInput").ap()
    d["wdtT"] = nc.dram_tensor("wdtT", [2, 128, TLOC], F32, kind="ExternalInput").ap()
    d["ss_vecs"] = nc.dram_tensor("ss_vecs", [128, 2 * 8 + 8 * DST + 1], F32, kind="ExternalInput").ap()
    d["ident"] = nc.dram_tensor("ident", [128, 128], F32, kind="ExternalInput").ap()
    PW1 = 8 * TLOC + 2 * TLOC + 128
    d["q_s"] = nc.dram_tensor("q_s", [128, CH * TLOC], BF16, kind="Internal").ap()
    d["k_s"] = nc.dram_tensor("k_s", [128, CH * (TLOC + WIN)], BF16, kind="Internal").ap()
    d["v_s"] = nc.dram_tensor("v_s", [128, CH * (TLOC + WIN)], BF16, kind="Internal").ap()
    d["cc1_in"] = nc.dram_tensor("cc1_in", [128, PW1], BF16, kind="Internal").ap()
    d["cc1_out"] = nc.dram_tensor("cc1_out", [256, PW1], BF16, kind="Internal").ap()
    d["cc2_in"] = nc.dram_tensor("cc2_in", [128, 8 * TLOC], BF16, kind="Internal").ap()
    d["cc2_out"] = nc.dram_tensor("cc2_out", [256, 8 * TLOC], BF16, kind="Internal").ap()
    d["bc_sB"] = nc.dram_tensor("bc_sB", [S, 8], BF16, kind="Internal").ap()
    d["bc_sC"] = nc.dram_tensor("bc_sC", [S, 8], BF16, kind="Internal").ap()
    xout = nc.dram_tensor("xout", [128, CH, TLOC], F32, kind="ExternalOutput").ap()

    import os
    dbg = os.environ.get("SSM_DEBUG") == "1"
    if dbg:
        d["dbg_y"] = nc.dram_tensor("dbg_y", [128, 8 * S], BF16, kind="ExternalOutput").ap()
        d["dbg_xsg"] = nc.dram_tensor("dbg_xsg", [128, 8 * S], BF16, kind="ExternalOutput").ap()
        d["dbg_dtg"] = nc.dram_tensor("dbg_dtg", [128, 2 * S], BF16, kind="ExternalOutput").ap()
    ctx = Ctx(nc)
    with ctx.es:
        xin_sb = ctx.es.enter_context(nc.sbuf_tensor("xin_sb", [128, CH * TLOC], F32))
        blk = ctx.es.enter_context(nc.Block())
        t_x = ctx.dma("sync", lambda h: h.dma_start(
            out=xin_sb[:], in_=d["xin"].rearrange("p c t -> p (c t)")), stream="x")
        out_toks = emit_l1ssm(ctx, xin_sb, [t_x], d)
        st = ctx.dma("sync", lambda h: h.dma_start(
            out=xout.rearrange("p c t -> p (c t)"), in_=xin_sb[:]),
            waits=out_toks, stream="xo")
        ctx.wait("sync", st)
        ctx.finish(blk)
    return nc



def _perm_headwise(wT):
    out = wT.reshape(D, HEADS, DH)
    perm = np.concatenate([np.arange(0, DH, 2), np.arange(1, DH, 2)])
    out = out[:, :, perm]
    return np.ascontiguousarray(out.reshape(D, D))


def _rot(fm, h):
    if h == 0:
        return fm
    return np.ascontiguousarray(np.concatenate([fm[:, 8:], fm[:, :8]], axis=1))


def _rot_ax(a, ax):
    n = a.shape[ax]
    h = n // 2
    idx = [slice(None)] * a.ndim
    idx1, idx2 = list(idx), list(idx)
    idx1[ax] = slice(h, None)
    idx2[ax] = slice(0, h)
    return np.ascontiguousarray(np.concatenate([a[tuple(idx1)], a[tuple(idx2)]], axis=ax))


FUSED = True


def _host_prep(inp):
    ident = np.eye(128, dtype=np.float32)
    wqT0 = host_wT_oc(_perm_headwise(np.ascontiguousarray(inp["L0_attn_Wq"].T)))
    wkT0 = host_wT_oc(_perm_headwise(np.ascontiguousarray(inp["L0_attn_Wk"].T)))
    wvT0 = host_wT_oc(np.ascontiguousarray(inp["L0_attn_Wv"].T))
    woT0 = host_wT_oc(np.ascontiguousarray(inp["L0_attn_Wo"].T))
    inv = 1.0 / (10000.0 ** (np.arange(0, DH, 2, dtype=np.float32) / DH))
    tril = np.tril(np.ones((WIN, WIN), np.float32))
    onesw = np.ones((WIN, WIN), np.float32)
    shgT0 = host_wT_oc(np.ascontiguousarray(inp["L0_ffn_sh_gate"].T))
    shuT0 = host_wT_oc(np.ascontiguousarray(inp["L0_ffn_sh_up"].T))
    shdT0 = host_wT_oc(np.ascontiguousarray(inp["L0_ffn_sh_down"].T))
    rguT0 = np.stack([host_wT_oc(np.ascontiguousarray(inp["L0_ffn_r_gate"].T)),
                      host_wT_oc(np.ascontiguousarray(inp["L0_ffn_r_up"].T))])
    rdT0 = host_wT_oc(np.ascontiguousarray(inp["L0_ffn_r_down"].T))
    routT0 = np.ascontiguousarray(inp["L0_ffn_router_weight"].T)
    xpT_full = np.ascontiguousarray(inp["L1_attn_W_xproj"].T)
    l1g0 = host_wT_oc(inp["L1_ffn_gate_w"])
    l1u0 = host_wT_oc(inp["L1_ffn_up_w"])
    l1d0 = host_wT_oc(inp["L1_ffn_down_w"])
    gsc = host_colvec(inp["L1_ffn_gate_scale"])
    usc = host_colvec(inp["L1_ffn_up_scale"])
    dsc0 = host_colvec(inp["L1_ffn_down_scale"])

    per_h = {}
    for h in (0, 1):
        t0 = h * TLOC
        pos = np.arange(t0 - WIN, t0 + TLOC, dtype=np.float32)
        fr = np.outer(pos, inv).astype(np.float32)
        cs = np.concatenate([np.cos(fr).T, np.sin(fr).T], axis=1).astype(np.float32)
        left = np.zeros((WIN, WIN), np.float32) if h == 0 else onesw
        m0 = np.concatenate([left, tril], axis=1)
        mr = np.concatenate([onesw, tril], axis=1)
        masks = np.stack([m0[0:128], m0[128:256], mr[0:128], mr[128:256]])
        myd = slice(h * 1024, (h + 1) * 1024)
        xp = xpT_full.reshape(CH, 128, 272)
        if h == 1:
            xp = np.concatenate([xp[8:], xp[:8]], axis=0)
        v = np.zeros((128, 81), np.float32)
        v[:, 0:8] = inp["L1_attn_b_dt"][myd].reshape(8, 128).T
        v[:, 8:16] = inp["L1_attn_D_param"][myd].reshape(8, 128).T
        v[:, 16:80] = inp["L1_attn_A_log"][myd].reshape(8, 128, DST).transpose(1, 0, 2).reshape(128, 64)
        v[:, 80] = float(h)
        R = (lambda a, ax: a) if h == 0 else _rot_ax
        per_h[h] = {
            "wqT": R(wqT0, 2), "wkT": R(wkT0, 2), "wvT": R(wvT0, 2), "woT": R(woT0, 0),
            "cossin": cs, "masks": masks, "ident": ident,
            "sh_gateT": R(shgT0, 2), "sh_upT": R(shuT0, 2), "sh_dT": R(shdT0, 0),
            "r_guT": R(rguT0, 3), "r_dT": R(rdT0, 0), "routerT": R(routT0, 0),
            "xprojT": np.ascontiguousarray(np.transpose(xp, (1, 0, 2))),
            "wdtT": np.ascontiguousarray(inp["L1_attn_W_dt"].T[:, myd].reshape(2, 128, TLOC)),
            "ss_vecs": v,
            "l1_gateT": R(l1g0, 2), "l1_upT": R(l1u0, 2), "l1_downT": R(l1d0, 0),
            "l1_scales": np.concatenate([gsc, usc, R(dsc0, 1)], axis=1),
        }
    x = inp["L0_attn_x"]
    in_maps = []
    for c in range(NCORE):
        b, h = c // 2, c % 2
        t0 = h * TLOC
        halo = (np.zeros((128, CH, WIN), np.float32) if h == 0
                else _rot(host_fm(x[b, t0 - WIN:t0]), h))
        m = dict(per_h[h])
        m["xin"] = _rot(host_fm(x[b, t0:t0 + TLOC]), h)
        m["halo"] = halo
        in_maps.append(m)
    return in_maps


def kernel(**inp):
    inp = {k: np.asarray(v, dtype=np.float32) for k, v in inp.items()}
    in_maps = _host_prep(inp)
    if FUSED:
        nc = build_prog_fused()
        res = run_bass_kernel_spmd(nc, in_maps, core_ids=list(range(NCORE)))
        out = np.empty((B, S, D), np.float32)
        for c in range(NCORE):
            b, h = c // 2, c % 2
            out[b, h * TLOC:(h + 1) * TLOC] = host_unfm(
                _rot(np.asarray(res.results[c]["xout"], np.float32), h))
        return out
    KEYS1 = ["xin", "halo", "wqT", "wkT", "wvT", "woT", "cossin", "masks", "ident"]
    nc = build_prog_l0attn()
    res = run_bass_kernel_spmd(nc, [{k: m[k] for k in KEYS1} for m in in_maps],
                               core_ids=list(range(NCORE)))
    xs1 = [np.asarray(res.results[c]["xout"], np.float32) for c in range(NCORE)]
    KEYS2 = ["sh_gateT", "sh_upT", "sh_dT", "r_guT", "r_dT", "routerT", "ident"]
    nc = build_prog_l0ffn()
    res = run_bass_kernel_spmd(nc, [dict({k: m[k] for k in KEYS2}, xin=xs1[c])
                                    for c, m in enumerate(in_maps)],
                               core_ids=list(range(NCORE)))
    xs2 = [np.asarray(res.results[c]["xout"], np.float32) for c in range(NCORE)]
    KEYS3 = ["xprojT", "wdtT", "ss_vecs", "ident"]
    nc = build_prog_l1ssm()
    res = run_bass_kernel_spmd(nc, [dict({k: m[k] for k in KEYS3}, xin=xs2[c])
                                    for c, m in enumerate(in_maps)],
                               core_ids=list(range(NCORE)))
    xs3 = [np.asarray(res.results[c]["xout"], np.float32) for c in range(NCORE)]
    KEYS4 = ["l1_gateT", "l1_upT", "l1_downT", "l1_scales"]
    nc = build_prog_l1ffn()
    res = run_bass_kernel_spmd(nc, [dict({k: m[k] for k in KEYS4}, xin=xs3[c])
                                    for c, m in enumerate(in_maps)],
                               core_ids=list(range(NCORE)))
    out = np.empty((B, S, D), np.float32)
    for c in range(NCORE):
        b, h = c // 2, c % 2
        out[b, h * TLOC:(h + 1) * TLOC] = host_unfm(
            _rot(np.asarray(res.results[c]["xout"], np.float32), h))
    return out



def build_prog_fused():
    nc = bass.Bass("TRN2", target_bir_lowering=False)
    d = {}
    EI = lambda n, sh: nc.dram_tensor(n, sh, F32, kind="ExternalInput").ap()
    d["xin"] = EI("xin", [128, CH, TLOC])
    d["halo"] = EI("halo", [128, CH, WIN])
    d["wqT"] = EI("wqT", [CH, 128, CH, 128])
    d["wkT"] = EI("wkT", [CH, 128, CH, 128])
    d["wvT"] = EI("wvT", [CH, 128, CH, 128])
    d["woT"] = EI("woT", [CH, 128, CH, 128])
    d["cossin"] = EI("cossin", [64, 2 * (TLOC + WIN)])
    d["masks"] = EI("masks", [4, 128, 512])
    d["ident"] = EI("ident", [128, 128])
    d["sh_gateT"] = EI("sh_gateT", [ISH // 128, 128, CH, 128])
    d["sh_upT"] = EI("sh_upT", [ISH // 128, 128, CH, 128])
    d["sh_dT"] = EI("sh_dT", [CH, 128, ISH // 128, 128])
    d["r_guT"] = EI("r_guT", [2, IRT // 128, 128, CH, 128])
    d["r_dT"] = EI("r_dT", [CH, 128, IRT // 128, 128])
    d["routerT"] = EI("routerT", [D, 8])
    d["xprojT"] = EI("xprojT", [128, CH, 272])
    d["wdtT"] = EI("wdtT", [2, 128, TLOC])
    d["ss_vecs"] = EI("ss_vecs", [128, 2 * 8 + 8 * DST + 1])
    d["l1_gateT"] = EI("l1_gateT", [IL1 // 128, 128, CH, 128])
    d["l1_upT"] = EI("l1_upT", [IL1 // 128, 128, CH, 128])
    d["l1_downT"] = EI("l1_downT", [CH, 128, IL1 // 128, 128])
    d["l1_scales"] = EI("l1_scales", [128, 2 * (IL1 // 128) + CH])
    PW1 = 8 * TLOC + 2 * TLOC + 128
    d["q_s"] = nc.dram_tensor("q_s", [128, CH * TLOC], BF16, kind="Internal").ap()
    d["k_s"] = nc.dram_tensor("k_s", [128, CH * (TLOC + WIN)], BF16, kind="Internal").ap()
    d["v_s"] = nc.dram_tensor("v_s", [128, CH * (TLOC + WIN)], BF16, kind="Internal").ap()
    d["cc1_in"] = nc.dram_tensor("cc1_in", [128, PW1], BF16, kind="Internal").ap()
    d["cc1_out"] = nc.dram_tensor("cc1_out", [256, PW1], BF16, kind="Internal").ap()
    d["cc2_in"] = nc.dram_tensor("cc2_in", [128, 8 * TLOC], BF16, kind="Internal").ap()
    d["cc2_out"] = nc.dram_tensor("cc2_out", [256, 8 * TLOC], BF16, kind="Internal").ap()
    d["bc_sB"] = nc.dram_tensor("bc_sB", [S, 8], BF16, kind="Internal").ap()
    d["bc_sC"] = nc.dram_tensor("bc_sC", [S, 8], BF16, kind="Internal").ap()
    d["x3_s"] = nc.dram_tensor("x3_s", [128, CH * TLOC], F32, kind="Internal").ap()
    d["x1_s"] = nc.dram_tensor("x1_s", [128, CH * TLOC], F32, kind="Internal").ap()
    xout = nc.dram_tensor("xout", [128, CH, TLOC], F32, kind="ExternalOutput").ap()

    ctx = Ctx(nc)
    with ctx.es:
        xres = ctx.es.enter_context(nc.sbuf_tensor("xres", [128, CH * TLOC], F32))
        blk = ctx.es.enter_context(nc.Block())
        t_x0 = ctx.dma("sync", lambda h: h.dma_start(
            out=xres[:], in_=d["xin"].rearrange("p c t -> p (c t)")), stream="x")
        with ctx.scope():
            toks = emit_l0attn(ctx, d, xres, t_x0)
        ctx.barrier()
        with ctx.scope():
            toks = emit_l0ffn(ctx, xres, toks, d)
        ctx.barrier()
        with ctx.scope():
            toks = emit_l1ssm(ctx, xres, toks, d)
        ctx.barrier()
        with ctx.scope():
            toks = emit_l1ffn(ctx, xres, toks, d, xout)
        ctx.wait("sync", *toks)
        ctx.finish(blk)
    return nc
